# version 03
# baseline (speedup 1.0000x reference)
import numpy as np
import ml_dtypes
from collections import defaultdict
from contextlib import ExitStack

import concourse.bass as bass
import concourse.mybir as mybir
from concourse.bass_utils import run_bass_kernel_spmd

F32 = mybir.dt.float32
BF16 = mybir.dt.bfloat16
AF = mybir.ActivationFunctionType
ALU = mybir.AluOpType
AX = mybir.AxisListType

ENGINES = ("tensor", "vector", "scalar", "gpsimd", "sync")
NDMALANES = 8
DMALANES = tuple(f"dma{i}" for i in range(NDMALANES))
ALLSEMS = ENGINES + DMALANES


def _t3(ap, p=128):
    return ap.rearrange("(kt p) m -> p kt m", p=p)


def _s3(ap, inner):
    return ap.rearrange("p (kt m) -> p kt m", m=inner)


T = 4096
C = 1024
NT = 32
NCH = 8
H = 16
G = 4
D = 64
E = 8
FF = 4096
TBLK = 1024
NBLK = T // TBLK
SEGS = ((512, 1), (1024, 2), (2048, 4))



class _Tok:
    __slots__ = ("eng", "cnt", "clock")

    def __init__(self, eng, cnt, clock):
        self.eng, self.cnt, self.clock = eng, cnt, clock


class Prog:
    def __init__(self, nc):
        self.nc = nc
        self.ops = {e: [] for e in ENGINES}
        self.cnt = {e: 0 for e in ALLSEMS}
        self.clock = {e: {f: 0 for f in ALLSEMS} for e in ENGINES}
        self.writes = {}
        self.readers = defaultdict(list)
        self.sems = None
        self._lane_rr = 0

    def step(self, eng, fn, reads=(), writes=(), dma=False, tag=None,
             pclass="full"):
        chain = None
        if dma:
            if pclass == "full":
                lane = DMALANES[self._lane_rr % 5]
                self._lane_rr += 1
            elif pclass == "p64":
                lane = DMALANES[5]
            elif pclass == "p1":
                lane = DMALANES[6]
            else:
                lane = DMALANES[7]
                chain = self.cnt[lane]
            amt = 16
        else:
            lane = eng
            amt = 1
        my = self.clock[eng]
        need = {}
        toks = []
        for k in reads:
            t = self.writes.get(k)
            if t is not None:
                toks.append(t)
        for k in writes:
            t = self.writes.get(k)
            if t is not None:
                toks.append(t)
            toks.extend(self.readers.get(k, ()))
        for t in toks:
            if my.get(t.eng, 0) < t.cnt and need.get(t.eng, 0) < t.cnt:
                need[t.eng] = t.cnt
        waits = sorted(need.items())
        for e2, c2 in waits:
            my[e2] = c2
        for t in toks:
            if my.get(t.eng, 0) >= t.cnt:
                for e2, c2 in t.clock.items():
                    if my[e2] < c2:
                        my[e2] = c2
        self.cnt[lane] += amt
        mytok = _Tok(lane, self.cnt[lane], dict(my))
        mytok.clock[lane] = self.cnt[lane]
        for k in writes:
            self.writes[k] = mytok
            self.readers[k] = []
        for k in reads:
            self.readers[k].append(mytok)

        def closure(h, _waits=waits, _fn=fn, _amt=amt, _lane=lane, _tag=tag,
                    _chain=chain):
            for e2, c2 in _waits:
                h.wait_ge(self.sems[e2], c2)
            if _chain:
                h.wait_ge(self.sems[_lane], _chain)
            last = _fn(h)
            if last is None:
                raise RuntimeError(f"step returned None ({_lane}, {_tag})")
            last.then_inc(self.sems[_lane], _amt)

        self.ops[eng].append(closure)
        return mytok

    def barrier(self):
        toks = {e: self.cnt[e] for e in ALLSEMS}
        for e in ENGINES:
            my = self.clock[e]
            for e2, c in toks.items():
                if e2 != e and my[e2] < c and c > 0:
                    def w(h, _e2=e2, _c=c):
                        h.wait_ge(self.sems[_e2], _c)
                    self.ops[e].append(w)
                    my[e2] = c

    def final_wait(self):
        for ln in DMALANES:
            c = self.cnt[ln]
            if c > 0:
                def w(h, _c=c, _ln=ln):
                    h.wait_ge(self.sems[_ln], _c)
                self.ops["sync"].append(w)

    def build(self, block, sems):
        self.sems = sems
        for e in ENGINES:
            if not self.ops[e]:
                continue
            lst = self.ops[e]

            def runner(h, _lst=lst):
                for fn in _lst:
                    fn(h)

            getattr(block, e)(runner)



def host_constants():
    HALF = 512
    theta = 1.0 / 10000.0 ** (np.arange(HALF, dtype=np.float64) / HALF)
    ang = np.arange(T, dtype=np.float64)[:, None] * theta[None, :]
    cos = np.cos(ang).astype(np.float32)
    sin = np.sin(ang).astype(np.float32)
    u = np.arange(896)[None, :]
    p = np.arange(128)[:, None]
    stair1 = np.where(u > p + 384, np.float32(-8e9), np.float32(0.0)).astype(np.float32)
    stairT = np.where(u < p + 384, np.float32(-8e9), np.float32(0.0)).astype(np.float32)
    ident = np.eye(128, dtype=np.float32).astype(ml_dtypes.bfloat16)
    identf = np.eye(128, dtype=np.float32)
    iota8 = np.broadcast_to(np.arange(8, dtype=np.float32), (128, 8)).copy()
    iotam99 = (iota8 - 99.0).astype(np.float32)
    ones128f = np.ones((1, 128), np.float32)
    ones128b = np.ones((1, 128), np.float32).astype(ml_dtypes.bfloat16)
    return dict(cos=cos, sin=sin, stair1=stair1, stairT=stairT, ident=ident,
                identf=identf, iota8=iota8, iotam99=iotam99, ones128f=ones128f,
                ones128b=ones128b)



class KB:
    def __init__(self, taps=(), phases=("prologue", "ffn1", "sparse", "ffn2"),
                 stop_after=None):
        self.nc = bass.Bass("TRN2", target_bir_lowering=False)
        self.p = Prog(self.nc)
        self.es = ExitStack()
        self.taps = set(taps)
        self.phases = phases
        self.stop_after = stop_after
        self._names = {}

    def dram(self, name, shape, dt=F32, kind="Internal"):
        return self.nc.dram_tensor(name, shape, dt, kind=kind).ap()

    def _uniq(self, name):
        n = self._names.get(name, 0)
        self._names[name] = n + 1
        return name if n == 0 else f"{name}__{n}"

    def sb(self, name, shape, dt=F32, stack=None):
        return (stack or self.es).enter_context(
            self.nc.sbuf_tensor(self._uniq(name), shape, dt))

    def ps(self, name, shape, dt=F32, stack=None):
        return (stack or self.es).enter_context(
            self.nc.psum_tensor(self._uniq(name), shape, dt))

    def tap(self, name, ap_sbuf, reads):
        if name not in self.taps:
            return
        shape = list(ap_sbuf.shape)
        hbm = self.nc.dram_tensor("tap_" + name, shape, ap_sbuf.tensor.dtype,
                                  kind="ExternalOutput").ap()
        self.p.step("sync", lambda h: h.dma_start(hbm, ap_sbuf),
                    reads=tuple(reads), writes=("tap_" + name,), dma=True, pclass="misc")

    def tap_dram(self, name, dram_ap, reads):
        if name not in self.taps:
            return
        shape = list(dram_ap.shape)
        hbm = self.nc.dram_tensor("tap_" + name, shape, dram_ap.tensor.dtype,
                                  kind="ExternalOutput").ap()
        st = ExitStack()
        rows, cols = shape
        tbuf = self.sb(f"tapb_{name}", [128, cols], dram_ap.tensor.dtype, stack=st)
        for r0 in range(0, rows, 128):
            rr = min(128, rows - r0)
            self.p.step("sync", lambda h, r0=r0, rr=rr: h.dma_start(
                tbuf[0:rr, :], dram_ap[r0:r0 + rr, :]),
                reads=tuple(reads), writes=(f"tapb_{name}",), dma=True, pclass="misc")
            self.p.step("sync", lambda h, r0=r0, rr=rr: h.dma_start(
                hbm[r0:r0 + rr, :], tbuf[0:rr, :]),
                reads=(f"tapb_{name}",), writes=("tap_" + name,), dma=True, pclass="misc")
        st.close()

    @staticmethod
    def _mmgroup(h, out_ap, pairs, start=True, stop=True):
        last = None
        n = len(pairs)
        for i, (l, r) in enumerate(pairs):
            last = h.matmul(out_ap, l, r, start=(start and i == 0),
                            stop=(stop and i == n - 1))
        return last

    def build(self):
        nc = self.nc
        self.x_in = self.dram("x", [T, C], kind="ExternalInput")
        shapes = {"ca_w": [C, 3 * C], "ca_b": [3 * C], "q_w": [C, C], "q_b": [C],
                  "k_w": [C, C], "k_b": [C], "v_w": [C, C], "v_b": [C],
                  "o_w": [C, C], "o_b": [C], "f_n1": [C], "f_wq": [C, C],
                  "f_wk": [C, 256], "f_wv": [C, 256], "f_wo": [C, C],
                  "f_n2": [C], "f_rw": [C, E], "f_rb": [E],
                  "f_w1": [E, C, FF], "f_b1": [E, FF], "f_w2": [E, C, FF],
                  "f_b2": [E, FF], "f_w3": [E, FF, C], "f_b3": [E, C]}
        self.w = {n: self.dram(n, s, kind="ExternalInput") for n, s in shapes.items()}
        cshapes = {"cos": [T, 512], "sin": [T, 512], "stair1": [128, 896],
                   "stairT": [128, 896], "iota8": [128, 8], "iotam99": [128, 8],
                   "ones128f": [1, 128], "identf": [128, 128]}
        self.cst = {n: self.dram("c_" + n, s, kind="ExternalInput")
                    for n, s in cshapes.items()}
        self.cst["ident"] = self.nc.dram_tensor("c_ident", [128, 128], BF16,
                                                kind="ExternalInput").ap()
        self.cst["ones128b"] = self.nc.dram_tensor("c_ones128b", [1, 128], BF16,
                                                   kind="ExternalInput").ap()
        self.y_out = self.nc.dram_tensor("y", [T, C], F32, kind="ExternalOutput").ap()
        self.xs1 = self.dram("xs1", [T, C])
        self.xs2 = self.dram("xs2", [T, C])
        self.cinit = self.dram("cinit", [T, C])
        self.bT_hbm = self.nc.dram_tensor("bT_hbm", [C, T], BF16, kind="Internal").ap()
        self.mflat = self.dram("mflat", [32, 128])
        self.wb = {}
        for n in ["f_wq", "f_wk", "f_wv", "f_wo", "f_w1", "f_w2", "f_w3",
                  "ca_w", "q_w", "k_w", "v_w", "o_w"]:
            self.wb[n] = self.nc.dram_tensor(n + "_b", shapes[n], BF16,
                                             kind="Internal").ap()
        self.qT2 = self.nc.dram_tensor("qT2", [C, T], BF16, kind="Internal").ap()
        self.kT2 = self.nc.dram_tensor("kT2", [C, T], BF16, kind="Internal").ap()
        self.vT2 = self.nc.dram_tensor("vT2", [C, T], BF16, kind="Internal").ap()

        self.c_stair1 = self.sb("c_stair1s", [128, 896])
        self.c_stairT = self.sb("c_stairTs", [128, 896])
        self.c_ident = self.sb("c_idents", [128, 128], BF16)
        self.c_identf = self.sb("c_identfs", [128, 128])
        self.c_iota8 = self.sb("c_iota8s", [128, 8])
        self.c_iotam99 = self.sb("c_iotam99s", [128, 8])
        self.c_ones128f = self.sb("c_ones128fs", [1, 128])
        self.c_ones128b = self.sb("c_ones128bs", [1, 128], BF16)
        self.epscol5 = self.sb("epscol5", [128, 1])
        self.epscol6 = self.sb("epscol6", [128, 1])
        self.normw1 = self.sb("normw1", [128, C])
        self.normw2 = self.sb("normw2", [128, C])
        self.muS = self.sb("muS", [128, NT])
        self.rstdS = self.sb("rstdS", [128, NT])
        self.comb = self.sb("comb", [128, NT * E])
        self.routw = self.sb("routw", [128, E * 8])
        self.routwh = self.sb("routwh", [128, E * 8], BF16)
        self.routwl = self.sb("routwl", [128, E * 8], BF16)
        self.routbbc = self.sb("routbbc", [128, E])
        self.b1sb = self.sb("b1sb", [128, E * (FF // 128)])
        self.b2sb = self.sb("b2sb", [128, E * (FF // 128)])
        self.b3cat = self.sb("b3cat", [1, E * C], BF16)
        self.obrow = self.sb("obrow", [1, C], BF16)
        self.cabT = self.sb("cabT", [128, 24])
        self.qb8T = self.sb("qb8T", [128, 8])
        self.kbT = self.sb("kbT", [128, 8])
        self.vbT = self.sb("vbT", [128, 8])

        self._load_persistent()
        self.cvt_stack = None
        if "prologue" in self.phases:
            self.cvt_stack = ExitStack()
            self.cvt_stage = [self.sb(f"cvt_st{i}", [128, 2048], stack=self.cvt_stack)
                              for i in range(3)]
            self.cvt_out = [self.sb(f"cvt_out{i}", [128, 2048], BF16, stack=self.cvt_stack)
                            for i in range(3)]
            self._convert_weights("qkv")
        if "ffn1" in self.phases:
            self._ffn_call(self.x_in, self.xs1, "A")
        elif "ffn1f1" in self.phases:
            self.stop_after = "f1"
            self._ffn_f12(self.x_in, "A")
            self.p.barrier()
            self.f12_stack.close()
        elif "ffn1f2c" in self.phases:
            self.stop_after = "f2c"
            self._ffn_f12(self.x_in, "A")
            self.p.barrier()
            self.f12_stack.close()
        elif "ffn1f2d" in self.phases:
            self.stop_after = "f2d"
            self._ffn_f12(self.x_in, "A")
            self.p.barrier()
            self.f12_stack.close()
        elif "ffn1f2" in self.phases:
            self.stop_after = "f2"
            self._ffn_f12(self.x_in, "A")
            self.p.barrier()
            self.f12_stack.close()
        elif "ffn1f12" in self.phases:
            self._ffn_f12(self.x_in, "A")
            self.p.barrier()
            self.f12_stack.close()
        if "sparse" in self.phases:
            self._sparse_slot(self.xs1, self.xs2, "S")
        elif "sparse_s1" in self.phases:
            self._sparse_s1(self.xs1, "S")
            self.p.barrier()
        if "ffn2" in self.phases:
            self._ffn_call(self.xs2, self.y_out, "B")
        if self.cvt_stack is not None:
            self.p.barrier()
            self.cvt_stack.close()
            self.cvt_stack = None
        self.p.final_wait()

        sems = {e: self.es.enter_context(self.nc.semaphore("sem_" + e))
                for e in ALLSEMS}
        block = self.es.enter_context(self.nc.Block())
        self.p.build(block, sems)
        self.es.close()
        return self.nc

    def _load_persistent(self):
        p, w, cst = self.p, self.w, self.cst

        def ld(dst, src, key, reads=(), pc="full"):
            p.step("sync", lambda h: h.dma_start(dst, src), reads=reads,
                   writes=(key,), dma=True, pclass=pc)

        def ldT(dst, src, key, reads=()):
            p.step("sync", lambda h: h.dma_start_transpose(dst, src), reads=reads,
                   writes=(key,), dma=True, pclass="misc")

        ld(self.c_stair1[:], cst["stair1"], "c_stair1")
        ld(self.c_stairT[:], cst["stairT"], "c_stairT")
        ld(self.c_ident[:], cst["ident"], "c_ident")
        ld(self.c_identf[:], cst["identf"], "c_identf")
        ld(self.c_iota8[:], cst["iota8"], "c_iota8")
        ld(self.c_iotam99[:], cst["iotam99"], "c_iotam99")
        ld(self.c_ones128f[:], cst["ones128f"], "c_ones128f", pc="p1")
        ld(self.c_ones128b[:], cst["ones128b"], "c_ones128b", pc="p1")
        p.step("vector", lambda h: h.memset(self.epscol5[:], 1e-5), writes=("epscol5",))
        p.step("vector", lambda h: h.memset(self.epscol6[:], 1e-6), writes=("epscol6",))
        ld(self.normw1[:], w["f_n1"].rearrange("c -> () c").broadcast_to((128, C)), "normw1")
        ld(self.normw2[:], w["f_n2"].rearrange("c -> () c").broadcast_to((128, C)), "normw2")
        ld(_s3(self.routw[:], 8), _t3(w["f_rw"]), "routw")
        ld(self.routbbc[:], w["f_rb"].rearrange("e -> () e").broadcast_to((128, E)), "routbbc")
        p.step("vector", lambda h: h.tensor_copy(self.routwh[:], self.routw[:]),
               reads=("routw",), writes=("routwh",))
        p.step("vector", lambda h: h.tensor_tensor(
            self.routwl[:], self.routw[:], self.routwh[:], ALU.subtract),
            reads=("routw", "routwh"), writes=("routwl",))
        self.tap("routwh", self.routwh[:], ["routwh"])
        self.tap("routwf", self.routw[:], ["routw"])
        ptmp = ExitStack()
        b1nat = self.sb("b1nat", [8, FF], stack=ptmp)
        b2nat = self.sb("b2nat", [8, FF], stack=ptmp)
        ptp = self.ps("ptp_tmp", [128, 512], stack=ptmp)
        ld(b1nat[:], w["f_b1"], "b1nat", pc="misc")
        ld(b2nat[:], w["f_b2"], "b2nat", pc="misc")
        for nat, knat, dst, kdst in ((b1nat, "b1nat", self.b1sb, "b1sb"),
                                     (b2nat, "b2nat", self.b2sb, "b2sb")):
            for ft in range(32):
                tb = ft % 2
                p.step("tensor", lambda h, nat=nat, ft=ft, tb=tb: h.transpose(
                    ptp[:, tb * 128: tb * 128 + 8], nat[:, ft * 128:(ft + 1) * 128],
                    self.c_identf[0:8, 0:8]),
                    reads=(knat, "c_identf"), writes=("ptp",))
                p.step("vector", lambda h, dst=dst, ft=ft, tb=tb: h.tensor_copy(
                    dst[:, ft: 224 + ft + 1: 32], ptp[:, tb * 128: tb * 128 + 8]),
                    reads=("ptp",), writes=(kdst + str(ft),))
        b3f = self.sb("b3f_tmp", [1, E * C], stack=ptmp)
        ld(b3f[:], w["f_b3"].rearrange("e c -> () (e c)"), "b3f_tmp", pc="misc")
        p.step("vector", lambda h: h.tensor_copy(self.b3cat[:], b3f[:]),
               reads=("b3f_tmp",), writes=("b3cat",))
        obf = self.sb("obf_tmp", [1, C], stack=ptmp)
        ld(obf[:], w["o_b"].rearrange("c -> () c"), "obf_tmp", pc="misc")
        p.step("vector", lambda h: h.tensor_copy(self.obrow[:], obf[:]),
               reads=("obf_tmp",), writes=("obrow",))
        cabnat = self.sb("cabnat", [24, 128], stack=ptmp)
        qbnat = self.sb("qbnat", [8, 128], stack=ptmp)
        kbnat = self.sb("kbnat", [8, 128], stack=ptmp)
        vbnat = self.sb("vbnat", [8, 128], stack=ptmp)
        ld(cabnat[:], w["ca_b"].rearrange("(mt p) -> mt p", p=128), "cabnat", pc="misc")
        ld(qbnat[:], w["q_b"].rearrange("(mt p) -> mt p", p=128), "qbnat", pc="misc")
        ld(kbnat[:], w["k_b"].rearrange("(mt p) -> mt p", p=128), "kbnat", pc="misc")
        ld(vbnat[:], w["v_b"].rearrange("(mt p) -> mt p", p=128), "vbnat", pc="misc")
        p.step("tensor", lambda h: h.transpose(
            ptp[:, 256:280], cabnat[:], self.c_identf[0:24, 0:24]),
            reads=("cabnat", "c_identf"), writes=("ptp",))
        p.step("vector", lambda h: h.tensor_copy(self.cabT[:], ptp[:, 256:280]),
               reads=("ptp",), writes=("cabT",))
        p.step("tensor", lambda h: h.transpose(
            ptp[:, 280:288], qbnat[:], self.c_identf[0:8, 0:8]),
            reads=("qbnat", "c_identf"), writes=("ptp",))
        p.step("vector", lambda h: h.tensor_scalar(
            self.qb8T[:], ptp[:, 280:288], 0.125, None, ALU.mult),
            reads=("ptp",), writes=("qb8T",))
        p.step("tensor", lambda h: h.transpose(
            ptp[:, 288:296], kbnat[:], self.c_identf[0:8, 0:8]),
            reads=("kbnat", "c_identf"), writes=("ptp",))
        p.step("vector", lambda h: h.tensor_copy(self.kbT[:], ptp[:, 288:296]),
               reads=("ptp",), writes=("kbT",))
        p.step("tensor", lambda h: h.transpose(
            ptp[:, 296:304], vbnat[:], self.c_identf[0:8, 0:8]),
            reads=("vbnat", "c_identf"), writes=("ptp",))
        p.step("vector", lambda h: h.tensor_copy(self.vbT[:], ptp[:, 296:304]),
               reads=("ptp",), writes=("vbT",))
        p.barrier()
        ptmp.close()

    def _convert_weights(self, part):
        g = self._convert_weights_gen(part)
        for _ in g:
            pass

    def _convert_weights_gen(self, part):
        p = self.p
        stage, out = self.cvt_stage, self.cvt_out
        jobs = []

        def add(src, dst, rows, cols, scale=None):
            for r in range(rows // 128):
                c0 = 0
                while c0 < cols:
                    cw = min(2048, cols - c0)
                    jobs.append((src[r * 128:(r + 1) * 128, c0:c0 + cw],
                                 dst[r * 128:(r + 1) * 128, c0:c0 + cw], cw, scale))
                    c0 += cw

        if part == "qkv":
            for n in ["f_wq", "f_wk", "f_wv"]:
                sh = self.w[n].shape
                add(self.w[n], self.wb[n], sh[0], sh[1])
        else:
            for n in ["f_wo", "ca_w", "k_w", "v_w", "o_w"]:
                sh = self.w[n].shape
                add(self.w[n], self.wb[n], sh[0], sh[1])
            add(self.w["q_w"], self.wb["q_w"], C, C, scale=0.125)
            for n in ["f_w1", "f_w2", "f_w3"]:
                for e in range(E):
                    sh = self.w[n].shape
                    add(self.w[n][e], self.wb[n][e], sh[1], sh[2])
        for i, (src, dst, cw, scale) in enumerate(jobs):
            sl = i % 3
            kst, kout = f"cvt_st{sl}", f"cvt_out{sl}"
            p.step("sync", lambda h, s=stage[sl], a=src, c=cw: h.dma_start(s[:, 0:c], a),
                   writes=(kst,), dma=True, tag="cvtin")
            if scale is None:
                p.step("gpsimd", lambda h, s=stage[sl], o=out[sl], c=cw: h.tensor_copy(
                    o[:, 0:c], s[:, 0:c]),
                    reads=(kst,), writes=(kout,), tag="cvtcast")
            else:
                p.step("gpsimd", lambda h, s=stage[sl], o=out[sl], c=cw, sc=scale: h.tensor_scalar(
                    o[:, 0:c], s[:, 0:c], sc, None, ALU.mult),
                    reads=(kst,), writes=(kout,), tag="cvtscale")
            p.step("sync", lambda h, o=out[sl], d=dst, c=cw: h.dma_start(d, o[:, 0:c]),
                   reads=(kout,), writes=(dst.tensor.name + "_done",), dma=True, tag="cvtout")
            yield

    def _ffn_call(self, x_hbm, out_hbm, pref):
        self._ffn_f12(x_hbm, pref)
        self.p.barrier()
        self.f12_stack.close()
        if self.cvt_stack is not None:
            self.cvt_stack.close()
            self.cvt_stack = None
        self._ffn_f3(out_hbm, pref)
        self.p.barrier()

    def _ffn_f12(self, x_hbm, pref):
        p = self.p
        P = pref
        st = self.f12_stack = ExitStack()
        KT2 = [self.sb(P + f"KT2_{i}", [65, T], BF16, stack=st) for i in range(4)]
        Vx = self.sb(P + "Vx", [128, NT * 260], BF16, stack=st)
        QTh = self.nc.dram_tensor(P + "QTh", [C, T], BF16, kind="Internal").ap()
        OCh = self.nc.dram_tensor(P + "OCh", [C, T], BF16, kind="Internal").ap()
        PA = self.ps(P + "PA_f12", [128, 2048], stack=st)
        PB = self.ps(P + "PB_f12", [128, 1024], stack=st)
        PT = self.ps(P + "PT_f12", [128, 2048], BF16, stack=st)
        A = [f"{P}PAb{i}" for i in range(4)]
        Bk = [f"{P}PBb{i}" for i in range(2)]
        Tk = [f"{P}PTb{i}" for i in range(2)]

        ph = ExitStack()
        wqb = self.sb("wqb", [128, 8 * C], BF16, stack=ph)
        wkb = self.sb("wkb", [128, 8 * 256], BF16, stack=ph)
        wvb = self.sb("wvb", [128, 8 * 256], BF16, stack=ph)
        xt = [self.sb(f"xt{i}", [128, C], stack=ph) for i in range(2)]
        csb = [self.sb(f"csb{i}", [128, 512], stack=ph) for i in range(2)]
        snb = [self.sb(f"snb{i}", [128, 512], stack=ph) for i in range(2)]
        h0t = [self.sb(f"h0t{i}", [128, C], stack=ph) for i in range(2)]
        at = [self.sb(f"at{i}", [128, C], stack=ph) for i in range(2)]
        xrb = [self.sb(f"xrb{i}", [128, C], BF16, stack=ph) for i in range(2)]
        sq = self.sb("sq_scr", [128, C], stack=ph)
        rs = self.sb("rs_scr", [128, C], stack=ph)
        smallc = self.sb("smallc", [128, 8], stack=ph)
        xrT = self.sb("xrT", [128, 8 * 512], BF16, stack=ph)
        qstg = [self.sb(f"qstg{i}", [128, 512], BF16, stack=ph) for i in range(2)]

        p.step("sync", lambda h: h.dma_start(
            _s3(wqb[:], 1024), _t3(self.wb["f_wq"])),
            reads=("f_wq_b_done",), writes=("wqb",), dma=True)
        p.step("sync", lambda h: h.dma_start(
            _s3(wkb[:], 256), _t3(self.wb["f_wk"])),
            reads=("f_wk_b_done",), writes=("wkb",), dma=True)
        p.step("sync", lambda h: h.dma_start(
            _s3(wvb[:], 256), _t3(self.wb["f_wv"])),
            reads=("f_wv_b_done",), writes=("wvb",), dma=True)
        p.step("vector", lambda h: h.memset(Vx[:], 1.0), writes=(f"{P}Vxones",))
        for g in range(G):
            p.step("vector", lambda h, g=g: h.memset(KT2[g][64:65, :], 1.0),
                   writes=(f"{P}KTones{g}",))

        for tc in range(NCH):
            for qq in range(4):
                tt = tc * 4 + qq
                b = tt % 2
                kx, kcs = f"xt{b}", f"cs{b}"
                kh0, ka, kxr = f"h0t{b}", f"at{b}", f"xrb{b}"
                p.step("sync", lambda h, b=b, tt=tt: h.dma_start(
                    xt[b][:], x_hbm[tt * 128:(tt + 1) * 128, :]),
                    writes=(kx,), dma=True, tag=f"{P}x{tt}")
                p.step("sync", lambda h, b=b, tt=tt: h.dma_start(
                    csb[b][:], self.cst["cos"][tt * 128:(tt + 1) * 128, :]),
                    writes=(kcs + "c",), dma=True)
                p.step("sync", lambda h, b=b, tt=tt: h.dma_start(
                    snb[b][:], self.cst["sin"][tt * 128:(tt + 1) * 128, :]),
                    writes=(kcs + "s",), dma=True)
                self._ln_stats(xt[b], kx, self.muS[:, tt:tt + 1], f"{P}mu{tt}",
                               self.rstdS[:, tt:tt + 1], f"{P}rstd{tt}",
                               sq, smallc)
                p.step("vector", lambda h, b=b, tt=tt: h.tensor_scalar(
                    h0t[b][:], xt[b][:], self.muS[:, tt:tt + 1],
                    self.rstdS[:, tt:tt + 1], ALU.subtract, ALU.mult),
                    reads=(kx, f"{P}mu{tt}", f"{P}rstd{tt}"), writes=(kh0,))
                p.step("scalar", lambda h, b=b: h.activation(
                    sq[:], h0t[b][:], AF.Square, accum_out=smallc[:, 6:7]),
                    reads=(kh0,), writes=("sm6", "sqA", "sqB"))
                p.step("vector", lambda h: h.tensor_scalar(
                    smallc[:, 7:8], smallc[:, 6:7], 1.0 / C, None, ALU.mult),
                    reads=("sm6",), writes=("sm7",))
                p.step("scalar", lambda h: h.activation(
                    smallc[:, 0:1], smallc[:, 7:8], AF.Sqrt, bias=self.epscol6[:]),
                    reads=("sm7", "epscol6"), writes=("sm0",))
                p.step("vector", lambda h: h.reciprocal(smallc[:, 1:2], smallc[:, 0:1]),
                    reads=("sm0",), writes=("sm1",))
                p.step("vector", lambda h, b=b: h.scalar_tensor_tensor(
                    at[b][:], h0t[b][:], smallc[:, 1:2], self.normw1[:],
                    ALU.mult, ALU.mult),
                    reads=(kh0, "sm1", "normw1"), writes=(ka,))
                p.step("vector", lambda h, b=b: h.tensor_tensor(
                    sq[:, 0:512], at[b][:, 0::2], csb[b][:], ALU.mult),
                    reads=(ka, kcs + "c"), writes=("sqA",))
                p.step("vector", lambda h, b=b: h.tensor_tensor(
                    sq[:, 512:1024], at[b][:, 1::2], snb[b][:], ALU.mult),
                    reads=(ka, kcs + "s"), writes=("sqB",))
                p.step("vector", lambda h, b=b: h.tensor_tensor(
                    xrb[b][:, 0:512], sq[:, 0:512], sq[:, 512:1024], ALU.subtract),
                    reads=("sqA", "sqB"), writes=(kxr + "a",))
                p.step("vector", lambda h, b=b: h.tensor_tensor(
                    rs[:, 0:512], at[b][:, 0::2], snb[b][:], ALU.mult),
                    reads=(ka, kcs + "s"), writes=("rsA",))
                p.step("vector", lambda h, b=b: h.tensor_tensor(
                    rs[:, 512:1024], at[b][:, 1::2], csb[b][:], ALU.mult),
                    reads=(ka, kcs + "c"), writes=("rsB",))
                p.step("vector", lambda h, b=b: h.tensor_tensor(
                    xrb[b][:, 512:1024], rs[:, 0:512], rs[:, 512:1024], ALU.add),
                    reads=("rsA", "rsB"), writes=(kxr + "b",))
                for kt in range(8):
                    tb = kt % 2
                    p.step("tensor", lambda h, b=b, kt=kt, tb=tb: h.transpose(
                        PT[:, tb * 1024: tb * 1024 + 128],
                        xrb[b][:, kt * 128:(kt + 1) * 128], self.c_ident[:]),
                        reads=(kxr + "a", kxr + "b", "c_ident"), writes=(Tk[tb],))
                    p.step("vector", lambda h, kt=kt, qq=qq, tb=tb: h.tensor_copy(
                        xrT[:, kt * 512 + qq * 128: kt * 512 + qq * 128 + 128],
                        PT[:, tb * 1024: tb * 1024 + 128]),
                        reads=(Tk[tb],), writes=(f"xrT{kt}q{qq}",))
                vb = 2 + tt % 2
                p.step("tensor", lambda h, qq=qq, vb=vb: self._mmgroup(
                    h, PA[:, (vb - 2) * 512 + 1024: (vb - 2) * 512 + 1024 + 256],
                    [(xrT[:, kt * 512 + qq * 128: kt * 512 + qq * 128 + 128],
                      wvb[:, kt * 256:(kt + 1) * 256]) for kt in range(8)]),
                    reads=tuple(f"xrT{kt}q{qq}" for kt in range(8)) + ("wvb",),
                    writes=(A[vb],))
                for g in range(G):
                    p.step("scalar", lambda h, g=g, tt=tt, vb=vb: h.copy(
                        Vx[:, tt * 260 + g * 65: tt * 260 + g * 65 + 64],
                        PA[:, (vb - 2) * 512 + 1024 + g * 64:
                           (vb - 2) * 512 + 1024 + (g + 1) * 64]),
                        reads=(A[vb], f"{P}Vxones"), writes=(f"{P}Vx{tt}g{g}",))
            xr_all = tuple(f"xrT{kt}q{qq}" for kt in range(8) for qq in range(4))
            for mt in range(8):
                bq = mt % 2
                ob = mt % 2
                p.step("tensor", lambda h, mt=mt, bq=bq: self._mmgroup(
                    h, PA[:, bq * 512: bq * 512 + 512],
                    [(wqb[:, kt * 1024 + mt * 128: kt * 1024 + mt * 128 + 128],
                      xrT[:, kt * 512:(kt + 1) * 512]) for kt in range(8)]),
                    reads=xr_all + ("wqb",), writes=(A[bq],))
                p.step("scalar", lambda h, bq=bq, ob=ob: h.copy(
                    qstg[ob][:], PA[:, bq * 512: bq * 512 + 512]),
                    reads=(A[bq],), writes=(f"qstg{ob}",))
                p.step("sync", lambda h, mt=mt, ob=ob, tc=tc: h.dma_start(
                    QTh[mt * 128:(mt + 1) * 128, tc * 512:(tc + 1) * 512],
                    qstg[ob][:]),
                    reads=(f"qstg{ob}",), writes=(f"{P}QTm{mt}c{tc}",), dma=True)
            for mt in range(2):
                bq = mt % 2
                p.step("tensor", lambda h, mt=mt, bq=bq: self._mmgroup(
                    h, PA[:, bq * 512: bq * 512 + 512],
                    [(wkb[:, kt * 256 + mt * 128: kt * 256 + mt * 128 + 128],
                      xrT[:, kt * 512:(kt + 1) * 512]) for kt in range(8)]),
                    reads=xr_all + ("wkb",), writes=(A[bq],))
                for gg in range(2):
                    g = 2 * mt + gg
                    p.step("scalar", lambda h, g=g, gg=gg, bq=bq, tc=tc: h.copy(
                        KT2[g][0:64, tc * 512:(tc + 1) * 512],
                        PA[gg * 64:(gg + 1) * 64, bq * 512: bq * 512 + 512]),
                        reads=(A[bq],), writes=(f"{P}KT{g}c{tc}",))
        if (P + "QT") in self.taps:
            self.tap_dram(P + "QT", QTh,
                          [f"{P}QTm{mt}c{tc}" for mt in range(8) for tc in range(8)])
        self.tap(P + "KT0", KT2[0][:], [f"{P}KT0c{tc}" for tc in range(8)])

        self.tap(P + "Vx", Vx[:], [f"{P}Vx{tt}g{g}" for tt in range(NT) for g in range(G)])
        if P == "A" and self.cvt_stack is not None and self.stop_after == "f1":
            self._convert_weights("moe")
        p.barrier()
        ph.close()
        if self.stop_after == "f1":
            return

        ph = ExitStack()
        qhT = [self.sb(f"f2qhT{i}", [65, T], BF16, stack=ph) for i in range(2)]
        mbuf = self.sb("mbuf", [128, 32], stack=ph)
        mrow = self.sb("mrow", [1, T], stack=ph)
        m32 = self.sb("m32", [32, 128], stack=ph)
        pbuf = [self.sb(f"pbuf{i}", [128, 512], BF16, stack=ph) for i in range(3)]
        rrow = self.sb("rrow", [1, 512], stack=ph)
        rbs = self.sb("rbs", [64, 512], stack=ph)
        ohT = self.sb("ohT", [64, T], BF16, stack=ph)
        sc2 = self.sb("f2small", [128, 2], stack=ph)
        KTkeys = lambda g: tuple(f"{P}KT{g}c{tc}" for tc in range(8))
        Vxkeys = lambda g: tuple(f"{P}Vx{tt}g{g}" for tt in range(NT))
        QTkeys = tuple(f"{P}QTm{mt}c{tc}" for mt in range(8) for tc in range(8))

        def gen_pass1(hh):
            g = hh // 4
            qb = hh % 2
            kq = f"f2qhT{qb}"
            p.step("sync", lambda h, hh=hh, qb=qb: h.dma_start(
                qhT[qb][0:64, :], QTh[hh * 64:(hh + 1) * 64, :]),
                reads=QTkeys, writes=(kq,), dma=True, pclass="p64")
            yield
            for it in range(NT):
                njc = it // 4 + 1
                for jc in range(njc):
                    bq = jc % 2
                    p.step("tensor", lambda h, it=it, jc=jc, bq=bq, qb=qb,
                           g=g: h.matmul(
                        PB[:, bq * 512: bq * 512 + 512],
                        qhT[qb][0:64, it * 128:(it + 1) * 128],
                        KT2[g][0:64, jc * 512:(jc + 1) * 512],
                        start=True, stop=True),
                        reads=(kq,) + KTkeys(g), writes=(Bk[bq],), tag="p1mm")
                    yield
                    if jc == it // 4:
                        dlt = 128 * (it % 4)
                        p.step("vector", lambda h, bq=bq, dlt=dlt: h.tensor_tensor(
                            PB[:, bq * 512: bq * 512 + 512],
                            PB[:, bq * 512: bq * 512 + 512],
                            self.c_stair1[:, 384 - dlt: 896 - dlt], ALU.add),
                            reads=(Bk[bq], "c_stair1"), writes=(Bk[bq],))
                        yield
                    if jc == 0:
                        p.step("vector", lambda h, it=it, bq=bq: h.tensor_reduce(
                            mbuf[:, it:it + 1], PB[:, bq * 512: bq * 512 + 512],
                            AX.X, ALU.max),
                            reads=(Bk[bq],), writes=(f"mb{it}",))
                        yield
                    else:
                        p.step("vector", lambda h, bq=bq: h.tensor_reduce(
                            sc2[:, 0:1], PB[:, bq * 512: bq * 512 + 512],
                            AX.X, ALU.max),
                            reads=(Bk[bq],), writes=("f2sm",))
                        yield
                        p.step("vector", lambda h, it=it: h.tensor_tensor(
                            mbuf[:, it:it + 1], mbuf[:, it:it + 1],
                            sc2[:, 0:1], ALU.max),
                            reads=(f"mb{it}", "f2sm"), writes=(f"mb{it}",))
                        yield
            mkeys = tuple(f"mb{it}" for it in range(NT))
            p.step("tensor", lambda h: h.transpose(
                PB[0:32, 512:640], mbuf[:], self.c_identf[:]),
                reads=mkeys + ("c_identf",), writes=(Bk[1],), tag="mT")
            yield
            p.step("vector", lambda h: h.tensor_copy(m32[:], PB[0:32, 512:640]),
                   reads=(Bk[1],), writes=("m32",))
            yield
            p.step("sync", lambda h: h.dma_start(self.mflat, m32[:]),
                   reads=("m32",), writes=("mflat",), dma=True, pclass="misc")
            yield
            p.step("sync", lambda h: h.dma_start(
                mrow[:], self.mflat.rearrange("p n -> () (p n)")),
                reads=("mflat",), writes=("mrow",), dma=True, pclass="p1")
            yield
            p.step("scalar", lambda h, qb=qb: h.activation(
                qhT[qb][64:65, :], mrow[:], AF.Copy, scale=-1.0),
                reads=("mrow",), writes=(kq + "m",))
            yield

        def gen_pass2(hh):
            g = hh // 4
            qb = hh % 2
            kq = f"f2qhT{qb}"
            pend = [None]

            def make_finals(isp):
                po = 2 + isp % 2
                po2 = 2 + (isp + 1) % 2
                poo = 1024 + (isp % 2) * 512

                def finals():
                    p.step("vector", lambda h, poo=poo: h.reciprocal(
                        rrow[:], PA[64:65, poo: poo + 512]),
                        reads=(A[po],), writes=("rrow",))
                    p.step("tensor", lambda h, po2=po2: h.matmul(
                        PA[0:64, po2 * 512: po2 * 512 + 512],
                        self.c_ones128f[:, 0:64], rrow[:],
                        start=True, stop=True),
                        reads=("rrow", "c_ones128f"), writes=(A[po2],), tag="rb")
                    p.step("vector", lambda h, po2=po2: h.tensor_copy(
                        rbs[:], PA[0:64, po2 * 512: po2 * 512 + 512]),
                        reads=(A[po2],), writes=("rbs",))
                    p.step("vector", lambda h, isp=isp, poo=poo: h.tensor_tensor(
                        ohT[:, isp * 512:(isp + 1) * 512],
                        PA[0:64, poo: poo + 512], rbs[:], ALU.mult),
                        reads=(A[po], "rbs"), writes=("ohT",))
                return finals

            for isp in range(8):
                njc = 4 * isp + 4
                lastjc = njc - 1
                po = 2 + isp % 2
                poo = 1024 + (isp % 2) * 512

                def emit_pv(jc, lastjc=lastjc, g=g, po=po, poo=poo):
                    pb3 = jc % 3
                    p.step("tensor", lambda h, jc=jc, pb3=pb3, g=g,
                           lastjc=lastjc, poo=poo: h.matmul(
                        PA[0:65, poo: poo + 512],
                        Vx[:, jc * 260 + g * 65: jc * 260 + (g + 1) * 65],
                        pbuf[pb3][:], start=(jc == 0), stop=(jc == lastjc)),
                        reads=(f"pbuf{pb3}",) + Vxkeys(g), writes=(A[po],),
                        tag="pv")

                for jc in range(njc):
                    bq = jc % 2
                    pb3 = jc % 3
                    p.step("tensor", lambda h, isp=isp, jc=jc, bq=bq, qb=qb,
                           g=g: h.matmul(
                        PA[:, bq * 512: bq * 512 + 512],
                        KT2[g][:, jc * 128:(jc + 1) * 128],
                        qhT[qb][:, isp * 512:(isp + 1) * 512],
                        start=True, stop=True),
                        reads=(kq, kq + "m", f"{P}KTones{g}") + KTkeys(g),
                        writes=(A[bq],), tag="p2mm")
                    yield
                    if jc // 4 == isp:
                        d2 = 128 * (jc % 4)
                        p.step("vector", lambda h, bq=bq, d2=d2: h.tensor_tensor(
                            PA[:, bq * 512: bq * 512 + 512],
                            PA[:, bq * 512: bq * 512 + 512],
                            self.c_stairT[:, 384 - d2: 896 - d2], ALU.add),
                            reads=(A[bq], "c_stairT"), writes=(A[bq],))
                        yield
                    p.step("scalar", lambda h, bq=bq, pb3=pb3: h.activation(
                        pbuf[pb3][:], PA[:, bq * 512: bq * 512 + 512],
                        AF.Exp, scale=0.125),
                        reads=(A[bq],), writes=(f"pbuf{pb3}",))
                    yield
                    if jc == 1 and pend[0] is not None:
                        pend[0]()
                        pend[0] = None
                        yield
                    if jc >= 2:
                        emit_pv(jc - 2)
                        yield
                emit_pv(njc - 2)
                emit_pv(njc - 1)
                yield
                pend[0] = make_finals(isp)
            pend[0]()
            p.step("sync", lambda h, hh=hh: h.dma_start(
                OCh[hh * 64:(hh + 1) * 64, :], ohT[:]),
                reads=("ohT",), writes=(f"{P}OC{hh}",), dma=True, pclass="p64")
            yield

        cvtg = None
        if P == "A" and self.cvt_stack is not None:
            cvtg = self._convert_weights_gen("moe")
        for _ in gen_pass1(0):
            pass
        for hh in range(H):
            g2 = gen_pass2(hh)
            g1 = gen_pass1(hh + 1) if hh + 1 < H else None
            k = 0
            while True:
                done2 = False
                for _ in range(3):
                    try:
                        next(g2)
                    except StopIteration:
                        done2 = True
                        break
                if g1 is not None:
                    for _ in range(2):
                        try:
                            next(g1)
                        except StopIteration:
                            g1 = None
                            break
                if cvtg is not None:
                    k += 1
                    if k % 4 == 0:
                        try:
                            next(cvtg)
                        except StopIteration:
                            cvtg = None
                if done2:
                    if g1 is not None:
                        for _ in g1:
                            pass
                    break
        if cvtg is not None:
            for _ in cvtg:
                pass
        ockeys = tuple(f"{P}OC{hh}" for hh in range(H))
        if (P + "ocT") in self.taps:
            self.tap_dram(P + "ocT", OCh, ockeys)
        p.barrier()
        ph.close()
        if self.stop_after == "f2":
            return

        ph = ExitStack()
        wob = self.sb("wob", [128, 8 * C], BF16, stack=ph)
        occ = [self.sb(f"occ{i}", [128, 8 * 128], BF16, stack=ph) for i in range(2)]
        xt2 = [self.sb(f"xt2{i}", [128, C], stack=ph) for i in range(2)]
        h0r = self.sb("h0r", [128, C], stack=ph)
        ht = self.sb("ht", [128, C], stack=ph)
        c0t = [self.sb(f"c0t{i}", [128, C], stack=ph) for i in range(2)]
        bt = self.sb("bt", [128, C], stack=ph)
        bthi = self.sb("bthi", [128, C], BF16, stack=ph)
        btlo = self.sb("btlo", [128, C], BF16, stack=ph)
        blosb = self.sb("blosb", [128, C], BF16, stack=ph)
        lgs = self.sb("lgs", [128, 8], stack=ph)
        lgall = self.sb("lgall", [128, NT * 8], stack=ph)
        dbg2 = self.sb("dbg2", [128, NT * 4], stack=ph)
        btb = [self.sb(f"btb{i}", [128, C], BF16, stack=ph) for i in range(2)]
        sq2 = self.sb("sq2_scr", [128, C], stack=ph)
        smallc2 = self.sb("smallc2", [128, 8], stack=ph)
        tp8 = self.sb("tp8", [128, 8], stack=ph)
        self.lg2scr = self.sb("lg2scr", [128, 8], stack=ph)

        p.step("sync", lambda h: h.dma_start(
            _s3(wob[:], 1024), _t3(self.wb["f_wo"])),
            reads=("f_wo_b_done",), writes=("wob",), dma=True)

        SUB = {"f2c": 1, "f2d": 2}.get(self.stop_after, 9)
        for tt in range(NT):
            b = tt % 2
            p.step("sync", lambda h, b=b, tt=tt: h.dma_start(
                xt2[b][:], x_hbm[tt * 128:(tt + 1) * 128, :]),
                writes=(f"xt2{b}",), dma=True)
            p.step("sync", lambda h, b=b, tt=tt: h.dma_start(
                _s3(occ[b][:], 128), _t3(OCh[:, tt * 128:(tt + 1) * 128])),
                reads=ockeys, writes=(f"occ{b}",), dma=True)
            for n0 in range(2):
                p.step("tensor", lambda h, b=b, n0=n0: self._mmgroup(
                    h, PB[:, n0 * 512:(n0 + 1) * 512],
                    [(occ[b][:, kt * 128:(kt + 1) * 128],
                      wob[:, kt * C + n0 * 512: kt * C + n0 * 512 + 512])
                     for kt in range(8)]),
                    reads=(f"occ{b}", "wob"), writes=(Bk[n0],), tag="womm")
            p.step("vector", lambda h, b=b, tt=tt: h.tensor_scalar(
                h0r[:], xt2[b][:], self.muS[:, tt:tt + 1], self.rstdS[:, tt:tt + 1],
                ALU.subtract, ALU.mult),
                reads=(f"xt2{b}", f"{P}mu{tt}", f"{P}rstd{tt}"), writes=("h0r",))
            for n0 in range(2):
                p.step("vector", lambda h, n0=n0: h.tensor_tensor(
                    ht[:, n0 * 512:(n0 + 1) * 512], PB[:, n0 * 512:(n0 + 1) * 512],
                    h0r[:, n0 * 512:(n0 + 1) * 512], ALU.add),
                    reads=(Bk[n0], "h0r"), writes=(f"ht{n0}",))
            p.step("vector", lambda h, b=b: h.scalar_tensor_tensor(
                c0t[b][:], ht[:], 0.5, xt2[b][:], ALU.mult, ALU.add),
                reads=("ht0", "ht1", f"xt2{b}"), writes=(f"c0t{b}",))
            p.step("sync", lambda h, b=b, tt=tt: h.dma_start(
                self.cinit[tt * 128:(tt + 1) * 128, :], c0t[b][:]),
                reads=(f"c0t{b}",), writes=(f"{P}cinit{tt}",), dma=True)
            if SUB < 2:
                continue
            p.step("scalar", lambda h: h.activation(
                sq2[:], ht[:], AF.Square, accum_out=smallc2[:, 0:1]),
                reads=("ht0", "ht1"), writes=("s2m0", "sq2"))
            p.step("vector", lambda h: h.tensor_scalar(
                smallc2[:, 1:2], smallc2[:, 0:1], 1.0 / C, None, ALU.mult),
                reads=("s2m0",), writes=("s2m1",))
            p.step("scalar", lambda h: h.activation(
                smallc2[:, 2:3], smallc2[:, 1:2], AF.Sqrt, bias=self.epscol6[:]),
                reads=("s2m1", "epscol6"), writes=("s2m2",))
            p.step("vector", lambda h: h.reciprocal(smallc2[:, 3:4], smallc2[:, 2:3]),
                reads=("s2m2",), writes=("s2m3",))
            p.step("vector", lambda h: h.scalar_tensor_tensor(
                bt[:], ht[:], smallc2[:, 3:4], self.normw2[:], ALU.mult, ALU.mult),
                reads=("ht0", "ht1", "s2m3", "normw2"), writes=("bt",))
            p.step("scalar", lambda h: h.activation(
                bthi[:], bt[:], AF.Copy),
                reads=("bt",), writes=("bthi",))
            p.step("vector", lambda h: h.tensor_tensor(
                btlo[:], bt[:], bthi[:], ALU.subtract),
                reads=("bt", "bthi"), writes=("btlo",))
            for kt in range(8):
                tb = kt % 2
                p.step("tensor", lambda h, kt=kt, tb=tb: h.transpose(
                    PT[:, tb * 1024: tb * 1024 + 128], bthi[:, kt * 128:(kt + 1) * 128],
                    self.c_ident[:]),
                    reads=("bthi", "c_ident"), writes=(Tk[tb],), tag="btT")
                p.step("scalar", lambda h, kt=kt, tb=tb, b=b: h.copy(
                    btb[b][:, kt * 128:(kt + 1) * 128], PT[:, tb * 1024: tb * 1024 + 128]),
                    reads=(Tk[tb],), writes=(f"btb{b}_{kt}",))
                p.step("tensor", lambda h, kt=kt, tb=tb: h.transpose(
                    PT[:, tb * 1024 + 128: tb * 1024 + 256],
                    btlo[:, kt * 128:(kt + 1) * 128], self.c_ident[:]),
                    reads=("btlo", "c_ident"), writes=(Tk[tb],), tag="bloT")
                p.step("vector", lambda h, kt=kt, tb=tb: h.tensor_copy(
                    blosb[:, kt * 128:(kt + 1) * 128],
                    PT[:, tb * 1024 + 128: tb * 1024 + 256]),
                    reads=(Tk[tb],), writes=(f"blo{kt}",))
            p.step("sync", lambda h, b=b, tt=tt: h.dma_start(
                _t3(self.bT_hbm[:, tt * 128:(tt + 1) * 128]), _s3(btb[b][:], 128)),
                reads=tuple(f"btb{b}_{kt}" for kt in range(8)),
                writes=(f"{P}bT{tt}",), dma=True)
            if SUB < 3:
                continue
            p.step("tensor", lambda h, b=b: self._mmgroup(
                h, PA[:, 0:8],
                [(btb[b][:, kt * 128:(kt + 1) * 128], self.routwh[:, kt * 8:(kt + 1) * 8])
                 for kt in range(8)] +
                [(btb[b][:, kt * 128:(kt + 1) * 128], self.routwl[:, kt * 8:(kt + 1) * 8])
                 for kt in range(8)] +
                [(blosb[:, kt * 128:(kt + 1) * 128], self.routwh[:, kt * 8:(kt + 1) * 8])
                 for kt in range(8)]),
                reads=tuple(f"btb{b}_{kt}" for kt in range(8)) +
                      tuple(f"blo{kt}" for kt in range(8)) + ("routwh", "routwl"),
                writes=(A[0],), tag="router")
            p.step("vector", lambda h: h.tensor_tensor(
                lgs[:], PA[:, 0:8], self.routbbc[:, 0:8], ALU.add),
                reads=(A[0], "routbbc"), writes=("lgs",))
            if (P + "lgall") in self.taps:
                p.step("vector", lambda h, tt=tt: h.tensor_copy(
                    lgall[:, tt * 8:(tt + 1) * 8], lgs[:]),
                    reads=("lgs",), writes=(f"lgall{tt}",))
            self._top2(tt, lgs[:], "lgs", smallc2, tp8, P)
            if (P + "dbg2") in self.taps:
                p.step("vector", lambda h, tt=tt: h.tensor_copy(
                    dbg2[:, tt * 4: tt * 4 + 1], smallc2[:, 4:5]),
                    reads=("t2w1",), writes=(f"dbg2a{tt}",))
                p.step("vector", lambda h, tt=tt: h.tensor_copy(
                    dbg2[:, tt * 4 + 1: tt * 4 + 2], smallc2[:, 5:6]),
                    reads=("t2i2",), writes=(f"dbg2b{tt}",))
                p.step("vector", lambda h, tt=tt: h.tensor_copy(
                    dbg2[:, tt * 4 + 2: tt * 4 + 3], smallc2[:, 6:7]),
                    reads=("t2w2",), writes=(f"dbg2c{tt}",))
                p.step("vector", lambda h, tt=tt: h.tensor_copy(
                    dbg2[:, tt * 4 + 3: tt * 4 + 4], smallc2[:, 7:8]),
                    reads=("t2m2b",), writes=(f"dbg2d{tt}",))
        if (P + "lgall") in self.taps:
            self.tap(P + "lgall", lgall[:], [f"lgall{tt}" for tt in range(NT)])
        if (P + "dbg2") in self.taps:
            self.tap(P + "dbg2", dbg2[:], [f"dbg2a{tt}" for tt in range(NT)] +
                     [f"dbg2d{tt}" for tt in range(NT)])
        p.barrier()
        ph.close()
        if (P + "cinit") in self.taps:
            self.tap_dram(P + "cinit", self.cinit,
                          [f"{P}cinit{tt}" for tt in range(NT)])
        if (P + "bT") in self.taps:
            self.tap_dram(P + "bT", self.bT_hbm, [f"{P}bT{tt}" for tt in range(NT)])
        if (P + "comb") in self.taps:
            self.tap(P + "comb", self.comb[:], [f"{P}comb{tt}" for tt in range(NT)])

    def _top2(self, tt, lg_psum, lg_key, smallc2, tp8, P):
        p = self.p
        iota = self.c_iota8
        lg2 = self.lg2scr
        p.step("vector", lambda h: h.tensor_reduce(
            smallc2[:, 4:5], lg_psum, AX.X, ALU.max),
            reads=(lg_key,), writes=("t2m1",))
        p.step("vector", lambda h: h.tensor_scalar(
            tp8[:], lg_psum, smallc2[:, 4:5], None, ALU.is_ge),
            reads=(lg_key, "t2m1"), writes=("t2a",))
        p.step("vector", lambda h: h.tensor_scalar(
            tp8[:], tp8[:], -99.0, 99.0, ALU.mult, ALU.add),
            reads=("t2a",), writes=("t2a",))
        p.step("vector", lambda h: h.tensor_tensor(
            tp8[:], tp8[:], iota[:], ALU.add),
            reads=("t2a", "c_iota8"), writes=("t2a",))
        p.step("vector", lambda h: h.tensor_reduce(
            smallc2[:, 5:6], tp8[:], AX.X, ALU.min),
            reads=("t2a",), writes=("t2i1",))
        p.step("vector", lambda h: h.tensor_scalar(
            tp8[:], iota[:], smallc2[:, 5:6], None, ALU.is_equal),
            reads=("c_iota8", "t2i1"), writes=("t2mask1",))
        p.step("vector", lambda h: h.scalar_tensor_tensor(
            lg2[:], tp8[:], -1e30, lg_psum, ALU.mult, ALU.add),
            reads=("t2mask1", lg_key), writes=("t2lg2",))
        p.step("vector", lambda h: h.tensor_reduce(
            smallc2[:, 6:7], lg2[:], AX.X, ALU.max),
            reads=("t2lg2",), writes=("t2m2",))
        p.step("vector", lambda h: h.tensor_tensor(
            smallc2[:, 7:8], smallc2[:, 4:5], smallc2[:, 6:7], ALU.subtract),
            reads=("t2m1", "t2m2"), writes=("t2d",))
        p.step("scalar", lambda h: h.activation(
            smallc2[:, 4:5], smallc2[:, 7:8], AF.Sigmoid),
            reads=("t2d",), writes=("t2w1",))
        p.step("scalar", lambda h: h.activation(
            smallc2[:, 6:7], smallc2[:, 7:8], AF.Sigmoid, scale=-1.0),
            reads=("t2d",), writes=("t2w2",))
        p.step("vector", lambda h, tt=tt: h.tensor_scalar(
            self.comb[:, tt * 8:(tt + 1) * 8], tp8[:], smallc2[:, 4:5], 0.5,
            ALU.mult, ALU.mult),
            reads=("t2mask1", "t2w1"), writes=(f"{P}combA{tt}",))
        p.step("vector", lambda h: h.tensor_reduce(
            smallc2[:, 7:8], lg2[:], AX.X, ALU.max),
            reads=("t2lg2",), writes=("t2m2b",))
        p.step("vector", lambda h: h.tensor_scalar(
            tp8[:], lg2[:], smallc2[:, 7:8], None, ALU.is_ge),
            reads=("t2lg2", "t2m2b"), writes=("t2b",))
        p.step("vector", lambda h: h.tensor_scalar(
            tp8[:], tp8[:], -99.0, 99.0, ALU.mult, ALU.add),
            reads=("t2b",), writes=("t2b",))
        p.step("vector", lambda h: h.tensor_tensor(
            tp8[:], tp8[:], iota[:], ALU.add),
            reads=("t2b", "c_iota8"), writes=("t2b",))
        p.step("vector", lambda h: h.tensor_reduce(
            smallc2[:, 5:6], tp8[:], AX.X, ALU.min),
            reads=("t2b",), writes=("t2i2",))
        p.step("vector", lambda h: h.tensor_scalar(
            tp8[:], iota[:], smallc2[:, 5:6], None, ALU.is_equal),
            reads=("c_iota8", "t2i2"), writes=("t2mask2",))
        p.step("vector", lambda h, tt=tt: h.tensor_scalar(
            lg2[:], tp8[:], smallc2[:, 6:7], 0.5, ALU.mult, ALU.mult),
            reads=("t2mask2", "t2w2"), writes=("t2part2",))
        p.step("vector", lambda h, tt=tt: h.tensor_tensor(
            self.comb[:, tt * 8:(tt + 1) * 8],
            self.comb[:, tt * 8:(tt + 1) * 8], lg2[:], ALU.add),
            reads=(f"{P}combA{tt}", "t2part2"), writes=(f"{P}comb{tt}",))

    def _ffn_f3(self, out_hbm, pref):
        p = self.p
        P = pref
        st = ExitStack()
        hhT = self.sb("hhT", [128, 32 * TBLK], BF16, stack=st)
        cblk = self.sb("cblk", [128, 8 * C], stack=st)
        bTb = self.sb("bTb", [128, 8 * TBLK], BF16, stack=st)
        w12 = [self.sb(f"w12_{i}", [128, 2 * 8 * 512], BF16, stack=st) for i in range(2)]
        w3s = [self.sb(f"w3s_{i}", [128, 4 * C], BF16, stack=st) for i in range(2)]
        s1b = [self.sb(f"s1b_{i}", [128, 512], BF16, stack=st) for i in range(2)]
        PA = self.ps("PA_f3", [128, 2048], stack=st)
        PB = self.ps("PB_f3", [128, 2048], stack=st)
        A = [f"{P}3PAb{i}" for i in range(4)]
        Bk = [f"{P}3PBb{i}" for i in range(4)]

        w1b, w2b, w3b = self.wb["f_w1"], self.wb["f_w2"], self.wb["f_w3"]

        for nb in range(NBLK):
            p.step("sync", lambda h, nb=nb: h.dma_start(
                _s3(cblk[:], C), _t3(self.cinit[nb * TBLK:(nb + 1) * TBLK, :])),
                reads=tuple(f"{P}cinit{nb * 8 + i}" for i in range(8)),
                writes=(f"cblk",), dma=True)
            p.step("sync", lambda h, nb=nb: h.dma_start(
                _s3(bTb[:], TBLK), _t3(self.bT_hbm[:, nb * TBLK:(nb + 1) * TBLK])),
                reads=tuple(f"{P}bT{nb * 8 + i}" for i in range(8)),
                writes=("bTb",), dma=True)
            for e in range(E):
                for fc in range(8):
                    sl = (e * 8 + fc) % 2
                    ksl = f"w12_{sl}"
                    p.step("sync", lambda h, e=e, fc=fc, sl=sl: h.dma_start(
                        _s3(w12[sl][:, 0:4096], 512),
                        _t3(w1b[e][:, fc * 512:(fc + 1) * 512])),
                        reads=("f_w1_b_done",), writes=(ksl,), dma=True)
                    p.step("sync", lambda h, e=e, fc=fc, sl=sl: h.dma_start(
                        _s3(w12[sl][:, 4096:8192], 512),
                        _t3(w2b[e][:, fc * 512:(fc + 1) * 512])),
                        reads=("f_w2_b_done",), writes=(ksl + "x",), dma=True)
                    for ft in range(4):
                        fg = fc * 4 + ft
                        b1c = self.b1sb[:, e * 32 + fg: e * 32 + fg + 1]
                        b2c = self.b2sb[:, e * 32 + fg: e * 32 + fg + 1]
                        for th in range(2):
                            r = (ft * 2 + th) % 2
                            kh1, kh2 = A[r], A[2 + r]
                            sb1 = s1b[r]
                            p.step("tensor", lambda h, sl=sl, ft=ft, th=th, r=r: self._mmgroup(
                                h, PA[:, r * 512: r * 512 + 512],
                                [(w12[sl][:, kt * 512 + ft * 128: kt * 512 + ft * 128 + 128],
                                  bTb[:, kt * TBLK + th * 512: kt * TBLK + th * 512 + 512])
                                 for kt in range(8)]),
                                reads=(ksl, "bTb"), writes=(kh1,), tag="h1mm")
                            p.step("tensor", lambda h, sl=sl, ft=ft, th=th, r=r: self._mmgroup(
                                h, PA[:, 1024 + r * 512: 1024 + r * 512 + 512],
                                [(w12[sl][:, 4096 + kt * 512 + ft * 128:
                                          4096 + kt * 512 + ft * 128 + 128],
                                  bTb[:, kt * TBLK + th * 512: kt * TBLK + th * 512 + 512])
                                 for kt in range(8)]),
                                reads=(ksl + "x", "bTb"), writes=(kh2,), tag="h2mm")
                            p.step("scalar", lambda h, r=r, b1c=b1c: h.activation(
                                s1b[r][:], PA[:, r * 512: r * 512 + 512],
                                AF.Silu, bias=b1c),
                                reads=(kh1, "b1sb"), writes=(f"s1b_{r}",))
                            p.step("vector", lambda h, r=r, b2c=b2c, fg=fg, th=th: h.scalar_tensor_tensor(
                                hhT[:, fg * TBLK + th * 512: fg * TBLK + th * 512 + 512],
                                PA[:, 1024 + r * 512: 1024 + r * 512 + 512],
                                b2c, s1b[r][:], ALU.add, ALU.mult),
                                reads=(kh2, "b2sb", f"s1b_{r}"),
                                writes=(f"hhT{fg}t{th}",))
                hkeys = tuple(f"hhT{fg}t{th}" for fg in range(32) for th in range(2))
                for sp in range(2):
                    for fc in range(8):
                        sl = fc % 2
                        kw3 = f"w3s_{sl}"
                        p.step("sync", lambda h, e=e, fc=fc, sl=sl: h.dma_start(
                            _s3(w3s[sl][:], C),
                            _t3(w3b[e][fc * 512:(fc + 1) * 512, :])),
                            reads=("f_w3_b_done",), writes=(kw3,), dma=True)
                        for tq in range(4):
                            tt = sp * 4 + tq
                            pb = [PA, PA, PB, PB][tq]
                            off = [0, 1024, 0, 1024][tq]
                            bk = [A, A, Bk, Bk][tq]
                            bko = [0, 2, 0, 2][tq]
                            for n0 in range(2):
                                p.step("tensor", lambda h, sl=sl, fc=fc, tt=tt, n0=n0,
                                       pb=pb, off=off: self._mmgroup(
                                    h, pb[:, off + n0 * 512: off + n0 * 512 + 512],
                                    [(hhT[:, (fc * 4 + fi) * TBLK + tt * 128:
                                           (fc * 4 + fi) * TBLK + tt * 128 + 128],
                                      w3s[sl][:, fi * C + n0 * 512: fi * C + n0 * 512 + 512])
                                     for fi in range(4)],
                                    start=(fc == 0), stop=False),
                                    reads=(kw3,) + hkeys,
                                    writes=(bk[bko + n0],), tag="w3mm")
                    for tq in range(4):
                        tt = sp * 4 + tq
                        pb = [PA, PA, PB, PB][tq]
                        off = [0, 1024, 0, 1024][tq]
                        bk = [A, A, Bk, Bk][tq]
                        bko = [0, 2, 0, 2][tq]
                        for n0 in range(2):
                            p.step("tensor", lambda h, e=e, n0=n0, pb=pb, off=off: h.matmul(
                                pb[:, off + n0 * 512: off + n0 * 512 + 512],
                                self.c_ones128b[:],
                                self.b3cat[:, e * C + n0 * 512: e * C + n0 * 512 + 512],
                                start=False, stop=True),
                                reads=("c_ones128b", "b3cat", bk[bko + n0]),
                                writes=(bk[bko + n0],), tag="b3mm")
                        combc = self.comb[:, (nb * 8 + tt) * 8 + e: (nb * 8 + tt) * 8 + e + 1]
                        p.step("vector", lambda h, tt=tt, e=e, pb=pb, off=off,
                               combc=combc: h.scalar_tensor_tensor(
                            cblk[:, tt * C: (tt + 1) * C], pb[:, off: off + 1024],
                            combc, cblk[:, tt * C: (tt + 1) * C], ALU.mult, ALU.add),
                            reads=(bk[bko], bk[bko + 1], f"{P}comb{nb * 8 + tt}", "cblk"),
                            writes=("cblk",), tag="cacc")
            p.step("sync", lambda h, nb=nb: h.dma_start(
                _t3(out_hbm[nb * TBLK:(nb + 1) * TBLK, :]), _s3(cblk[:], C)),
                reads=("cblk",), writes=(f"{P}out{nb}",), dma=True)
        if (P + "out") in self.taps:
            self.tap_dram(P + "out", out_hbm, [f"{P}out{nb}" for nb in range(NBLK)])
        st.close()

    def _sparse_slot(self, in_hbm, out_hbm, pref):
        self._sparse_s1(in_hbm, pref)
        self.p.barrier()
        self._sparse_s2(pref)
        self.p.barrier()
        self._sparse_s3(in_hbm, out_hbm, pref)
        self.p.barrier()

    def _sparse_s1(self, in_hbm, P):
        p = self.p
        st = ExitStack()
        cawb = self.sb("cawb", [128, 8 * 3 * C], BF16, stack=st)
        qwb = self.sb("s_qwb", [128, 8 * C], BF16, stack=st)
        kwb = self.sb("s_kwb", [128, 8 * C], BF16, stack=st)
        vwb = self.sb("s_vwb", [128, 8 * C], BF16, stack=st)
        xt = [self.sb(f"s_xt{i}", [128, C], stack=st) for i in range(2)]
        gt = [self.sb(f"s_gt{i}", [128, C], BF16, stack=st) for i in range(2)]
        sq = self.sb("s_sq", [128, C], stack=st)
        smallc = self.sb("s_smallc", [128, 8], stack=st)
        gT = self.sb("s_gT", [128, 8 * 512], BF16, stack=st)
        qkvT = self.sb("s_qkvT", [128, 24 * 512], BF16, stack=st)
        ostg = [self.sb(f"s_ostg{i}", [128, 512], BF16, stack=st) for i in range(2)]
        PA = self.ps("PA_s1", [128, 2048], stack=st)
        PT = self.ps("PT_s1", [128, 2048], BF16, stack=st)
        A = [f"{P}1PAb{i}" for i in range(4)]
        Tk = [f"{P}1PTb{i}" for i in range(2)]

        p.step("sync", lambda h: h.dma_start(
            _s3(cawb[:], {"f_wq":1024,"f_wk":256,"f_wv":256,"f_wo":1024,"ca_w":3072,"q_w":1024,"k_w":1024,"v_w":1024,"o_w":1024}["ca_w"]), _t3(self.wb["ca_w"])),
            reads=("ca_w_b_done",), writes=("cawb",), dma=True)
        p.step("sync", lambda h: h.dma_start(
            _s3(qwb[:], {"f_wq":1024,"f_wk":256,"f_wv":256,"f_wo":1024,"ca_w":3072,"q_w":1024,"k_w":1024,"v_w":1024,"o_w":1024}["q_w"]), _t3(self.wb["q_w"])),
            reads=("q_w_b_done",), writes=("s_qwb",), dma=True)
        p.step("sync", lambda h: h.dma_start(
            _s3(kwb[:], {"f_wq":1024,"f_wk":256,"f_wv":256,"f_wo":1024,"ca_w":3072,"q_w":1024,"k_w":1024,"v_w":1024,"o_w":1024}["k_w"]), _t3(self.wb["k_w"])),
            reads=("k_w_b_done",), writes=("s_kwb",), dma=True)
        p.step("sync", lambda h: h.dma_start(
            _s3(vwb[:], {"f_wq":1024,"f_wk":256,"f_wv":256,"f_wo":1024,"ca_w":3072,"q_w":1024,"k_w":1024,"v_w":1024,"o_w":1024}["v_w"]), _t3(self.wb["v_w"])),
            reads=("v_w_b_done",), writes=("s_vwb",), dma=True)

        for tc in range(NCH):
            for qq in range(4):
                tt = tc * 4 + qq
                b = tt % 2
                kx, kg = f"s_xt{b}", f"s_gt{b}"
                p.step("sync", lambda h, b=b, tt=tt: h.dma_start(
                    xt[b][:], in_hbm[tt * 128:(tt + 1) * 128, :]),
                    writes=(kx,), dma=True)
                self._ln_stats(xt[b], kx, self.muS[:, tt:tt + 1], f"{P}mu{tt}",
                               self.rstdS[:, tt:tt + 1], f"{P}rstd{tt}",
                               sq, smallc)
                p.step("vector", lambda h, b=b, tt=tt: h.tensor_scalar(
                    gt[b][:], xt[b][:], self.muS[:, tt:tt + 1],
                    self.rstdS[:, tt:tt + 1], ALU.subtract, ALU.mult),
                    reads=(kx, f"{P}mu{tt}", f"{P}rstd{tt}"), writes=(kg,))
                for kt in range(8):
                    tb = kt % 2
                    p.step("tensor", lambda h, b=b, kt=kt, tb=tb: h.transpose(
                        PT[:, tb * 1024: tb * 1024 + 128],
                        gt[b][:, kt * 128:(kt + 1) * 128], self.c_ident[:]),
                        reads=(kg, "c_ident"), writes=(Tk[tb],))
                    p.step("vector", lambda h, kt=kt, qq=qq, tb=tb: h.tensor_copy(
                        gT[:, kt * 512 + qq * 128: kt * 512 + qq * 128 + 128],
                        PT[:, tb * 1024: tb * 1024 + 128]),
                        reads=(Tk[tb],), writes=(f"s_gT{kt}q{qq}",))
            gkeys = tuple(f"s_gT{kt}q{qq}" for kt in range(8) for qq in range(4))
            for mt in range(24):
                bq = mt % 2
                p.step("tensor", lambda h, mt=mt, bq=bq: self._mmgroup(
                    h, PA[:, bq * 512: bq * 512 + 512],
                    [(cawb[:, kt * 3072 + mt * 128: kt * 3072 + mt * 128 + 128],
                      gT[:, kt * 512:(kt + 1) * 512]) for kt in range(8)]),
                    reads=gkeys + ("cawb",), writes=(A[bq],))
                p.step("scalar", lambda h, mt=mt, bq=bq: h.activation(
                    qkvT[:, mt * 512:(mt + 1) * 512], PA[:, bq * 512: bq * 512 + 512],
                    AF.Identity, bias=self.cabT[:, mt:mt + 1]),
                    reads=(A[bq], "cabT"), writes=(f"s_qkvT{mt}",))
            midq = tuple(f"s_qkvT{mt}" for mt in range(8))
            midk = tuple(f"s_qkvT{mt}" for mt in range(8, 16))
            midv = tuple(f"s_qkvT{mt}" for mt in range(16, 24))
            for which, wsb, wkey, bias, bkey, dst, mids, moff in (
                    ("q", qwb, "s_qwb", self.qb8T, "qb8T", self.qT2, midq, 0),
                    ("k", kwb, "s_kwb", self.kbT, "kbT", self.kT2, midk, 8),
                    ("v", vwb, "s_vwb", self.vbT, "vbT", self.vT2, midv, 16)):
                for mt in range(8):
                    bq = 2 + mt % 2
                    ob = mt % 2
                    p.step("tensor", lambda h, mt=mt, bq=bq, wsb=wsb, moff=moff: self._mmgroup(
                        h, PA[:, bq * 512: bq * 512 + 512],
                        [(wsb[:, kt * C + mt * 128: kt * C + mt * 128 + 128],
                          qkvT[:, (moff + kt) * 512:(moff + kt + 1) * 512])
                         for kt in range(8)]),
                        reads=mids + (wkey,), writes=(A[bq],))
                    p.step("scalar", lambda h, mt=mt, bq=bq, ob=ob, bias=bias: h.activation(
                        ostg[ob][:], PA[:, bq * 512: bq * 512 + 512],
                        AF.Identity, bias=bias[:, mt:mt + 1]),
                        reads=(A[bq], bkey), writes=(f"s_ostg{ob}",))
                    p.step("sync", lambda h, mt=mt, ob=ob, dst=dst, tc=tc: h.dma_start(
                        dst[mt * 128:(mt + 1) * 128, tc * 512:(tc + 1) * 512],
                        ostg[ob][:]),
                        reads=(f"s_ostg{ob}",),
                        writes=(f"{P}{dst.tensor.name}m{mt}c{tc}",), dma=True)
        self.tap(P + "mid", qkvT[:], [f"s_qkvT{mt}" for mt in range(24)])
        self.tap(P + "gT", gT[:], [f"s_gT{kt}q{qq}" for kt in range(8) for qq in range(4)])
        if (P + "qT2") in self.taps:
            self.tap_dram(P + "qT2", self.qT2,
                          [f"{P}qT2m{mt}c{tc}" for mt in range(8) for tc in range(8)])
        if (P + "kT2") in self.taps:
            self.tap_dram(P + "kT2", self.kT2,
                          [f"{P}kT2m{mt}c{tc}" for mt in range(8) for tc in range(8)])
        if (P + "vT2") in self.taps:
            self.tap_dram(P + "vT2", self.vT2,
                          [f"{P}vT2m{mt}c{tc}" for mt in range(8) for tc in range(8)])
        p.barrier()
        st.close()

    def _sparse_s2(self, P):
        p = self.p
        st = ExitStack()
        qhT = self.sb("qhT", [64, T], BF16, stack=st)
        khT = self.sb("khT", [64, T], BF16, stack=st)
        vhT = self.sb("vhT", [64, T], BF16, stack=st)
        qgex = [self.sb(f"qgex{i}", [65, 512], BF16, stack=st) for i in range(2)]
        kgex = [self.sb(f"kgex{i}", [65, 512], BF16, stack=st) for i in range(2)]
        vgx = [self.sb(f"vgx{i}", [128, 4 * 65], BF16, stack=st) for i in range(2)]
        mseg = self.sb("mseg", [128, 56], stack=st)
        m4s = self.sb("m4s", [56, 128], stack=st)
        mgrow = self.sb("mgrow", [1, T], stack=st)
        negm = self.sb("negm", [1, T], BF16, stack=st)
        oT = [self.sb(f"oT{b}", [64, T], BF16, stack=st) for b in range(3)]
        mtf = self.sb("mtf", [1, T], BF16, stack=st)
        lsef = self.sb("lsef", [1, T], stack=st)
        lnl = self.sb("lnl", [1, 512], stack=st)
        pbuf = [self.sb(f"s_pbuf{i}", [128, 512], BF16, stack=st) for i in range(3)]
        PA = self.ps("PA_s2", [128, 2048], stack=st)
        PT = self.ps("PT_s2", [128, 2048], BF16, stack=st)
        A = [f"{P}2PAb{i}" for i in range(4)]
        Tk = [f"{P}2PTb{i}" for i in range(2)]
        mfl = self.dram(P + "mfl", [56, 128])
        self.yT2 = self.nc.dram_tensor(P + "yT2", [C, T], BF16, kind="Internal").ap()
        oTh = [self.nc.dram_tensor(f"{P}oTh{b}", [C, T], BF16, kind="Internal").ap()
               for b in range(3)]
        LSEh = [self.dram(f"{P}LSEh{b}", [H, T]) for b in range(3)]
        MTh = [self.nc.dram_tensor(f"{P}MTh{b}", [H, T], BF16, kind="Internal").ap()
               for b in range(3)]
        Uh = [self.dram(f"{P}Uh{b}", [H, T]) for b in range(3)]

        qkeys = tuple(f"{P}qT2m{mt}c{tc}" for mt in range(8) for tc in range(8))
        kkeys = tuple(f"{P}kT2m{mt}c{tc}" for mt in range(8) for tc in range(8))
        vkeys = tuple(f"{P}vT2m{mt}c{tc}" for mt in range(8) for tc in range(8))

        for i in range(2):
            p.step("vector", lambda h, i=i: h.memset(vgx[i][:], 1.0),
                   writes=(f"vgx_ones{i}",))
            p.step("vector", lambda h, i=i: h.memset(kgex[i][64:65, :], 1.0),
                   writes=(f"kgex1_{i}",))

        BR_ROWS = ((0, 32), (32, 48), (48, 56))
        for hh in range(H):
            segl = []
            for bi, (wseg, r) in enumerate(SEGS):
                off = hh % r
                for sg in range(T // wseg):
                    segl.append((bi, sg, sg * wseg + off, r))
            p.step("sync", lambda h, hh=hh: h.dma_start(
                qhT[:], self.qT2[hh * 64:(hh + 1) * 64, :]),
                reads=qkeys, writes=("qhT",), dma=True, pclass="p64")
            p.step("sync", lambda h, hh=hh: h.dma_start(
                khT[:], self.kT2[hh * 64:(hh + 1) * 64, :]),
                reads=kkeys, writes=("khT",), dma=True, pclass="p64")
            p.step("sync", lambda h, hh=hh: h.dma_start(
                vhT[:], self.vT2[hh * 64:(hh + 1) * 64, :]),
                reads=vkeys, writes=("vhT",), dma=True, pclass="p64")
            mc = 0
            for bi, sg, base, r in segl:
                for ic in range(4):
                    bq = ic % 2
                    p.step("tensor", lambda h, base=base, r=r, ic=ic, bq=bq: h.matmul(
                        PA[:, bq * 512: bq * 512 + 512],
                        qhT[:, base + ic * 128 * r: base + (ic * 128 + 127) * r + 1: r],
                        khT[:, base: base + 511 * r + 1: r], start=True, stop=True),
                        reads=("qhT", "khT"), writes=(A[bq],), tag="sp1mm")
                    dlt = 128 * ic
                    p.step("vector", lambda h, bq=bq, dlt=dlt: h.tensor_tensor(
                        PA[:, bq * 512: bq * 512 + 512],
                        PA[:, bq * 512: bq * 512 + 512],
                        self.c_stair1[:, 384 - dlt: 896 - dlt], ALU.add),
                        reads=(A[bq], "c_stair1"), writes=(A[bq],))
                    p.step("vector", lambda h, bq=bq, mc=mc, ic=ic: h.tensor_reduce(
                        mseg[:, mc + ic: mc + ic + 1],
                        PA[:, bq * 512: bq * 512 + 512], AX.X, ALU.max),
                        reads=(A[bq],), writes=(f"msg{mc + ic}",))
                mc += 4
            mall = tuple(f"msg{i}" for i in range(56))
            p.step("tensor", lambda h: h.transpose(
                PA[0:56, 1024:1152], mseg[:], self.c_identf[:]),
                reads=mall + ("c_identf",), writes=(A[2],), tag="msT")
            p.step("vector", lambda h: h.tensor_copy(m4s[:], PA[0:56, 1024:1152]),
                   reads=(A[2],), writes=("m4s",))
            p.step("sync", lambda h: h.dma_start(mfl, m4s[:]),
                   reads=("m4s",), writes=("mfl",), dma=True, pclass="misc")
            for bi, (wseg, r) in enumerate(SEGS):
                off = hh % r
                nseg = T // wseg
                blen = nseg * 512
                r0, r1 = BR_ROWS[bi]
                p.step("sync", lambda h, r0=r0, r1=r1, blen=blen: h.dma_start(
                    mgrow[:, 0:blen],
                    mfl[r0:r1].rearrange("p n -> () (p n)")),
                    reads=("mfl",), writes=("mgrow",), dma=True, pclass="p1")
                p.step("scalar", lambda h, blen=blen: h.activation(
                    negm[:, 0:blen], mgrow[:, 0:blen], AF.Copy, scale=-1.0),
                    reads=("mgrow",), writes=("negm",))
                p.step("vector", lambda h: h.memset(mtf[:], 1e30), writes=("mtf",))
                p.step("vector", lambda h: h.memset(lsef[:], -1e30), writes=("lsef",))
                if bi > 0:
                    p.step("vector", lambda h, bi=bi: h.memset(oT[bi][:], 0.0),
                           writes=(f"oT{bi}",))
                pend = [None]

                def make_sfinals(sg, base, gbase, bi, r):
                    sb2 = sg % 2
                    po = 2 + sb2
                    poo = 1024 + sb2 * 512

                    def fin():
                        p.step("scalar", lambda h, poo=poo: h.activation(
                            lnl[:], PA[64:65, poo: poo + 512], AF.Ln),
                            reads=(A[po],), writes=("lnl",))
                        p.step("vector", lambda h, base=base, r=r, gbase=gbase: h.tensor_tensor(
                            lsef[:, base: base + 511 * r + 1: r], lnl[:],
                            negm[:, gbase: gbase + 512], ALU.subtract),
                            reads=("lnl", "negm", "lsef"), writes=("lsef",))
                        p.step("vector", lambda h, base=base, r=r, gbase=gbase: h.tensor_copy(
                            mtf[:, base: base + 511 * r + 1: r], negm[:, gbase: gbase + 512]),
                            reads=("negm", "mtf"), writes=("mtf",))
                        p.step("vector", lambda h, base=base, r=r, bi=bi,
                               poo=poo: h.tensor_copy(
                            oT[bi][0:64, base: base + 511 * r + 1: r],
                            PA[0:64, poo: poo + 512]),
                            reads=(A[po], f"oT{bi}"), writes=(f"oT{bi}",))
                    return fin

                for sg in range(nseg):
                    base = sg * wseg + off
                    gbase = sg * 512
                    sb2 = sg % 2
                    po = 2 + sb2
                    poo = 1024 + sb2 * 512
                    kk, kk1 = f"kgex{sb2}", f"kgex1_{sb2}"
                    kv, kvo = f"vgx{sb2}", f"vgx_ones{sb2}"
                    p.step("vector", lambda h, base=base, r=r, sb2=sb2: h.tensor_copy(
                        kgex[sb2][0:64, :], khT[:, base: base + 511 * r + 1: r]),
                        reads=("khT",), writes=(kk,))
                    p.step("vector", lambda h, base=base, r=r, sb2=sb2: h.tensor_copy(
                        qgex[sb2][0:64, :], qhT[:, base: base + 511 * r + 1: r]),
                        reads=("qhT",), writes=(f"qgex{sb2}",))
                    p.step("vector", lambda h, gbase=gbase, sb2=sb2: h.tensor_copy(
                        qgex[sb2][64:65, :], negm[:, gbase: gbase + 512]),
                        reads=("negm",), writes=(f"qgex1x{sb2}",))
                    for jc in range(4):
                        tb = jc % 2
                        p.step("tensor", lambda h, base=base, r=r, jc=jc, tb=tb: h.transpose(
                            PT[:, tb * 1024: tb * 1024 + 64],
                            vhT[:, base + jc * 128 * r: base + (jc * 128 + 127) * r + 1: r],
                            self.c_ident[0:64, 0:64]),
                            reads=("vhT", "c_ident"), writes=(Tk[tb],), tag="vgT")
                        p.step("vector", lambda h, jc=jc, tb=tb, sb2=sb2: h.tensor_copy(
                            vgx[sb2][:, jc * 65: jc * 65 + 64],
                            PT[:, tb * 1024: tb * 1024 + 64]),
                            reads=(Tk[tb], kvo), writes=(kv,))
                    for jc in range(4):
                        bq = jc % 2
                        pb3 = jc % 3
                        p.step("tensor", lambda h, jc=jc, bq=bq, sb2=sb2: h.matmul(
                            PA[:, bq * 512: bq * 512 + 512],
                            kgex[sb2][:, jc * 128:(jc + 1) * 128], qgex[sb2][:],
                            start=True, stop=True),
                            reads=(kk, kk1, f"qgex{sb2}", f"qgex1x{sb2}"),
                            writes=(A[bq],), tag="sp2mm")
                        d2 = 128 * jc
                        p.step("vector", lambda h, bq=bq, d2=d2: h.tensor_tensor(
                            PA[:, bq * 512: bq * 512 + 512],
                            PA[:, bq * 512: bq * 512 + 512],
                            self.c_stairT[:, 384 - d2: 896 - d2], ALU.add),
                            reads=(A[bq], "c_stairT"), writes=(A[bq],))
                        p.step("scalar", lambda h, bq=bq, pb3=pb3: h.activation(
                            pbuf[pb3][:], PA[:, bq * 512: bq * 512 + 512], AF.Exp),
                            reads=(A[bq],), writes=(f"s_pbuf{pb3}",))
                        if jc == 1 and pend[0] is not None:
                            pend[0]()
                            pend[0] = None
                        if jc >= 2:
                            jp = jc - 2
                            p.step("tensor", lambda h, jp=jp, sb2=sb2, poo=poo,
                                   po=po: h.matmul(
                                PA[0:65, poo: poo + 512],
                                vgx[sb2][:, jp * 65:(jp + 1) * 65],
                                pbuf[jp % 3][:], start=(jp == 0), stop=False),
                                reads=(f"s_pbuf{jp % 3}", kv), writes=(A[po],),
                                tag="spv")
                    for jp in (2, 3):
                        p.step("tensor", lambda h, jp=jp, sb2=sb2, poo=poo,
                               po=po: h.matmul(
                            PA[0:65, poo: poo + 512],
                            vgx[sb2][:, jp * 65:(jp + 1) * 65],
                            pbuf[jp % 3][:], start=False, stop=(jp == 3)),
                            reads=(f"s_pbuf{jp % 3}", kv), writes=(A[po],),
                            tag="spv")
                    pend[0] = make_sfinals(sg, base, gbase, bi, r)
                if pend[0] is not None:
                    pend[0]()
                    pend[0] = None
                p.step("sync", lambda h, bi=bi, hh=hh: h.dma_start(
                    LSEh[bi][hh:hh + 1, :], lsef[:]),
                    reads=("lsef",), writes=(f"{P}LSE{bi}h{hh}",), dma=True, pclass="p1")
                p.step("sync", lambda h, bi=bi, hh=hh: h.dma_start(
                    MTh[bi][hh:hh + 1, :], mtf[:]),
                    reads=("mtf",), writes=(f"{P}MT{bi}h{hh}",), dma=True, pclass="p1")
                p.step("sync", lambda h, bi=bi, hh=hh: h.dma_start(
                    oTh[bi][hh * 64:(hh + 1) * 64, :], oT[bi][:]),
                    reads=(f"oT{bi}",), writes=(f"{P}oTh{bi}h{hh}",), dma=True, pclass="p64")
        p.barrier()
        st.close()

        st = ExitStack()
        CW = 2048
        Lc = [self.sb(f"Lc{b}", [H, CW], stack=st) for b in range(3)]
        Mc = [self.sb(f"Mc{b}", [H, CW], BF16, stack=st) for b in range(3)]
        mg = self.sb("mg_m", [H, CW], stack=st)
        rden = self.sb("rden_m", [H, CW], stack=st)
        escr = self.sb("escr_m", [H, CW], stack=st)
        allL = tuple(f"{P}LSE{b}h{hh}" for b in range(3) for hh in range(H))
        allM = tuple(f"{P}MT{b}h{hh}" for b in range(3) for hh in range(H))
        for ch in range(T // CW):
            c0 = ch * CW
            for b in range(3):
                p.step("sync", lambda h, b=b, c0=c0: h.dma_start(
                    Lc[b][:], LSEh[b][:, c0:c0 + CW]),
                    reads=allL, writes=(f"Lc{b}",), dma=True, pclass="misc")
                p.step("sync", lambda h, b=b, c0=c0: h.dma_start(
                    Mc[b][:], MTh[b][:, c0:c0 + CW]),
                    reads=allM, writes=(f"Mc{b}",), dma=True, pclass="misc")
            p.step("vector", lambda h: h.tensor_tensor(
                mg[:], Lc[0][:], Lc[1][:], ALU.max),
                reads=("Lc0", "Lc1"), writes=("mg_m",))
            p.step("vector", lambda h: h.tensor_tensor(
                mg[:], mg[:], Lc[2][:], ALU.max),
                reads=("mg_m", "Lc2"), writes=("mg_m",))
            for b in range(3):
                p.step("vector", lambda h, b=b: h.tensor_tensor(
                    escr[:], Lc[b][:], mg[:], ALU.subtract),
                    reads=(f"Lc{b}", "mg_m"), writes=("escr_m",))
                p.step("scalar", lambda h, b=b: h.activation(
                    Lc[b][:], escr[:], AF.Exp),
                    reads=("escr_m",), writes=(f"Le{b}",))
            p.step("vector", lambda h: h.tensor_tensor(
                rden[:], Lc[0][:], Lc[1][:], ALU.add),
                reads=("Le0", "Le1"), writes=("rden_m",))
            p.step("vector", lambda h: h.tensor_tensor(
                rden[:], rden[:], Lc[2][:], ALU.add),
                reads=("rden_m", "Le2"), writes=("rden_m",))
            p.step("vector", lambda h: h.reciprocal(rden[:], rden[:]),
                   reads=("rden_m",), writes=("rden_m",))
            for b in range(3):
                p.step("vector", lambda h, b=b: h.tensor_tensor(
                    escr[:], Mc[b][:], mg[:], ALU.add),
                    reads=(f"Mc{b}", "mg_m"), writes=("escr_m",))
                p.step("scalar", lambda h, b=b: h.activation(
                    escr[:], escr[:], AF.Exp, scale=-1.0),
                    reads=("escr_m",), writes=("escr_m",))
                p.step("vector", lambda h, b=b: h.tensor_tensor(
                    Lc[b][:], escr[:], rden[:], ALU.mult),
                    reads=("escr_m", "rden_m"), writes=(f"Uc{b}",))
                p.step("sync", lambda h, b=b, c0=c0: h.dma_start(
                    Uh[b][:, c0:c0 + CW], Lc[b][:]),
                    reads=(f"Uc{b}",), writes=(f"{P}U{b}c{c0}",), dma=True, pclass="misc")
        p.barrier()
        st.close()

        st = ExitStack()
        urow = [self.sb(f"urow{b}", [1, T], stack=st) for b in range(3)]
        oTb = [self.sb(f"oTb{b}", [64, T], BF16, stack=st) for b in range(3)]
        ych = self.sb("ych", [64, 512], stack=st)
        ytmp = self.sb("ytmp", [64, 512], stack=st)
        yhT = self.sb("yhT", [64, T], BF16, stack=st)
        PB = self.ps("PB_s2y", [128, 2048], stack=st)
        Bk = [f"{P}yPBb{i}" for i in range(4)]
        ukeys = tuple(f"{P}U{b}c{c0}" for b in range(3) for c0 in range(0, T, CW))
        for hh in range(H):
            for b in range(3):
                p.step("sync", lambda h, b=b, hh=hh: h.dma_start(
                    urow[b][:], Uh[b][hh:hh + 1, :]),
                    reads=ukeys, writes=(f"urow{b}",), dma=True, pclass="p1")
                p.step("sync", lambda h, b=b, hh=hh: h.dma_start(
                    oTb[b][:], oTh[b][hh * 64:(hh + 1) * 64, :]),
                    reads=(f"{P}oTh{b}h{hh}",), writes=(f"oTb{b}",), dma=True, pclass="p64")
            for ch in range(8):
                c0 = ch * 512
                for b in range(3):
                    p.step("tensor", lambda h, b=b, c0=c0: h.matmul(
                        PB[0:64, b * 512:(b + 1) * 512], self.c_ones128f[:, 0:64],
                        urow[b][:, c0:c0 + 512], start=True, stop=True),
                        reads=(f"urow{b}", "c_ones128f"), writes=(Bk[b],), tag="ubc")
                p.step("vector", lambda h, c0=c0: h.tensor_tensor(
                    ych[:], PB[0:64, 0:512], oTb[0][:, c0:c0 + 512], ALU.mult),
                    reads=(Bk[0], "oTb0"), writes=("ych",))
                p.step("vector", lambda h, c0=c0: h.tensor_tensor(
                    ytmp[:], PB[0:64, 512:1024], oTb[1][:, c0:c0 + 512], ALU.mult),
                    reads=(Bk[1], "oTb1"), writes=("ytmp",))
                p.step("vector", lambda h: h.tensor_tensor(
                    ych[:], ych[:], ytmp[:], ALU.add),
                    reads=("ych", "ytmp"), writes=("ych",))
                p.step("vector", lambda h, c0=c0: h.tensor_tensor(
                    ytmp[:], PB[0:64, 1024:1536], oTb[2][:, c0:c0 + 512], ALU.mult),
                    reads=(Bk[2], "oTb2"), writes=("ytmp",))
                p.step("vector", lambda h, c0=c0: h.tensor_tensor(
                    yhT[:, c0:c0 + 512], ych[:], ytmp[:], ALU.add),
                    reads=("ych", "ytmp"), writes=("yhT",))
            p.step("sync", lambda h, hh=hh: h.dma_start(
                self.yT2[hh * 64:(hh + 1) * 64, :], yhT[:]),
                reads=("yhT",), writes=(f"{P}yT2h{hh}",), dma=True, pclass="p64")
        if (P + "yT2") in self.taps:
            self.tap_dram(P + "yT2", self.yT2,
                          [f"{P}yT2h{hh}" for hh in range(H)])
        p.barrier()
        st.close()

    def _sparse_s3(self, in_hbm, out_hbm, P):
        p = self.p
        st = ExitStack()
        owb = self.sb("s3_owb", [128, 8 * C], BF16, stack=st)
        yTc = self.sb("s3_yTc", [128, 8 * 512], BF16, stack=st)
        xt = [self.sb(f"s3_xt{i}", [128, C], stack=st) for i in range(2)]
        ot = [self.sb(f"s3_ot{i}", [128, C], stack=st) for i in range(2)]
        PA = self.ps("PA_s3", [128, 2048], stack=st)
        A = [f"{P}3PAb{i}" for i in range(4)]
        ykeys = tuple(f"{P}yT2h{hh}" for hh in range(H))
        p.step("sync", lambda h: h.dma_start(
            _s3(owb[:], {"f_wq":1024,"f_wk":256,"f_wv":256,"f_wo":1024,"ca_w":3072,"q_w":1024,"k_w":1024,"v_w":1024,"o_w":1024}["o_w"]), _t3(self.wb["o_w"])),
            reads=("o_w_b_done",), writes=("s3_owb",), dma=True)
        for tc in range(NCH):
            p.step("sync", lambda h, tc=tc: h.dma_start(
                _s3(yTc[:], 512), _t3(self.yT2[:, tc * 512:(tc + 1) * 512])),
                reads=ykeys, writes=("s3_yTc",), dma=True)
            for qq in range(4):
                tt = tc * 4 + qq
                b = tt % 2
                ab = 2 * b
                p.step("sync", lambda h, b=b, tt=tt: h.dma_start(
                    xt[b][:], in_hbm[tt * 128:(tt + 1) * 128, :]),
                    writes=(f"s3_xt{b}",), dma=True)
                for n0 in range(2):
                    p.step("tensor", lambda h, qq=qq, n0=n0, ab=ab: self._mmgroup(
                        h, PA[:, (ab + n0) * 512:(ab + n0 + 1) * 512],
                        [(yTc[:, kt * 512 + qq * 128: kt * 512 + qq * 128 + 128],
                          owb[:, kt * C + n0 * 512: kt * C + n0 * 512 + 512])
                         for kt in range(8)] +
                        [(self.c_ones128b[:],
                          self.obrow[:, n0 * 512:(n0 + 1) * 512])]),
                        reads=("s3_yTc", "s3_owb", "c_ones128b", "obrow"),
                        writes=(A[ab + n0],), tag="s3mm")
                p.step("vector", lambda h, b=b, ab=ab: h.tensor_tensor(
                    ot[b][:, 0:512], PA[:, ab * 512:(ab + 1) * 512],
                    xt[b][:, 0:512], ALU.add),
                    reads=(A[ab], f"s3_xt{b}"), writes=(f"s3_ot{b}a",))
                p.step("vector", lambda h, b=b, ab=ab: h.tensor_tensor(
                    ot[b][:, 512:1024], PA[:, (ab + 1) * 512:(ab + 2) * 512],
                    xt[b][:, 512:1024], ALU.add),
                    reads=(A[ab + 1], f"s3_xt{b}"), writes=(f"s3_ot{b}b",))
                p.step("sync", lambda h, b=b, tt=tt: h.dma_start(
                    out_hbm[tt * 128:(tt + 1) * 128, :], ot[b][:]),
                    reads=(f"s3_ot{b}a", f"s3_ot{b}b"),
                    writes=(f"{P}xout{tt}",), dma=True)
        if (P + "xout") in self.taps:
            self.tap_dram(P + "xout", out_hbm, [f"{P}xout{tt}" for tt in range(NT)])
        st.close()


    def _ln_stats(self, xtile, kx, mu_ap, kmu, rstd_ap, krstd, sq, smallc):
        p = self.p
        p.step("vector", lambda h: h.tensor_reduce(
            smallc[:, 0:1], xtile[:], AX.X, ALU.add),
            reads=(kx,), writes=("sm0",))
        p.step("scalar", lambda h: h.activation(
            sq[:], xtile[:], AF.Square, accum_out=smallc[:, 1:2]),
            reads=(kx,), writes=("sm1", "sqA", "sqB"))
        p.step("vector", lambda h: h.tensor_scalar(
            mu_ap, smallc[:, 0:1], 1.0 / C, None, ALU.mult),
            reads=("sm0",), writes=(kmu,))
        p.step("vector", lambda h: h.tensor_scalar(
            smallc[:, 2:3], smallc[:, 1:2], 1.0 / C, None, ALU.mult),
            reads=("sm1",), writes=("sm2",))
        p.step("vector", lambda h: h.tensor_tensor(
            smallc[:, 3:4], mu_ap, mu_ap, ALU.mult),
            reads=(kmu,), writes=("sm3",))
        p.step("vector", lambda h: h.tensor_tensor(
            smallc[:, 4:5], smallc[:, 2:3], smallc[:, 3:4], ALU.subtract),
            reads=("sm2", "sm3"), writes=("sm4",))
        p.step("scalar", lambda h: h.activation(
            smallc[:, 5:6], smallc[:, 4:5], AF.Sqrt, bias=self.epscol5[:]),
            reads=("sm4", "epscol5"), writes=("sm5",))
        p.step("vector", lambda h: h.reciprocal(rstd_ap, smallc[:, 5:6]),
            reads=("sm5",), writes=(krstd,))



def _in_maps(inputs, consts):
    m = {
        "ca_w": inputs["L0_attn_c_attn_w"], "ca_b": inputs["L0_attn_c_attn_b"],
        "q_w": inputs["L0_attn_q_w"], "q_b": inputs["L0_attn_q_b"],
        "k_w": inputs["L0_attn_k_w"], "k_b": inputs["L0_attn_k_b"],
        "v_w": inputs["L0_attn_v_w"], "v_b": inputs["L0_attn_v_b"],
        "o_w": inputs["L0_attn_o_w"], "o_b": inputs["L0_attn_o_b"],
        "f_n1": inputs["L0_ffn_norm1_w"], "f_wq": inputs["L0_ffn_Wq"],
        "f_wk": inputs["L0_ffn_Wk"], "f_wv": inputs["L0_ffn_Wv"],
        "f_wo": inputs["L0_ffn_Wo"], "f_n2": inputs["L0_ffn_norm2_w"],
        "f_rw": inputs["L0_ffn_router_w"], "f_rb": inputs["L0_ffn_router_b"],
        "f_w1": inputs["L0_ffn_W1"], "f_b1": inputs["L0_ffn_b1"],
        "f_w2": inputs["L0_ffn_W2"], "f_b2": inputs["L0_ffn_b2"],
        "f_w3": inputs["L0_ffn_W3"], "f_b3": inputs["L0_ffn_b3"],
    }
    for k, v in consts.items():
        m["c_" + k] = v
    x = np.ascontiguousarray(inputs["L0_attn_x"], dtype=np.float32)
    maps = []
    for b in range(8):
        d = {k: np.ascontiguousarray(v) for k, v in m.items()}
        d["x"] = np.ascontiguousarray(x[b])
        maps.append(d)
    return maps


def kernel(**inputs):
    consts = host_constants()
    kb = KB()
    nc = kb.build()
    maps = _in_maps(inputs, consts)
    res = run_bass_kernel_spmd(nc, maps, core_ids=list(range(8)))
    return np.stack([res.results[i]["y"] for i in range(8)], axis=0)


# version 04
# speedup vs baseline: 1.0015x; 1.0015x over previous
import numpy as np
import ml_dtypes
from collections import defaultdict
from contextlib import ExitStack

import concourse.bass as bass
import concourse.mybir as mybir
from concourse.bass_utils import run_bass_kernel_spmd

F32 = mybir.dt.float32
BF16 = mybir.dt.bfloat16
AF = mybir.ActivationFunctionType
ALU = mybir.AluOpType
AX = mybir.AxisListType

ENGINES = ("tensor", "vector", "scalar", "gpsimd", "sync")
NDMALANES = 8
DMALANES = tuple(f"dma{i}" for i in range(NDMALANES))
ALLSEMS = ENGINES + DMALANES


def _t3(ap, p=128):
    return ap.rearrange("(kt p) m -> p kt m", p=p)


def _s3(ap, inner):
    return ap.rearrange("p (kt m) -> p kt m", m=inner)


T = 4096
C = 1024
NT = 32
NCH = 8
H = 16
G = 4
D = 64
E = 8
FF = 4096
TBLK = 1024
NBLK = T // TBLK
SEGS = ((512, 1), (1024, 2), (2048, 4))



class _Tok:
    __slots__ = ("eng", "cnt", "clock")

    def __init__(self, eng, cnt, clock):
        self.eng, self.cnt, self.clock = eng, cnt, clock


class Prog:
    def __init__(self, nc):
        self.nc = nc
        self.ops = {e: [] for e in ENGINES}
        self.cnt = {e: 0 for e in ALLSEMS}
        self.clock = {e: {f: 0 for f in ALLSEMS} for e in ENGINES}
        self.writes = {}
        self.readers = defaultdict(list)
        self.sems = None
        self._lane_rr = 0

    def step(self, eng, fn, reads=(), writes=(), dma=False, tag=None,
             pclass="full"):
        chain = None
        if dma:
            if pclass == "full":
                lane = DMALANES[self._lane_rr % 5]
                self._lane_rr += 1
            elif pclass == "p64":
                lane = DMALANES[5]
            elif pclass == "p1":
                lane = DMALANES[6]
            else:
                lane = DMALANES[7]
                chain = self.cnt[lane]
            amt = 16
        else:
            lane = eng
            amt = 1
        my = self.clock[eng]
        need = {}
        toks = []
        for k in reads:
            t = self.writes.get(k)
            if t is not None:
                toks.append(t)
        for k in writes:
            t = self.writes.get(k)
            if t is not None:
                toks.append(t)
            toks.extend(self.readers.get(k, ()))
        for t in toks:
            if my.get(t.eng, 0) < t.cnt and need.get(t.eng, 0) < t.cnt:
                need[t.eng] = t.cnt
        waits = sorted(need.items())
        for e2, c2 in waits:
            my[e2] = c2
        for t in toks:
            if my.get(t.eng, 0) >= t.cnt:
                for e2, c2 in t.clock.items():
                    if my[e2] < c2:
                        my[e2] = c2
        self.cnt[lane] += amt
        mytok = _Tok(lane, self.cnt[lane], dict(my))
        mytok.clock[lane] = self.cnt[lane]
        for k in writes:
            self.writes[k] = mytok
            self.readers[k] = []
        for k in reads:
            self.readers[k].append(mytok)

        def closure(h, _waits=waits, _fn=fn, _amt=amt, _lane=lane, _tag=tag,
                    _chain=chain):
            for e2, c2 in _waits:
                h.wait_ge(self.sems[e2], c2)
            if _chain:
                h.wait_ge(self.sems[_lane], _chain)
            last = _fn(h)
            if last is None:
                raise RuntimeError(f"step returned None ({_lane}, {_tag})")
            last.then_inc(self.sems[_lane], _amt)

        self.ops[eng].append(closure)
        return mytok

    def barrier(self):
        toks = {e: self.cnt[e] for e in ALLSEMS}
        for e in ENGINES:
            my = self.clock[e]
            for e2, c in toks.items():
                if e2 != e and my[e2] < c and c > 0:
                    def w(h, _e2=e2, _c=c):
                        h.wait_ge(self.sems[_e2], _c)
                    self.ops[e].append(w)
                    my[e2] = c

    def final_wait(self):
        for ln in DMALANES:
            c = self.cnt[ln]
            if c > 0:
                def w(h, _c=c, _ln=ln):
                    h.wait_ge(self.sems[_ln], _c)
                self.ops["sync"].append(w)

    def build(self, block, sems):
        self.sems = sems
        for e in ENGINES:
            if not self.ops[e]:
                continue
            lst = self.ops[e]

            def runner(h, _lst=lst):
                for fn in _lst:
                    fn(h)

            getattr(block, e)(runner)



def host_constants():
    HALF = 512
    theta = 1.0 / 10000.0 ** (np.arange(HALF, dtype=np.float64) / HALF)
    ang = np.arange(T, dtype=np.float64)[:, None] * theta[None, :]
    cos = np.cos(ang).astype(np.float32)
    sin = np.sin(ang).astype(np.float32)
    u = np.arange(896)[None, :]
    p = np.arange(128)[:, None]
    stair1 = np.where(u > p + 384, np.float32(-8e9), np.float32(0.0)).astype(np.float32)
    stairT = np.where(u < p + 384, np.float32(-8e9), np.float32(0.0)).astype(np.float32)
    ident = np.eye(128, dtype=np.float32).astype(ml_dtypes.bfloat16)
    identf = np.eye(128, dtype=np.float32)
    iota8 = np.broadcast_to(np.arange(8, dtype=np.float32), (128, 8)).copy()
    iotam99 = (iota8 - 99.0).astype(np.float32)
    ones128f = np.ones((1, 128), np.float32)
    ones128b = np.ones((1, 128), np.float32).astype(ml_dtypes.bfloat16)
    return dict(cos=cos, sin=sin, stair1=stair1, stairT=stairT, ident=ident,
                identf=identf, iota8=iota8, iotam99=iotam99, ones128f=ones128f,
                ones128b=ones128b)



class KB:
    def __init__(self, taps=(), phases=("prologue", "ffn1", "sparse", "ffn2"),
                 stop_after=None):
        self.nc = bass.Bass("TRN2", target_bir_lowering=False)
        self.p = Prog(self.nc)
        self.es = ExitStack()
        self.taps = set(taps)
        self.phases = phases
        self.stop_after = stop_after
        self._names = {}

    def dram(self, name, shape, dt=F32, kind="Internal"):
        return self.nc.dram_tensor(name, shape, dt, kind=kind).ap()

    def _uniq(self, name):
        n = self._names.get(name, 0)
        self._names[name] = n + 1
        return name if n == 0 else f"{name}__{n}"

    def sb(self, name, shape, dt=F32, stack=None):
        return (stack or self.es).enter_context(
            self.nc.sbuf_tensor(self._uniq(name), shape, dt))

    def ps(self, name, shape, dt=F32, stack=None):
        return (stack or self.es).enter_context(
            self.nc.psum_tensor(self._uniq(name), shape, dt))

    def tap(self, name, ap_sbuf, reads):
        if name not in self.taps:
            return
        shape = list(ap_sbuf.shape)
        hbm = self.nc.dram_tensor("tap_" + name, shape, ap_sbuf.tensor.dtype,
                                  kind="ExternalOutput").ap()
        self.p.step("sync", lambda h: h.dma_start(hbm, ap_sbuf),
                    reads=tuple(reads), writes=("tap_" + name,), dma=True, pclass="misc")

    def tap_dram(self, name, dram_ap, reads):
        if name not in self.taps:
            return
        shape = list(dram_ap.shape)
        hbm = self.nc.dram_tensor("tap_" + name, shape, dram_ap.tensor.dtype,
                                  kind="ExternalOutput").ap()
        st = ExitStack()
        rows, cols = shape
        tbuf = self.sb(f"tapb_{name}", [128, cols], dram_ap.tensor.dtype, stack=st)
        for r0 in range(0, rows, 128):
            rr = min(128, rows - r0)
            self.p.step("sync", lambda h, r0=r0, rr=rr: h.dma_start(
                tbuf[0:rr, :], dram_ap[r0:r0 + rr, :]),
                reads=tuple(reads), writes=(f"tapb_{name}",), dma=True, pclass="misc")
            self.p.step("sync", lambda h, r0=r0, rr=rr: h.dma_start(
                hbm[r0:r0 + rr, :], tbuf[0:rr, :]),
                reads=(f"tapb_{name}",), writes=("tap_" + name,), dma=True, pclass="misc")
        st.close()

    @staticmethod
    def _mmgroup(h, out_ap, pairs, start=True, stop=True):
        last = None
        n = len(pairs)
        for i, (l, r) in enumerate(pairs):
            last = h.matmul(out_ap, l, r, start=(start and i == 0),
                            stop=(stop and i == n - 1))
        return last

    def build(self):
        nc = self.nc
        self.x_in = self.dram("x", [T, C], kind="ExternalInput")
        shapes = {"ca_w": [C, 3 * C], "ca_b": [3 * C], "q_w": [C, C], "q_b": [C],
                  "k_w": [C, C], "k_b": [C], "v_w": [C, C], "v_b": [C],
                  "o_w": [C, C], "o_b": [C], "f_n1": [C], "f_wq": [C, C],
                  "f_wk": [C, 256], "f_wv": [C, 256], "f_wo": [C, C],
                  "f_n2": [C], "f_rw": [C, E], "f_rb": [E],
                  "f_w1": [E, C, FF], "f_b1": [E, FF], "f_w2": [E, C, FF],
                  "f_b2": [E, FF], "f_w3": [E, FF, C], "f_b3": [E, C]}
        self.w = {n: self.dram(n, s, kind="ExternalInput") for n, s in shapes.items()}
        cshapes = {"cos": [T, 512], "sin": [T, 512], "stair1": [128, 896],
                   "stairT": [128, 896], "iota8": [128, 8], "iotam99": [128, 8],
                   "ones128f": [1, 128], "identf": [128, 128]}
        self.cst = {n: self.dram("c_" + n, s, kind="ExternalInput")
                    for n, s in cshapes.items()}
        self.cst["ident"] = self.nc.dram_tensor("c_ident", [128, 128], BF16,
                                                kind="ExternalInput").ap()
        self.cst["ones128b"] = self.nc.dram_tensor("c_ones128b", [1, 128], BF16,
                                                   kind="ExternalInput").ap()
        self.y_out = self.nc.dram_tensor("y", [T, C], F32, kind="ExternalOutput").ap()
        self.xs1 = self.dram("xs1", [T, C])
        self.xs2 = self.dram("xs2", [T, C])
        self.cinit = self.dram("cinit", [T, C])
        self.bT_hbm = self.nc.dram_tensor("bT_hbm", [C, T], BF16, kind="Internal").ap()
        self.mflat = self.dram("mflat", [32, 128])
        self.mflatb = self.nc.dram_tensor("mflatb", [32, 128], BF16, kind="Internal").ap()
        self.wb = {}
        for n in ["f_wq", "f_wk", "f_wv", "f_wo", "f_w1", "f_w2", "f_w3",
                  "ca_w", "q_w", "k_w", "v_w", "o_w"]:
            self.wb[n] = self.nc.dram_tensor(n + "_b", shapes[n], BF16,
                                             kind="Internal").ap()
        self.qT2 = self.nc.dram_tensor("qT2", [C, T], BF16, kind="Internal").ap()
        self.kT2 = self.nc.dram_tensor("kT2", [C, T], BF16, kind="Internal").ap()
        self.vT2 = self.nc.dram_tensor("vT2", [C, T], BF16, kind="Internal").ap()

        self.c_stair1 = self.sb("c_stair1s", [128, 896])
        self.c_stairT = self.sb("c_stairTs", [128, 896])
        self.c_ident = self.sb("c_idents", [128, 128], BF16)
        self.c_identf = self.sb("c_identfs", [128, 128])
        self.c_iota8 = self.sb("c_iota8s", [128, 8])
        self.c_iotam99 = self.sb("c_iotam99s", [128, 8])
        self.c_ones128f = self.sb("c_ones128fs", [1, 128])
        self.c_ones128b = self.sb("c_ones128bs", [1, 128], BF16)
        self.epscol5 = self.sb("epscol5", [128, 1])
        self.epscol6 = self.sb("epscol6", [128, 1])
        self.normw1 = self.sb("normw1", [128, C])
        self.normw2 = self.sb("normw2", [128, C])
        self.muS = self.sb("muS", [128, NT])
        self.rstdS = self.sb("rstdS", [128, NT])
        self.comb = self.sb("comb", [128, NT * E])
        self.routw = self.sb("routw", [128, E * 8])
        self.routwh = self.sb("routwh", [128, E * 8], BF16)
        self.routwl = self.sb("routwl", [128, E * 8], BF16)
        self.routbbc = self.sb("routbbc", [128, E])
        self.b1sb = self.sb("b1sb", [128, E * (FF // 128)])
        self.b2sb = self.sb("b2sb", [128, E * (FF // 128)])
        self.b3cat = self.sb("b3cat", [1, E * C], BF16)
        self.obrow = self.sb("obrow", [1, C], BF16)
        self.cabT = self.sb("cabT", [128, 24])
        self.qb8T = self.sb("qb8T", [128, 8])
        self.kbT = self.sb("kbT", [128, 8])
        self.vbT = self.sb("vbT", [128, 8])

        self._load_persistent()
        self.cvt_stack = None
        if "prologue" in self.phases:
            self.cvt_stack = ExitStack()
            self.cvt_stage = [self.sb(f"cvt_st{i}", [128, 2048], stack=self.cvt_stack)
                              for i in range(3)]
            self.cvt_out = [self.sb(f"cvt_out{i}", [128, 2048], BF16, stack=self.cvt_stack)
                            for i in range(3)]
            self._convert_weights("qkv")
        if "ffn1" in self.phases:
            self._ffn_call(self.x_in, self.xs1, "A")
        elif "ffn1f1" in self.phases:
            self.stop_after = "f1"
            self._ffn_f12(self.x_in, "A")
            self.p.barrier()
            self.f12_stack.close()
        elif "ffn1f2c" in self.phases:
            self.stop_after = "f2c"
            self._ffn_f12(self.x_in, "A")
            self.p.barrier()
            self.f12_stack.close()
        elif "ffn1f2d" in self.phases:
            self.stop_after = "f2d"
            self._ffn_f12(self.x_in, "A")
            self.p.barrier()
            self.f12_stack.close()
        elif "ffn1f2" in self.phases:
            self.stop_after = "f2"
            self._ffn_f12(self.x_in, "A")
            self.p.barrier()
            self.f12_stack.close()
        elif "ffn1f12" in self.phases:
            self._ffn_f12(self.x_in, "A")
            self.p.barrier()
            self.f12_stack.close()
        if "sparse" in self.phases:
            self._sparse_slot(self.xs1, self.xs2, "S")
        elif "sparse_s1" in self.phases:
            self._sparse_s1(self.xs1, "S")
            self.p.barrier()
        if "ffn2" in self.phases:
            self._ffn_call(self.xs2, self.y_out, "B")
        if self.cvt_stack is not None:
            self.p.barrier()
            self.cvt_stack.close()
            self.cvt_stack = None
        self.p.final_wait()

        sems = {e: self.es.enter_context(self.nc.semaphore("sem_" + e))
                for e in ALLSEMS}
        block = self.es.enter_context(self.nc.Block())
        self.p.build(block, sems)
        self.es.close()
        return self.nc

    def _load_persistent(self):
        p, w, cst = self.p, self.w, self.cst

        def ld(dst, src, key, reads=(), pc="full"):
            p.step("sync", lambda h: h.dma_start(dst, src), reads=reads,
                   writes=(key,), dma=True, pclass=pc)

        def ldT(dst, src, key, reads=()):
            p.step("sync", lambda h: h.dma_start_transpose(dst, src), reads=reads,
                   writes=(key,), dma=True, pclass="misc")

        ld(self.c_stair1[:], cst["stair1"], "c_stair1")
        ld(self.c_stairT[:], cst["stairT"], "c_stairT")
        ld(self.c_ident[:], cst["ident"], "c_ident")
        ld(self.c_identf[:], cst["identf"], "c_identf")
        ld(self.c_iota8[:], cst["iota8"], "c_iota8")
        ld(self.c_iotam99[:], cst["iotam99"], "c_iotam99")
        ld(self.c_ones128f[:], cst["ones128f"], "c_ones128f", pc="p1")
        ld(self.c_ones128b[:], cst["ones128b"], "c_ones128b", pc="p1")
        p.step("vector", lambda h: h.memset(self.epscol5[:], 1e-5), writes=("epscol5",))
        p.step("vector", lambda h: h.memset(self.epscol6[:], 1e-6), writes=("epscol6",))
        ld(self.normw1[:], w["f_n1"].rearrange("c -> () c").broadcast_to((128, C)), "normw1")
        ld(self.normw2[:], w["f_n2"].rearrange("c -> () c").broadcast_to((128, C)), "normw2")
        ld(_s3(self.routw[:], 8), _t3(w["f_rw"]), "routw")
        ld(self.routbbc[:], w["f_rb"].rearrange("e -> () e").broadcast_to((128, E)), "routbbc")
        p.step("vector", lambda h: h.tensor_copy(self.routwh[:], self.routw[:]),
               reads=("routw",), writes=("routwh",))
        p.step("vector", lambda h: h.tensor_tensor(
            self.routwl[:], self.routw[:], self.routwh[:], ALU.subtract),
            reads=("routw", "routwh"), writes=("routwl",))
        self.tap("routwh", self.routwh[:], ["routwh"])
        self.tap("routwf", self.routw[:], ["routw"])
        ptmp = ExitStack()
        b1nat = self.sb("b1nat", [8, FF], stack=ptmp)
        b2nat = self.sb("b2nat", [8, FF], stack=ptmp)
        ptp = self.ps("ptp_tmp", [128, 512], stack=ptmp)
        ld(b1nat[:], w["f_b1"], "b1nat", pc="misc")
        ld(b2nat[:], w["f_b2"], "b2nat", pc="misc")
        for nat, knat, dst, kdst in ((b1nat, "b1nat", self.b1sb, "b1sb"),
                                     (b2nat, "b2nat", self.b2sb, "b2sb")):
            for ft in range(32):
                tb = ft % 2
                p.step("tensor", lambda h, nat=nat, ft=ft, tb=tb: h.transpose(
                    ptp[:, tb * 128: tb * 128 + 8], nat[:, ft * 128:(ft + 1) * 128],
                    self.c_identf[0:8, 0:8]),
                    reads=(knat, "c_identf"), writes=("ptp",))
                p.step("vector", lambda h, dst=dst, ft=ft, tb=tb: h.tensor_copy(
                    dst[:, ft: 224 + ft + 1: 32], ptp[:, tb * 128: tb * 128 + 8]),
                    reads=("ptp",), writes=(kdst + str(ft),))
        b3f = self.sb("b3f_tmp", [1, E * C], stack=ptmp)
        ld(b3f[:], w["f_b3"].rearrange("e c -> () (e c)"), "b3f_tmp", pc="misc")
        p.step("vector", lambda h: h.tensor_copy(self.b3cat[:], b3f[:]),
               reads=("b3f_tmp",), writes=("b3cat",))
        obf = self.sb("obf_tmp", [1, C], stack=ptmp)
        ld(obf[:], w["o_b"].rearrange("c -> () c"), "obf_tmp", pc="misc")
        p.step("vector", lambda h: h.tensor_copy(self.obrow[:], obf[:]),
               reads=("obf_tmp",), writes=("obrow",))
        cabnat = self.sb("cabnat", [24, 128], stack=ptmp)
        qbnat = self.sb("qbnat", [8, 128], stack=ptmp)
        kbnat = self.sb("kbnat", [8, 128], stack=ptmp)
        vbnat = self.sb("vbnat", [8, 128], stack=ptmp)
        ld(cabnat[:], w["ca_b"].rearrange("(mt p) -> mt p", p=128), "cabnat", pc="misc")
        ld(qbnat[:], w["q_b"].rearrange("(mt p) -> mt p", p=128), "qbnat", pc="misc")
        ld(kbnat[:], w["k_b"].rearrange("(mt p) -> mt p", p=128), "kbnat", pc="misc")
        ld(vbnat[:], w["v_b"].rearrange("(mt p) -> mt p", p=128), "vbnat", pc="misc")
        p.step("tensor", lambda h: h.transpose(
            ptp[:, 256:280], cabnat[:], self.c_identf[0:24, 0:24]),
            reads=("cabnat", "c_identf"), writes=("ptp",))
        p.step("vector", lambda h: h.tensor_copy(self.cabT[:], ptp[:, 256:280]),
               reads=("ptp",), writes=("cabT",))
        p.step("tensor", lambda h: h.transpose(
            ptp[:, 280:288], qbnat[:], self.c_identf[0:8, 0:8]),
            reads=("qbnat", "c_identf"), writes=("ptp",))
        p.step("vector", lambda h: h.tensor_scalar(
            self.qb8T[:], ptp[:, 280:288], 0.125, None, ALU.mult),
            reads=("ptp",), writes=("qb8T",))
        p.step("tensor", lambda h: h.transpose(
            ptp[:, 288:296], kbnat[:], self.c_identf[0:8, 0:8]),
            reads=("kbnat", "c_identf"), writes=("ptp",))
        p.step("vector", lambda h: h.tensor_copy(self.kbT[:], ptp[:, 288:296]),
               reads=("ptp",), writes=("kbT",))
        p.step("tensor", lambda h: h.transpose(
            ptp[:, 296:304], vbnat[:], self.c_identf[0:8, 0:8]),
            reads=("vbnat", "c_identf"), writes=("ptp",))
        p.step("vector", lambda h: h.tensor_copy(self.vbT[:], ptp[:, 296:304]),
               reads=("ptp",), writes=("vbT",))
        p.barrier()
        ptmp.close()

    def _convert_weights(self, part):
        g = self._convert_weights_gen(part)
        for _ in g:
            pass

    def _convert_weights_gen(self, part):
        p = self.p
        stage, out = self.cvt_stage, self.cvt_out
        jobs = []

        def add(src, dst, rows, cols, scale=None):
            for r in range(rows // 128):
                c0 = 0
                while c0 < cols:
                    cw = min(2048, cols - c0)
                    jobs.append((src[r * 128:(r + 1) * 128, c0:c0 + cw],
                                 dst[r * 128:(r + 1) * 128, c0:c0 + cw], cw, scale))
                    c0 += cw

        if part == "qkv":
            for n in ["f_wq", "f_wk", "f_wv"]:
                sh = self.w[n].shape
                add(self.w[n], self.wb[n], sh[0], sh[1])
        else:
            for n in ["f_wo", "ca_w", "k_w", "v_w", "o_w"]:
                sh = self.w[n].shape
                add(self.w[n], self.wb[n], sh[0], sh[1])
            add(self.w["q_w"], self.wb["q_w"], C, C, scale=0.125)
            for n in ["f_w1", "f_w2", "f_w3"]:
                for e in range(E):
                    sh = self.w[n].shape
                    add(self.w[n][e], self.wb[n][e], sh[1], sh[2])
        for i, (src, dst, cw, scale) in enumerate(jobs):
            sl = i % 3
            kst, kout = f"cvt_st{sl}", f"cvt_out{sl}"
            p.step("sync", lambda h, s=stage[sl], a=src, c=cw: h.dma_start(s[:, 0:c], a),
                   writes=(kst,), dma=True, tag="cvtin")
            if scale is None:
                p.step("gpsimd", lambda h, s=stage[sl], o=out[sl], c=cw: h.tensor_copy(
                    o[:, 0:c], s[:, 0:c]),
                    reads=(kst,), writes=(kout,), tag="cvtcast")
            else:
                p.step("gpsimd", lambda h, s=stage[sl], o=out[sl], c=cw, sc=scale: h.tensor_scalar(
                    o[:, 0:c], s[:, 0:c], sc, None, ALU.mult),
                    reads=(kst,), writes=(kout,), tag="cvtscale")
            p.step("sync", lambda h, o=out[sl], d=dst, c=cw: h.dma_start(d, o[:, 0:c]),
                   reads=(kout,), writes=(dst.tensor.name + "_done",), dma=True, tag="cvtout")
            yield

    def _ffn_call(self, x_hbm, out_hbm, pref):
        self._ffn_f12(x_hbm, pref)
        self.p.barrier()
        self.f12_stack.close()
        if self.cvt_stack is not None:
            self.cvt_stack.close()
            self.cvt_stack = None
        self._ffn_f3(out_hbm, pref)
        self.p.barrier()

    def _ffn_f12(self, x_hbm, pref):
        p = self.p
        P = pref
        st = self.f12_stack = ExitStack()
        KT2 = [self.sb(P + f"KT2_{i}", [65, T], BF16, stack=st) for i in range(4)]
        Vx = self.sb(P + "Vx", [128, NT * 260], BF16, stack=st)
        QTh = self.nc.dram_tensor(P + "QTh", [C, T], BF16, kind="Internal").ap()
        OCh = self.nc.dram_tensor(P + "OCh", [C, T], BF16, kind="Internal").ap()
        PA = self.ps(P + "PA_f12", [128, 2048], stack=st)
        PB = self.ps(P + "PB_f12", [128, 1024], stack=st)
        A = [f"{P}PAb{i}" for i in range(4)]
        Bk = [f"{P}PBb{i}" for i in range(2)]
        Tk = [f"{P}PTb{i}" for i in range(2)]

        ph = ExitStack()
        PT = self.ps(P + "PT_f1", [128, 2048], BF16, stack=ph)
        wqb = self.sb("wqb", [128, 8 * C], BF16, stack=ph)
        wkb = self.sb("wkb", [128, 8 * 256], BF16, stack=ph)
        wvb = self.sb("wvb", [128, 8 * 256], BF16, stack=ph)
        xt = [self.sb(f"xt{i}", [128, C], stack=ph) for i in range(2)]
        csb = [self.sb(f"csb{i}", [128, 512], stack=ph) for i in range(2)]
        snb = [self.sb(f"snb{i}", [128, 512], stack=ph) for i in range(2)]
        h0t = [self.sb(f"h0t{i}", [128, C], stack=ph) for i in range(2)]
        at = [self.sb(f"at{i}", [128, C], stack=ph) for i in range(2)]
        xrb = [self.sb(f"xrb{i}", [128, C], BF16, stack=ph) for i in range(2)]
        sq = self.sb("sq_scr", [128, C], stack=ph)
        rs = self.sb("rs_scr", [128, C], stack=ph)
        smallc = self.sb("smallc", [128, 8], stack=ph)
        xrT = self.sb("xrT", [128, 8 * 512], BF16, stack=ph)
        qstg = [self.sb(f"qstg{i}", [128, 512], BF16, stack=ph) for i in range(2)]

        p.step("sync", lambda h: h.dma_start(
            _s3(wqb[:], 1024), _t3(self.wb["f_wq"])),
            reads=("f_wq_b_done",), writes=("wqb",), dma=True)
        p.step("sync", lambda h: h.dma_start(
            _s3(wkb[:], 256), _t3(self.wb["f_wk"])),
            reads=("f_wk_b_done",), writes=("wkb",), dma=True)
        p.step("sync", lambda h: h.dma_start(
            _s3(wvb[:], 256), _t3(self.wb["f_wv"])),
            reads=("f_wv_b_done",), writes=("wvb",), dma=True)
        p.step("vector", lambda h: h.memset(Vx[:], 1.0), writes=(f"{P}Vxones",))
        for g in range(G):
            p.step("vector", lambda h, g=g: h.memset(KT2[g][64:65, :], 1.0),
                   writes=(f"{P}KTones{g}",))

        for tc in range(NCH):
            for qq in range(4):
                tt = tc * 4 + qq
                b = tt % 2
                kx, kcs = f"xt{b}", f"cs{b}"
                kh0, ka, kxr = f"h0t{b}", f"at{b}", f"xrb{b}"
                p.step("sync", lambda h, b=b, tt=tt: h.dma_start(
                    xt[b][:], x_hbm[tt * 128:(tt + 1) * 128, :]),
                    writes=(kx,), dma=True, tag=f"{P}x{tt}")
                p.step("sync", lambda h, b=b, tt=tt: h.dma_start(
                    csb[b][:], self.cst["cos"][tt * 128:(tt + 1) * 128, :]),
                    writes=(kcs + "c",), dma=True)
                p.step("sync", lambda h, b=b, tt=tt: h.dma_start(
                    snb[b][:], self.cst["sin"][tt * 128:(tt + 1) * 128, :]),
                    writes=(kcs + "s",), dma=True)
                self._ln_stats(xt[b], kx, self.muS[:, tt:tt + 1], f"{P}mu{tt}",
                               self.rstdS[:, tt:tt + 1], f"{P}rstd{tt}",
                               sq, smallc)
                p.step("vector", lambda h, b=b, tt=tt: h.tensor_scalar(
                    h0t[b][:], xt[b][:], self.muS[:, tt:tt + 1],
                    self.rstdS[:, tt:tt + 1], ALU.subtract, ALU.mult),
                    reads=(kx, f"{P}mu{tt}", f"{P}rstd{tt}"), writes=(kh0,))
                p.step("scalar", lambda h, b=b: h.activation(
                    sq[:], h0t[b][:], AF.Square, accum_out=smallc[:, 6:7]),
                    reads=(kh0,), writes=("sm6", "sqA", "sqB"))
                p.step("vector", lambda h: h.tensor_scalar(
                    smallc[:, 7:8], smallc[:, 6:7], 1.0 / C, None, ALU.mult),
                    reads=("sm6",), writes=("sm7",))
                p.step("scalar", lambda h: h.activation(
                    smallc[:, 0:1], smallc[:, 7:8], AF.Sqrt, bias=self.epscol6[:]),
                    reads=("sm7", "epscol6"), writes=("sm0",))
                p.step("vector", lambda h: h.reciprocal(smallc[:, 1:2], smallc[:, 0:1]),
                    reads=("sm0",), writes=("sm1",))
                p.step("vector", lambda h, b=b: h.scalar_tensor_tensor(
                    at[b][:], h0t[b][:], smallc[:, 1:2], self.normw1[:],
                    ALU.mult, ALU.mult),
                    reads=(kh0, "sm1", "normw1"), writes=(ka,))
                p.step("vector", lambda h, b=b: h.tensor_tensor(
                    sq[:, 0:512], at[b][:, 0::2], csb[b][:], ALU.mult),
                    reads=(ka, kcs + "c"), writes=("sqA",))
                p.step("vector", lambda h, b=b: h.tensor_tensor(
                    sq[:, 512:1024], at[b][:, 1::2], snb[b][:], ALU.mult),
                    reads=(ka, kcs + "s"), writes=("sqB",))
                p.step("vector", lambda h, b=b: h.tensor_tensor(
                    xrb[b][:, 0:512], sq[:, 0:512], sq[:, 512:1024], ALU.subtract),
                    reads=("sqA", "sqB"), writes=(kxr + "a",))
                p.step("vector", lambda h, b=b: h.tensor_tensor(
                    rs[:, 0:512], at[b][:, 0::2], snb[b][:], ALU.mult),
                    reads=(ka, kcs + "s"), writes=("rsA",))
                p.step("vector", lambda h, b=b: h.tensor_tensor(
                    rs[:, 512:1024], at[b][:, 1::2], csb[b][:], ALU.mult),
                    reads=(ka, kcs + "c"), writes=("rsB",))
                p.step("vector", lambda h, b=b: h.tensor_tensor(
                    xrb[b][:, 512:1024], rs[:, 0:512], rs[:, 512:1024], ALU.add),
                    reads=("rsA", "rsB"), writes=(kxr + "b",))
                for kt in range(8):
                    tb = kt % 2
                    p.step("tensor", lambda h, b=b, kt=kt, tb=tb: h.transpose(
                        PT[:, tb * 1024: tb * 1024 + 128],
                        xrb[b][:, kt * 128:(kt + 1) * 128], self.c_ident[:]),
                        reads=(kxr + "a", kxr + "b", "c_ident"), writes=(Tk[tb],))
                    p.step("scalar", lambda h, kt=kt, qq=qq, tb=tb: h.copy(
                        xrT[:, kt * 512 + qq * 128: kt * 512 + qq * 128 + 128],
                        PT[:, tb * 1024: tb * 1024 + 128]),
                        reads=(Tk[tb],), writes=(f"xrT{kt}q{qq}",))
                vb = 2 + tt % 2
                p.step("tensor", lambda h, qq=qq, vb=vb: self._mmgroup(
                    h, PA[:, (vb - 2) * 512 + 1024: (vb - 2) * 512 + 1024 + 256],
                    [(xrT[:, kt * 512 + qq * 128: kt * 512 + qq * 128 + 128],
                      wvb[:, kt * 256:(kt + 1) * 256]) for kt in range(8)]),
                    reads=tuple(f"xrT{kt}q{qq}" for kt in range(8)) + ("wvb",),
                    writes=(A[vb],))
                for g in range(G):
                    p.step("scalar", lambda h, g=g, tt=tt, vb=vb: h.copy(
                        Vx[:, tt * 260 + g * 65: tt * 260 + g * 65 + 64],
                        PA[:, (vb - 2) * 512 + 1024 + g * 64:
                           (vb - 2) * 512 + 1024 + (g + 1) * 64]),
                        reads=(A[vb], f"{P}Vxones"), writes=(f"{P}Vx{tt}g{g}",))
            xr_all = tuple(f"xrT{kt}q{qq}" for kt in range(8) for qq in range(4))
            for mt in range(8):
                bq = mt % 2
                ob = mt % 2
                p.step("tensor", lambda h, mt=mt, bq=bq: self._mmgroup(
                    h, PA[:, bq * 512: bq * 512 + 512],
                    [(wqb[:, kt * 1024 + mt * 128: kt * 1024 + mt * 128 + 128],
                      xrT[:, kt * 512:(kt + 1) * 512]) for kt in range(8)]),
                    reads=xr_all + ("wqb",), writes=(A[bq],))
                p.step("scalar", lambda h, bq=bq, ob=ob: h.copy(
                    qstg[ob][:], PA[:, bq * 512: bq * 512 + 512]),
                    reads=(A[bq],), writes=(f"qstg{ob}",))
                p.step("sync", lambda h, mt=mt, ob=ob, tc=tc: h.dma_start(
                    QTh[mt * 128:(mt + 1) * 128, tc * 512:(tc + 1) * 512],
                    qstg[ob][:]),
                    reads=(f"qstg{ob}",), writes=(f"{P}QTm{mt}c{tc}",), dma=True)
            for mt in range(2):
                bq = mt % 2
                p.step("tensor", lambda h, mt=mt, bq=bq: self._mmgroup(
                    h, PA[:, bq * 512: bq * 512 + 512],
                    [(wkb[:, kt * 256 + mt * 128: kt * 256 + mt * 128 + 128],
                      xrT[:, kt * 512:(kt + 1) * 512]) for kt in range(8)]),
                    reads=xr_all + ("wkb",), writes=(A[bq],))
                for gg in range(2):
                    g = 2 * mt + gg
                    p.step("scalar", lambda h, g=g, gg=gg, bq=bq, tc=tc: h.copy(
                        KT2[g][0:64, tc * 512:(tc + 1) * 512],
                        PA[gg * 64:(gg + 1) * 64, bq * 512: bq * 512 + 512]),
                        reads=(A[bq],), writes=(f"{P}KT{g}c{tc}",))
        if (P + "QT") in self.taps:
            self.tap_dram(P + "QT", QTh,
                          [f"{P}QTm{mt}c{tc}" for mt in range(8) for tc in range(8)])
        self.tap(P + "KT0", KT2[0][:], [f"{P}KT0c{tc}" for tc in range(8)])

        self.tap(P + "Vx", Vx[:], [f"{P}Vx{tt}g{g}" for tt in range(NT) for g in range(G)])
        if P == "A" and self.cvt_stack is not None and self.stop_after == "f1":
            self._convert_weights("moe")
        p.barrier()
        ph.close()
        if self.stop_after == "f1":
            return

        ph = ExitStack()
        PC = self.ps(P + "PC_f2", [128, 1024], stack=ph)
        Ck = [f"{P}PCb{i}" for i in range(2)]
        qhT = [self.sb(f"f2qhT{i}", [65, T], BF16, stack=ph) for i in range(2)]
        mbuf = self.sb("mbuf", [128, 32], stack=ph)
        m32b = self.sb("m32b", [32, 128], BF16, stack=ph)
        pbuf = [self.sb(f"pbuf{i}", [128, 512], BF16, stack=ph) for i in range(3)]
        rrow = self.sb("rrow", [1, 512], stack=ph)
        rbs = self.sb("rbs", [64, 512], stack=ph)
        ohT = self.sb("ohT", [64, T], BF16, stack=ph)
        sc2 = self.sb("f2small", [128, 2], stack=ph)
        KTkeys = lambda g: tuple(f"{P}KT{g}c{tc}" for tc in range(8))
        Vxkeys = lambda g: tuple(f"{P}Vx{tt}g{g}" for tt in range(NT))
        QTkeys = tuple(f"{P}QTm{mt}c{tc}" for mt in range(8) for tc in range(8))

        def gen_pass1(hh):
            g = hh // 4
            qb = hh % 2
            kq = f"f2qhT{qb}"
            p.step("sync", lambda h, hh=hh, qb=qb: h.dma_start(
                qhT[qb][0:64, :], QTh[hh * 64:(hh + 1) * 64, :]),
                reads=QTkeys, writes=(kq,), dma=True, pclass="p64")
            yield
            cntp = 0
            for it in range(NT):
                njc = it // 4 + 1
                jc = 0
                while jc < njc:
                    w2 = 2 if jc + 1 < njc else 1
                    cntp += 1
                    pb_t = PB
                    keys = (Bk[0], Bk[1])
                    for u in range(w2):
                        p.step("tensor", lambda h, it=it, jc=jc, u=u, qb=qb, g=g,
                               pb_t=pb_t: h.matmul(
                            pb_t[:, u * 512: u * 512 + 512],
                            qhT[qb][0:64, it * 128:(it + 1) * 128],
                            KT2[g][0:64, (jc + u) * 512:(jc + u + 1) * 512],
                            start=True, stop=True),
                            reads=(kq,) + KTkeys(g), writes=(keys[u],), tag="p1mm")
                        yield
                        if jc + u == it // 4:
                            dlt = 128 * (it % 4)
                            p.step("vector", lambda h, dlt=dlt, u=u, pb_t=pb_t: h.tensor_tensor(
                                pb_t[:, u * 512: u * 512 + 512],
                                pb_t[:, u * 512: u * 512 + 512],
                                self.c_stair1[:, 384 - dlt: 896 - dlt], ALU.add),
                                reads=(keys[u], "c_stair1"), writes=(keys[u],))
                            yield
                    if jc == 0:
                        p.step("vector", lambda h, it=it, w2=w2, pb_t=pb_t: h.tensor_reduce(
                            mbuf[:, it:it + 1], pb_t[:, 0: w2 * 512],
                            AX.X, ALU.max),
                            reads=keys[:w2], writes=(f"mb{it}",))
                        yield
                    else:
                        p.step("vector", lambda h, w2=w2, pb_t=pb_t: h.tensor_reduce(
                            sc2[:, 0:1], pb_t[:, 0: w2 * 512],
                            AX.X, ALU.max),
                            reads=keys[:w2], writes=("f2sm",))
                        yield
                        p.step("vector", lambda h, it=it: h.tensor_tensor(
                            mbuf[:, it:it + 1], mbuf[:, it:it + 1],
                            sc2[:, 0:1], ALU.max),
                            reads=(f"mb{it}", "f2sm"), writes=(f"mb{it}",))
                        yield
                    jc += w2
            mkeys = tuple(f"mb{it}" for it in range(NT))
            p.step("tensor", lambda h: h.transpose(
                PC[0:32, 512:640], mbuf[:], self.c_identf[:]),
                reads=mkeys + ("c_identf",), writes=(Ck[1],), tag="mT")
            yield
            p.step("vector", lambda h: h.tensor_scalar(
                m32b[:], PC[0:32, 512:640], -1.0, None, ALU.mult),
                reads=(Ck[1],), writes=("m32",))
            yield
            p.step("sync", lambda h: h.dma_start(self.mflatb, m32b[:]),
                   reads=("m32",), writes=("mflat",), dma=True, pclass="misc")
            yield
            p.step("sync", lambda h, qb=qb: h.dma_start(
                qhT[qb][64:65, :], self.mflatb.rearrange("p n -> () (p n)")),
                reads=("mflat",), writes=(kq + "m",), dma=True, pclass="p1")
            yield

        def gen_pass2(hh):
            g = hh // 4
            qb = hh % 2
            kq = f"f2qhT{qb}"
            pend = [None]

            def make_finals(isp):
                po = 2 + isp % 2
                po2 = 2 + (isp + 1) % 2
                poo = 1024 + (isp % 2) * 512

                def finals():
                    p.step("tensor", lambda h, po2=po2: h.matmul(
                        PA[0:64, po2 * 512: po2 * 512 + 512],
                        self.c_ones128f[:, 0:64], rrow[:],
                        start=True, stop=True),
                        reads=("rrow", "c_ones128f"), writes=(A[po2],), tag="rb")
                    p.step("vector", lambda h, po2=po2: h.tensor_copy(
                        rbs[:], PA[0:64, po2 * 512: po2 * 512 + 512]),
                        reads=(A[po2],), writes=("rbs",))
                    p.step("vector", lambda h, isp=isp, poo=poo: h.tensor_tensor(
                        ohT[:, isp * 512:(isp + 1) * 512],
                        PA[0:64, poo: poo + 512], rbs[:], ALU.mult),
                        reads=(A[po], "rbs"), writes=("ohT",))
                return finals

            for isp in range(8):
                njc = 4 * isp + 4
                lastjc = njc - 1
                po = 2 + isp % 2
                poo = 1024 + (isp % 2) * 512
                poo2_r = poo

                def emit_pv(jc, lastjc=lastjc, g=g, po=po, poo=poo):
                    pb3 = jc % 3
                    p.step("tensor", lambda h, jc=jc, pb3=pb3, g=g,
                           lastjc=lastjc, poo=poo: h.matmul(
                        PA[0:65, poo: poo + 512],
                        Vx[:, jc * 260 + g * 65: jc * 260 + (g + 1) * 65],
                        pbuf[pb3][:], start=(jc == 0), stop=(jc == lastjc)),
                        reads=(f"pbuf{pb3}",) + Vxkeys(g), writes=(A[po],),
                        tag="pv")

                for jc in range(njc):
                    b3 = jc % 3
                    spb = [PA, PA, PC][b3]
                    soff = [0, 512, 0][b3]
                    skey = [A[0], A[1], Ck[0]][b3]
                    pb3 = jc % 3
                    p.step("tensor", lambda h, isp=isp, jc=jc, qb=qb, g=g,
                           spb=spb, soff=soff: h.matmul(
                        spb[:, soff: soff + 512],
                        KT2[g][:, jc * 128:(jc + 1) * 128],
                        qhT[qb][:, isp * 512:(isp + 1) * 512],
                        start=True, stop=True),
                        reads=(kq, kq + "m", f"{P}KTones{g}") + KTkeys(g),
                        writes=(skey,), tag="p2mm")
                    yield
                    if jc // 4 == isp:
                        d2 = 128 * (jc % 4)
                        p.step("vector", lambda h, d2=d2, spb=spb,
                               soff=soff: h.tensor_tensor(
                            spb[:, soff: soff + 512],
                            spb[:, soff: soff + 512],
                            self.c_stairT[:, 384 - d2: 896 - d2], ALU.add),
                            reads=(skey, "c_stairT"), writes=(skey,))
                        yield
                    p.step("scalar", lambda h, pb3=pb3, spb=spb, soff=soff: h.activation(
                        pbuf[pb3][:], spb[:, soff: soff + 512],
                        AF.Exp, scale=0.125),
                        reads=(skey,), writes=(f"pbuf{pb3}",))
                    yield
                    if jc == 1 and pend[0] is not None:
                        pend[0]()
                        pend[0] = None
                        yield
                    if jc >= 2:
                        emit_pv(jc - 2)
                        yield
                emit_pv(njc - 2)
                emit_pv(njc - 1)
                yield
                p.step("vector", lambda h, poo=poo2_r: h.reciprocal(
                    rrow[:], PA[64:65, poo: poo + 512]),
                    reads=(A[po],), writes=("rrow",))
                pend[0] = make_finals(isp)
            pend[0]()
            p.step("sync", lambda h, hh=hh: h.dma_start(
                OCh[hh * 64:(hh + 1) * 64, :], ohT[:]),
                reads=("ohT",), writes=(f"{P}OC{hh}",), dma=True, pclass="p64")
            yield

        cvtg = None
        if P == "A" and self.cvt_stack is not None:
            cvtg = self._convert_weights_gen("moe")
        for _ in gen_pass1(0):
            pass
        for hh in range(H):
            g2 = gen_pass2(hh)
            g1 = gen_pass1(hh + 1) if hh + 1 < H else None
            k = 0
            while True:
                done2 = False
                for _ in range(3):
                    try:
                        next(g2)
                    except StopIteration:
                        done2 = True
                        break
                if g1 is not None:
                    for _ in range(2):
                        try:
                            next(g1)
                        except StopIteration:
                            g1 = None
                            break
                if cvtg is not None:
                    k += 1
                    if k % 4 == 0:
                        try:
                            next(cvtg)
                        except StopIteration:
                            cvtg = None
                if done2:
                    if g1 is not None:
                        for _ in g1:
                            pass
                    break
        if cvtg is not None:
            for _ in cvtg:
                pass
        ockeys = tuple(f"{P}OC{hh}" for hh in range(H))
        if (P + "ocT") in self.taps:
            self.tap_dram(P + "ocT", OCh, ockeys)
        p.barrier()
        ph.close()
        if self.stop_after == "f2":
            return

        ph = ExitStack()
        PT = self.ps(P + "PT_f2b", [128, 2048], BF16, stack=ph)
        wob = self.sb("wob", [128, 8 * C], BF16, stack=ph)
        occ = [self.sb(f"occ{i}", [128, 8 * 128], BF16, stack=ph) for i in range(2)]
        xt2 = [self.sb(f"xt2{i}", [128, C], stack=ph) for i in range(2)]
        h0r = self.sb("h0r", [128, C], stack=ph)
        ht = self.sb("ht", [128, C], stack=ph)
        c0t = [self.sb(f"c0t{i}", [128, C], stack=ph) for i in range(2)]
        bt = self.sb("bt", [128, C], stack=ph)
        bthi = self.sb("bthi", [128, C], BF16, stack=ph)
        btlo = self.sb("btlo", [128, C], BF16, stack=ph)
        blosb = self.sb("blosb", [128, C], BF16, stack=ph)
        lgs = self.sb("lgs", [128, 8], stack=ph)
        lgall = self.sb("lgall", [128, NT * 8], stack=ph)
        dbg2 = self.sb("dbg2", [128, NT * 4], stack=ph)
        btb = [self.sb(f"btb{i}", [128, C], BF16, stack=ph) for i in range(2)]
        sq2 = self.sb("sq2_scr", [128, C], stack=ph)
        smallc2 = self.sb("smallc2", [128, 8], stack=ph)
        tp8 = self.sb("tp8", [128, 8], stack=ph)
        self.lg2scr = self.sb("lg2scr", [128, 8], stack=ph)

        p.step("sync", lambda h: h.dma_start(
            _s3(wob[:], 1024), _t3(self.wb["f_wo"])),
            reads=("f_wo_b_done",), writes=("wob",), dma=True)

        SUB = {"f2c": 1, "f2d": 2}.get(self.stop_after, 9)
        for tt in range(NT):
            b = tt % 2
            p.step("sync", lambda h, b=b, tt=tt: h.dma_start(
                xt2[b][:], x_hbm[tt * 128:(tt + 1) * 128, :]),
                writes=(f"xt2{b}",), dma=True)
            p.step("sync", lambda h, b=b, tt=tt: h.dma_start(
                _s3(occ[b][:], 128), _t3(OCh[:, tt * 128:(tt + 1) * 128])),
                reads=ockeys, writes=(f"occ{b}",), dma=True)
            for n0 in range(2):
                p.step("tensor", lambda h, b=b, n0=n0: self._mmgroup(
                    h, PB[:, n0 * 512:(n0 + 1) * 512],
                    [(occ[b][:, kt * 128:(kt + 1) * 128],
                      wob[:, kt * C + n0 * 512: kt * C + n0 * 512 + 512])
                     for kt in range(8)]),
                    reads=(f"occ{b}", "wob"), writes=(Bk[n0],), tag="womm")
            p.step("vector", lambda h, b=b, tt=tt: h.tensor_scalar(
                h0r[:], xt2[b][:], self.muS[:, tt:tt + 1], self.rstdS[:, tt:tt + 1],
                ALU.subtract, ALU.mult),
                reads=(f"xt2{b}", f"{P}mu{tt}", f"{P}rstd{tt}"), writes=("h0r",))
            for n0 in range(2):
                p.step("vector", lambda h, n0=n0: h.tensor_tensor(
                    ht[:, n0 * 512:(n0 + 1) * 512], PB[:, n0 * 512:(n0 + 1) * 512],
                    h0r[:, n0 * 512:(n0 + 1) * 512], ALU.add),
                    reads=(Bk[n0], "h0r"), writes=(f"ht{n0}",))
            p.step("vector", lambda h, b=b: h.scalar_tensor_tensor(
                c0t[b][:], ht[:], 0.5, xt2[b][:], ALU.mult, ALU.add),
                reads=("ht0", "ht1", f"xt2{b}"), writes=(f"c0t{b}",))
            p.step("sync", lambda h, b=b, tt=tt: h.dma_start(
                self.cinit[tt * 128:(tt + 1) * 128, :], c0t[b][:]),
                reads=(f"c0t{b}",), writes=(f"{P}cinit{tt}",), dma=True)
            if SUB < 2:
                continue
            p.step("scalar", lambda h: h.activation(
                sq2[:], ht[:], AF.Square, accum_out=smallc2[:, 0:1]),
                reads=("ht0", "ht1"), writes=("s2m0", "sq2"))
            p.step("vector", lambda h: h.tensor_scalar(
                smallc2[:, 1:2], smallc2[:, 0:1], 1.0 / C, None, ALU.mult),
                reads=("s2m0",), writes=("s2m1",))
            p.step("scalar", lambda h: h.activation(
                smallc2[:, 2:3], smallc2[:, 1:2], AF.Sqrt, bias=self.epscol6[:]),
                reads=("s2m1", "epscol6"), writes=("s2m2",))
            p.step("vector", lambda h: h.reciprocal(smallc2[:, 3:4], smallc2[:, 2:3]),
                reads=("s2m2",), writes=("s2m3",))
            p.step("vector", lambda h: h.scalar_tensor_tensor(
                bt[:], ht[:], smallc2[:, 3:4], self.normw2[:], ALU.mult, ALU.mult),
                reads=("ht0", "ht1", "s2m3", "normw2"), writes=("bt",))
            p.step("scalar", lambda h: h.activation(
                bthi[:], bt[:], AF.Copy),
                reads=("bt",), writes=("bthi",))
            p.step("vector", lambda h: h.tensor_tensor(
                btlo[:], bt[:], bthi[:], ALU.subtract),
                reads=("bt", "bthi"), writes=("btlo",))
            for kt in range(8):
                tb = kt % 2
                p.step("tensor", lambda h, kt=kt, tb=tb: h.transpose(
                    PT[:, tb * 1024: tb * 1024 + 128], bthi[:, kt * 128:(kt + 1) * 128],
                    self.c_ident[:]),
                    reads=("bthi", "c_ident"), writes=(Tk[tb],), tag="btT")
                p.step("scalar", lambda h, kt=kt, tb=tb, b=b: h.copy(
                    btb[b][:, kt * 128:(kt + 1) * 128], PT[:, tb * 1024: tb * 1024 + 128]),
                    reads=(Tk[tb],), writes=(f"btb{b}_{kt}",))
                p.step("tensor", lambda h, kt=kt, tb=tb: h.transpose(
                    PT[:, tb * 1024 + 128: tb * 1024 + 256],
                    btlo[:, kt * 128:(kt + 1) * 128], self.c_ident[:]),
                    reads=("btlo", "c_ident"), writes=(Tk[tb],), tag="bloT")
                p.step("vector", lambda h, kt=kt, tb=tb: h.tensor_copy(
                    blosb[:, kt * 128:(kt + 1) * 128],
                    PT[:, tb * 1024 + 128: tb * 1024 + 256]),
                    reads=(Tk[tb],), writes=(f"blo{kt}",))
            p.step("sync", lambda h, b=b, tt=tt: h.dma_start(
                _t3(self.bT_hbm[:, tt * 128:(tt + 1) * 128]), _s3(btb[b][:], 128)),
                reads=tuple(f"btb{b}_{kt}" for kt in range(8)),
                writes=(f"{P}bT{tt}",), dma=True)
            if SUB < 3:
                continue
            p.step("tensor", lambda h, b=b: self._mmgroup(
                h, PA[:, 0:8],
                [(btb[b][:, kt * 128:(kt + 1) * 128], self.routwh[:, kt * 8:(kt + 1) * 8])
                 for kt in range(8)] +
                [(btb[b][:, kt * 128:(kt + 1) * 128], self.routwl[:, kt * 8:(kt + 1) * 8])
                 for kt in range(8)] +
                [(blosb[:, kt * 128:(kt + 1) * 128], self.routwh[:, kt * 8:(kt + 1) * 8])
                 for kt in range(8)]),
                reads=tuple(f"btb{b}_{kt}" for kt in range(8)) +
                      tuple(f"blo{kt}" for kt in range(8)) + ("routwh", "routwl"),
                writes=(A[0],), tag="router")
            p.step("vector", lambda h: h.tensor_tensor(
                lgs[:], PA[:, 0:8], self.routbbc[:, 0:8], ALU.add),
                reads=(A[0], "routbbc"), writes=("lgs",))
            if (P + "lgall") in self.taps:
                p.step("vector", lambda h, tt=tt: h.tensor_copy(
                    lgall[:, tt * 8:(tt + 1) * 8], lgs[:]),
                    reads=("lgs",), writes=(f"lgall{tt}",))
            self._top2(tt, lgs[:], "lgs", smallc2, tp8, P)
            if (P + "dbg2") in self.taps:
                p.step("vector", lambda h, tt=tt: h.tensor_copy(
                    dbg2[:, tt * 4: tt * 4 + 1], smallc2[:, 4:5]),
                    reads=("t2w1",), writes=(f"dbg2a{tt}",))
                p.step("vector", lambda h, tt=tt: h.tensor_copy(
                    dbg2[:, tt * 4 + 1: tt * 4 + 2], smallc2[:, 5:6]),
                    reads=("t2i2",), writes=(f"dbg2b{tt}",))
                p.step("vector", lambda h, tt=tt: h.tensor_copy(
                    dbg2[:, tt * 4 + 2: tt * 4 + 3], smallc2[:, 6:7]),
                    reads=("t2w2",), writes=(f"dbg2c{tt}",))
                p.step("vector", lambda h, tt=tt: h.tensor_copy(
                    dbg2[:, tt * 4 + 3: tt * 4 + 4], smallc2[:, 7:8]),
                    reads=("t2m2b",), writes=(f"dbg2d{tt}",))
        if (P + "lgall") in self.taps:
            self.tap(P + "lgall", lgall[:], [f"lgall{tt}" for tt in range(NT)])
        if (P + "dbg2") in self.taps:
            self.tap(P + "dbg2", dbg2[:], [f"dbg2a{tt}" for tt in range(NT)] +
                     [f"dbg2d{tt}" for tt in range(NT)])
        p.barrier()
        ph.close()
        if (P + "cinit") in self.taps:
            self.tap_dram(P + "cinit", self.cinit,
                          [f"{P}cinit{tt}" for tt in range(NT)])
        if (P + "bT") in self.taps:
            self.tap_dram(P + "bT", self.bT_hbm, [f"{P}bT{tt}" for tt in range(NT)])
        if (P + "comb") in self.taps:
            self.tap(P + "comb", self.comb[:], [f"{P}comb{tt}" for tt in range(NT)])

    def _top2(self, tt, lg_psum, lg_key, smallc2, tp8, P):
        p = self.p
        iota = self.c_iota8
        lg2 = self.lg2scr
        p.step("vector", lambda h: h.tensor_reduce(
            smallc2[:, 4:5], lg_psum, AX.X, ALU.max),
            reads=(lg_key,), writes=("t2m1",))
        p.step("vector", lambda h: h.tensor_scalar(
            tp8[:], lg_psum, smallc2[:, 4:5], None, ALU.is_ge),
            reads=(lg_key, "t2m1"), writes=("t2a",))
        p.step("vector", lambda h: h.tensor_scalar(
            tp8[:], tp8[:], -99.0, 99.0, ALU.mult, ALU.add),
            reads=("t2a",), writes=("t2a",))
        p.step("vector", lambda h: h.tensor_tensor(
            tp8[:], tp8[:], iota[:], ALU.add),
            reads=("t2a", "c_iota8"), writes=("t2a",))
        p.step("vector", lambda h: h.tensor_reduce(
            smallc2[:, 5:6], tp8[:], AX.X, ALU.min),
            reads=("t2a",), writes=("t2i1",))
        p.step("vector", lambda h: h.tensor_scalar(
            tp8[:], iota[:], smallc2[:, 5:6], None, ALU.is_equal),
            reads=("c_iota8", "t2i1"), writes=("t2mask1",))
        p.step("vector", lambda h: h.scalar_tensor_tensor(
            lg2[:], tp8[:], -1e30, lg_psum, ALU.mult, ALU.add),
            reads=("t2mask1", lg_key), writes=("t2lg2",))
        p.step("vector", lambda h: h.tensor_reduce(
            smallc2[:, 6:7], lg2[:], AX.X, ALU.max),
            reads=("t2lg2",), writes=("t2m2",))
        p.step("vector", lambda h: h.tensor_tensor(
            smallc2[:, 7:8], smallc2[:, 4:5], smallc2[:, 6:7], ALU.subtract),
            reads=("t2m1", "t2m2"), writes=("t2d",))
        p.step("scalar", lambda h: h.activation(
            smallc2[:, 4:5], smallc2[:, 7:8], AF.Sigmoid),
            reads=("t2d",), writes=("t2w1",))
        p.step("scalar", lambda h: h.activation(
            smallc2[:, 6:7], smallc2[:, 7:8], AF.Sigmoid, scale=-1.0),
            reads=("t2d",), writes=("t2w2",))
        p.step("vector", lambda h, tt=tt: h.tensor_scalar(
            self.comb[:, tt * 8:(tt + 1) * 8], tp8[:], smallc2[:, 4:5], 0.5,
            ALU.mult, ALU.mult),
            reads=("t2mask1", "t2w1"), writes=(f"{P}combA{tt}",))
        p.step("vector", lambda h: h.tensor_reduce(
            smallc2[:, 7:8], lg2[:], AX.X, ALU.max),
            reads=("t2lg2",), writes=("t2m2b",))
        p.step("vector", lambda h: h.tensor_scalar(
            tp8[:], lg2[:], smallc2[:, 7:8], None, ALU.is_ge),
            reads=("t2lg2", "t2m2b"), writes=("t2b",))
        p.step("vector", lambda h: h.tensor_scalar(
            tp8[:], tp8[:], -99.0, 99.0, ALU.mult, ALU.add),
            reads=("t2b",), writes=("t2b",))
        p.step("vector", lambda h: h.tensor_tensor(
            tp8[:], tp8[:], iota[:], ALU.add),
            reads=("t2b", "c_iota8"), writes=("t2b",))
        p.step("vector", lambda h: h.tensor_reduce(
            smallc2[:, 5:6], tp8[:], AX.X, ALU.min),
            reads=("t2b",), writes=("t2i2",))
        p.step("vector", lambda h: h.tensor_scalar(
            tp8[:], iota[:], smallc2[:, 5:6], None, ALU.is_equal),
            reads=("c_iota8", "t2i2"), writes=("t2mask2",))
        p.step("vector", lambda h, tt=tt: h.tensor_scalar(
            lg2[:], tp8[:], smallc2[:, 6:7], 0.5, ALU.mult, ALU.mult),
            reads=("t2mask2", "t2w2"), writes=("t2part2",))
        p.step("vector", lambda h, tt=tt: h.tensor_tensor(
            self.comb[:, tt * 8:(tt + 1) * 8],
            self.comb[:, tt * 8:(tt + 1) * 8], lg2[:], ALU.add),
            reads=(f"{P}combA{tt}", "t2part2"), writes=(f"{P}comb{tt}",))

    def _ffn_f3(self, out_hbm, pref):
        p = self.p
        P = pref
        st = ExitStack()
        hhT = self.sb("hhT", [128, 32 * TBLK], BF16, stack=st)
        cblk = self.sb("cblk", [128, 8 * C], stack=st)
        bTb = self.sb("bTb", [128, 8 * TBLK], BF16, stack=st)
        w12 = [self.sb(f"w12_{i}", [128, 2 * 8 * 512], BF16, stack=st) for i in range(2)]
        w3s = [self.sb(f"w3s_{i}", [128, 4 * C], BF16, stack=st) for i in range(2)]
        s1b = [self.sb(f"s1b_{i}", [128, 512], BF16, stack=st) for i in range(2)]
        PA = self.ps("PA_f3", [128, 2048], stack=st)
        PB = self.ps("PB_f3", [128, 2048], stack=st)
        A = [f"{P}3PAb{i}" for i in range(4)]
        Bk = [f"{P}3PBb{i}" for i in range(4)]

        w1b, w2b, w3b = self.wb["f_w1"], self.wb["f_w2"], self.wb["f_w3"]

        for nb in range(NBLK):
            p.step("sync", lambda h, nb=nb: h.dma_start(
                _s3(cblk[:], C), _t3(self.cinit[nb * TBLK:(nb + 1) * TBLK, :])),
                reads=tuple(f"{P}cinit{nb * 8 + i}" for i in range(8)),
                writes=(f"cblk",), dma=True)
            p.step("sync", lambda h, nb=nb: h.dma_start(
                _s3(bTb[:], TBLK), _t3(self.bT_hbm[:, nb * TBLK:(nb + 1) * TBLK])),
                reads=tuple(f"{P}bT{nb * 8 + i}" for i in range(8)),
                writes=("bTb",), dma=True)
            for e in range(E):
                for fc in range(8):
                    sl = (e * 8 + fc) % 2
                    ksl = f"w12_{sl}"
                    p.step("sync", lambda h, e=e, fc=fc, sl=sl: h.dma_start(
                        _s3(w12[sl][:, 0:4096], 512),
                        _t3(w1b[e][:, fc * 512:(fc + 1) * 512])),
                        reads=("f_w1_b_done",), writes=(ksl,), dma=True)
                    p.step("sync", lambda h, e=e, fc=fc, sl=sl: h.dma_start(
                        _s3(w12[sl][:, 4096:8192], 512),
                        _t3(w2b[e][:, fc * 512:(fc + 1) * 512])),
                        reads=("f_w2_b_done",), writes=(ksl + "x",), dma=True)
                    for ft in range(4):
                        fg = fc * 4 + ft
                        b1c = self.b1sb[:, e * 32 + fg: e * 32 + fg + 1]
                        b2c = self.b2sb[:, e * 32 + fg: e * 32 + fg + 1]
                        for th in range(2):
                            r = (ft * 2 + th) % 2
                            kh1, kh2 = A[r], A[2 + r]
                            sb1 = s1b[r]
                            p.step("tensor", lambda h, sl=sl, ft=ft, th=th, r=r: self._mmgroup(
                                h, PA[:, r * 512: r * 512 + 512],
                                [(w12[sl][:, kt * 512 + ft * 128: kt * 512 + ft * 128 + 128],
                                  bTb[:, kt * TBLK + th * 512: kt * TBLK + th * 512 + 512])
                                 for kt in range(8)]),
                                reads=(ksl, "bTb"), writes=(kh1,), tag="h1mm")
                            p.step("tensor", lambda h, sl=sl, ft=ft, th=th, r=r: self._mmgroup(
                                h, PA[:, 1024 + r * 512: 1024 + r * 512 + 512],
                                [(w12[sl][:, 4096 + kt * 512 + ft * 128:
                                          4096 + kt * 512 + ft * 128 + 128],
                                  bTb[:, kt * TBLK + th * 512: kt * TBLK + th * 512 + 512])
                                 for kt in range(8)]),
                                reads=(ksl + "x", "bTb"), writes=(kh2,), tag="h2mm")
                            p.step("scalar", lambda h, r=r, b1c=b1c: h.activation(
                                s1b[r][:], PA[:, r * 512: r * 512 + 512],
                                AF.Silu, bias=b1c),
                                reads=(kh1, "b1sb"), writes=(f"s1b_{r}",))
                            p.step("vector", lambda h, r=r, b2c=b2c, fg=fg, th=th: h.scalar_tensor_tensor(
                                hhT[:, fg * TBLK + th * 512: fg * TBLK + th * 512 + 512],
                                PA[:, 1024 + r * 512: 1024 + r * 512 + 512],
                                b2c, s1b[r][:], ALU.add, ALU.mult),
                                reads=(kh2, "b2sb", f"s1b_{r}"),
                                writes=(f"hhT{fg}t{th}",))
                hkeys = tuple(f"hhT{fg}t{th}" for fg in range(32) for th in range(2))
                for sp in range(2):
                    for fc in range(8):
                        sl = fc % 2
                        kw3 = f"w3s_{sl}"
                        p.step("sync", lambda h, e=e, fc=fc, sl=sl: h.dma_start(
                            _s3(w3s[sl][:], C),
                            _t3(w3b[e][fc * 512:(fc + 1) * 512, :])),
                            reads=("f_w3_b_done",), writes=(kw3,), dma=True)
                        for tq in range(4):
                            tt = sp * 4 + tq
                            pb = [PA, PA, PB, PB][tq]
                            off = [0, 1024, 0, 1024][tq]
                            bk = [A, A, Bk, Bk][tq]
                            bko = [0, 2, 0, 2][tq]
                            for n0 in range(2):
                                p.step("tensor", lambda h, sl=sl, fc=fc, tt=tt, n0=n0,
                                       pb=pb, off=off: self._mmgroup(
                                    h, pb[:, off + n0 * 512: off + n0 * 512 + 512],
                                    [(hhT[:, (fc * 4 + fi) * TBLK + tt * 128:
                                           (fc * 4 + fi) * TBLK + tt * 128 + 128],
                                      w3s[sl][:, fi * C + n0 * 512: fi * C + n0 * 512 + 512])
                                     for fi in range(4)],
                                    start=(fc == 0), stop=False),
                                    reads=(kw3,) + hkeys,
                                    writes=(bk[bko + n0],), tag="w3mm")
                    for tq in range(4):
                        tt = sp * 4 + tq
                        pb = [PA, PA, PB, PB][tq]
                        off = [0, 1024, 0, 1024][tq]
                        bk = [A, A, Bk, Bk][tq]
                        bko = [0, 2, 0, 2][tq]
                        for n0 in range(2):
                            p.step("tensor", lambda h, e=e, n0=n0, pb=pb, off=off: h.matmul(
                                pb[:, off + n0 * 512: off + n0 * 512 + 512],
                                self.c_ones128b[:],
                                self.b3cat[:, e * C + n0 * 512: e * C + n0 * 512 + 512],
                                start=False, stop=True),
                                reads=("c_ones128b", "b3cat", bk[bko + n0]),
                                writes=(bk[bko + n0],), tag="b3mm")
                        combc = self.comb[:, (nb * 8 + tt) * 8 + e: (nb * 8 + tt) * 8 + e + 1]
                        p.step("vector", lambda h, tt=tt, e=e, pb=pb, off=off,
                               combc=combc: h.scalar_tensor_tensor(
                            cblk[:, tt * C: (tt + 1) * C], pb[:, off: off + 1024],
                            combc, cblk[:, tt * C: (tt + 1) * C], ALU.mult, ALU.add),
                            reads=(bk[bko], bk[bko + 1], f"{P}comb{nb * 8 + tt}", "cblk"),
                            writes=("cblk",), tag="cacc")
            p.step("sync", lambda h, nb=nb: h.dma_start(
                _t3(out_hbm[nb * TBLK:(nb + 1) * TBLK, :]), _s3(cblk[:], C)),
                reads=("cblk",), writes=(f"{P}out{nb}",), dma=True)
        if (P + "out") in self.taps:
            self.tap_dram(P + "out", out_hbm, [f"{P}out{nb}" for nb in range(NBLK)])
        st.close()

    def _sparse_slot(self, in_hbm, out_hbm, pref):
        self._sparse_s1(in_hbm, pref)
        self.p.barrier()
        self._sparse_s2(pref)
        self.p.barrier()
        self._sparse_s3(in_hbm, out_hbm, pref)
        self.p.barrier()

    def _sparse_s1(self, in_hbm, P):
        p = self.p
        st = ExitStack()
        cawb = self.sb("cawb", [128, 8 * 3 * C], BF16, stack=st)
        qwb = self.sb("s_qwb", [128, 8 * C], BF16, stack=st)
        kwb = self.sb("s_kwb", [128, 8 * C], BF16, stack=st)
        vwb = self.sb("s_vwb", [128, 8 * C], BF16, stack=st)
        xt = [self.sb(f"s_xt{i}", [128, C], stack=st) for i in range(2)]
        gt = [self.sb(f"s_gt{i}", [128, C], BF16, stack=st) for i in range(2)]
        sq = self.sb("s_sq", [128, C], stack=st)
        smallc = self.sb("s_smallc", [128, 8], stack=st)
        gT = self.sb("s_gT", [128, 8 * 512], BF16, stack=st)
        qkvT = self.sb("s_qkvT", [128, 24 * 512], BF16, stack=st)
        ostg = [self.sb(f"s_ostg{i}", [128, 512], BF16, stack=st) for i in range(2)]
        PA = self.ps("PA_s1", [128, 2048], stack=st)
        PT = self.ps("PT_s1", [128, 2048], BF16, stack=st)
        A = [f"{P}1PAb{i}" for i in range(4)]
        Tk = [f"{P}1PTb{i}" for i in range(2)]

        p.step("sync", lambda h: h.dma_start(
            _s3(cawb[:], {"f_wq":1024,"f_wk":256,"f_wv":256,"f_wo":1024,"ca_w":3072,"q_w":1024,"k_w":1024,"v_w":1024,"o_w":1024}["ca_w"]), _t3(self.wb["ca_w"])),
            reads=("ca_w_b_done",), writes=("cawb",), dma=True)
        p.step("sync", lambda h: h.dma_start(
            _s3(qwb[:], {"f_wq":1024,"f_wk":256,"f_wv":256,"f_wo":1024,"ca_w":3072,"q_w":1024,"k_w":1024,"v_w":1024,"o_w":1024}["q_w"]), _t3(self.wb["q_w"])),
            reads=("q_w_b_done",), writes=("s_qwb",), dma=True)
        p.step("sync", lambda h: h.dma_start(
            _s3(kwb[:], {"f_wq":1024,"f_wk":256,"f_wv":256,"f_wo":1024,"ca_w":3072,"q_w":1024,"k_w":1024,"v_w":1024,"o_w":1024}["k_w"]), _t3(self.wb["k_w"])),
            reads=("k_w_b_done",), writes=("s_kwb",), dma=True)
        p.step("sync", lambda h: h.dma_start(
            _s3(vwb[:], {"f_wq":1024,"f_wk":256,"f_wv":256,"f_wo":1024,"ca_w":3072,"q_w":1024,"k_w":1024,"v_w":1024,"o_w":1024}["v_w"]), _t3(self.wb["v_w"])),
            reads=("v_w_b_done",), writes=("s_vwb",), dma=True)

        for tc in range(NCH):
            for qq in range(4):
                tt = tc * 4 + qq
                b = tt % 2
                kx, kg = f"s_xt{b}", f"s_gt{b}"
                p.step("sync", lambda h, b=b, tt=tt: h.dma_start(
                    xt[b][:], in_hbm[tt * 128:(tt + 1) * 128, :]),
                    writes=(kx,), dma=True)
                self._ln_stats(xt[b], kx, self.muS[:, tt:tt + 1], f"{P}mu{tt}",
                               self.rstdS[:, tt:tt + 1], f"{P}rstd{tt}",
                               sq, smallc)
                p.step("vector", lambda h, b=b, tt=tt: h.tensor_scalar(
                    gt[b][:], xt[b][:], self.muS[:, tt:tt + 1],
                    self.rstdS[:, tt:tt + 1], ALU.subtract, ALU.mult),
                    reads=(kx, f"{P}mu{tt}", f"{P}rstd{tt}"), writes=(kg,))
                for kt in range(8):
                    tb = kt % 2
                    p.step("tensor", lambda h, b=b, kt=kt, tb=tb: h.transpose(
                        PT[:, tb * 1024: tb * 1024 + 128],
                        gt[b][:, kt * 128:(kt + 1) * 128], self.c_ident[:]),
                        reads=(kg, "c_ident"), writes=(Tk[tb],))
                    p.step("vector", lambda h, kt=kt, qq=qq, tb=tb: h.tensor_copy(
                        gT[:, kt * 512 + qq * 128: kt * 512 + qq * 128 + 128],
                        PT[:, tb * 1024: tb * 1024 + 128]),
                        reads=(Tk[tb],), writes=(f"s_gT{kt}q{qq}",))
            gkeys = tuple(f"s_gT{kt}q{qq}" for kt in range(8) for qq in range(4))
            for mt in range(24):
                bq = mt % 2
                p.step("tensor", lambda h, mt=mt, bq=bq: self._mmgroup(
                    h, PA[:, bq * 512: bq * 512 + 512],
                    [(cawb[:, kt * 3072 + mt * 128: kt * 3072 + mt * 128 + 128],
                      gT[:, kt * 512:(kt + 1) * 512]) for kt in range(8)]),
                    reads=gkeys + ("cawb",), writes=(A[bq],))
                p.step("scalar", lambda h, mt=mt, bq=bq: h.activation(
                    qkvT[:, mt * 512:(mt + 1) * 512], PA[:, bq * 512: bq * 512 + 512],
                    AF.Identity, bias=self.cabT[:, mt:mt + 1]),
                    reads=(A[bq], "cabT"), writes=(f"s_qkvT{mt}",))
            midq = tuple(f"s_qkvT{mt}" for mt in range(8))
            midk = tuple(f"s_qkvT{mt}" for mt in range(8, 16))
            midv = tuple(f"s_qkvT{mt}" for mt in range(16, 24))
            for which, wsb, wkey, bias, bkey, dst, mids, moff in (
                    ("q", qwb, "s_qwb", self.qb8T, "qb8T", self.qT2, midq, 0),
                    ("k", kwb, "s_kwb", self.kbT, "kbT", self.kT2, midk, 8),
                    ("v", vwb, "s_vwb", self.vbT, "vbT", self.vT2, midv, 16)):
                for mt in range(8):
                    bq = 2 + mt % 2
                    ob = mt % 2
                    p.step("tensor", lambda h, mt=mt, bq=bq, wsb=wsb, moff=moff: self._mmgroup(
                        h, PA[:, bq * 512: bq * 512 + 512],
                        [(wsb[:, kt * C + mt * 128: kt * C + mt * 128 + 128],
                          qkvT[:, (moff + kt) * 512:(moff + kt + 1) * 512])
                         for kt in range(8)]),
                        reads=mids + (wkey,), writes=(A[bq],))
                    p.step("scalar", lambda h, mt=mt, bq=bq, ob=ob, bias=bias: h.activation(
                        ostg[ob][:], PA[:, bq * 512: bq * 512 + 512],
                        AF.Identity, bias=bias[:, mt:mt + 1]),
                        reads=(A[bq], bkey), writes=(f"s_ostg{ob}",))
                    p.step("sync", lambda h, mt=mt, ob=ob, dst=dst, tc=tc: h.dma_start(
                        dst[mt * 128:(mt + 1) * 128, tc * 512:(tc + 1) * 512],
                        ostg[ob][:]),
                        reads=(f"s_ostg{ob}",),
                        writes=(f"{P}{dst.tensor.name}m{mt}c{tc}",), dma=True)
        self.tap(P + "mid", qkvT[:], [f"s_qkvT{mt}" for mt in range(24)])
        self.tap(P + "gT", gT[:], [f"s_gT{kt}q{qq}" for kt in range(8) for qq in range(4)])
        if (P + "qT2") in self.taps:
            self.tap_dram(P + "qT2", self.qT2,
                          [f"{P}qT2m{mt}c{tc}" for mt in range(8) for tc in range(8)])
        if (P + "kT2") in self.taps:
            self.tap_dram(P + "kT2", self.kT2,
                          [f"{P}kT2m{mt}c{tc}" for mt in range(8) for tc in range(8)])
        if (P + "vT2") in self.taps:
            self.tap_dram(P + "vT2", self.vT2,
                          [f"{P}vT2m{mt}c{tc}" for mt in range(8) for tc in range(8)])
        p.barrier()
        st.close()

    def _sparse_s2(self, P):
        p = self.p
        st = ExitStack()
        qhT = self.sb("qhT", [64, T], BF16, stack=st)
        khT = self.sb("khT", [64, T], BF16, stack=st)
        vhT = self.sb("vhT", [64, T], BF16, stack=st)
        qgex = [self.sb(f"qgex{i}", [65, 512], BF16, stack=st) for i in range(2)]
        kgex = [self.sb(f"kgex{i}", [65, 512], BF16, stack=st) for i in range(2)]
        vgx = [self.sb(f"vgx{i}", [128, 4 * 65], BF16, stack=st) for i in range(2)]
        mseg = self.sb("mseg", [128, 56], stack=st)
        m4s = self.sb("m4s", [56, 128], stack=st)
        mgrow = self.sb("mgrow", [1, T], stack=st)
        negm = self.sb("negm", [1, T], BF16, stack=st)
        oT = [self.sb(f"oT{b}", [64, T], BF16, stack=st) for b in range(3)]
        mtf = self.sb("mtf", [1, T], BF16, stack=st)
        lsef = self.sb("lsef", [1, T], stack=st)
        lnl = self.sb("lnl", [1, 512], stack=st)
        pbuf = [self.sb(f"s_pbuf{i}", [128, 512], BF16, stack=st) for i in range(3)]
        PA = self.ps("PA_s2", [128, 2048], stack=st)
        PT = self.ps("PT_s2", [128, 2048], BF16, stack=st)
        A = [f"{P}2PAb{i}" for i in range(4)]
        Tk = [f"{P}2PTb{i}" for i in range(2)]
        mfl = self.dram(P + "mfl", [56, 128])
        self.yT2 = self.nc.dram_tensor(P + "yT2", [C, T], BF16, kind="Internal").ap()
        oTh = [self.nc.dram_tensor(f"{P}oTh{b}", [C, T], BF16, kind="Internal").ap()
               for b in range(3)]
        LSEh = [self.dram(f"{P}LSEh{b}", [H, T]) for b in range(3)]
        MTh = [self.nc.dram_tensor(f"{P}MTh{b}", [H, T], BF16, kind="Internal").ap()
               for b in range(3)]
        Uh = [self.dram(f"{P}Uh{b}", [H, T]) for b in range(3)]

        qkeys = tuple(f"{P}qT2m{mt}c{tc}" for mt in range(8) for tc in range(8))
        kkeys = tuple(f"{P}kT2m{mt}c{tc}" for mt in range(8) for tc in range(8))
        vkeys = tuple(f"{P}vT2m{mt}c{tc}" for mt in range(8) for tc in range(8))

        for i in range(2):
            p.step("vector", lambda h, i=i: h.memset(vgx[i][:], 1.0),
                   writes=(f"vgx_ones{i}",))
            p.step("vector", lambda h, i=i: h.memset(kgex[i][64:65, :], 1.0),
                   writes=(f"kgex1_{i}",))

        BR_ROWS = ((0, 32), (32, 48), (48, 56))
        for hh in range(H):
            segl = []
            for bi, (wseg, r) in enumerate(SEGS):
                off = hh % r
                for sg in range(T // wseg):
                    segl.append((bi, sg, sg * wseg + off, r))
            p.step("sync", lambda h, hh=hh: h.dma_start(
                qhT[:], self.qT2[hh * 64:(hh + 1) * 64, :]),
                reads=qkeys, writes=("qhT",), dma=True, pclass="p64")
            p.step("sync", lambda h, hh=hh: h.dma_start(
                khT[:], self.kT2[hh * 64:(hh + 1) * 64, :]),
                reads=kkeys, writes=("khT",), dma=True, pclass="p64")
            p.step("sync", lambda h, hh=hh: h.dma_start(
                vhT[:], self.vT2[hh * 64:(hh + 1) * 64, :]),
                reads=vkeys, writes=("vhT",), dma=True, pclass="p64")
            mc = 0
            for bi, sg, base, r in segl:
                for ic in range(4):
                    bq = ic % 2
                    p.step("tensor", lambda h, base=base, r=r, ic=ic, bq=bq: h.matmul(
                        PA[:, bq * 512: bq * 512 + 512],
                        qhT[:, base + ic * 128 * r: base + (ic * 128 + 127) * r + 1: r],
                        khT[:, base: base + 511 * r + 1: r], start=True, stop=True),
                        reads=("qhT", "khT"), writes=(A[bq],), tag="sp1mm")
                    dlt = 128 * ic
                    p.step("vector", lambda h, bq=bq, dlt=dlt: h.tensor_tensor(
                        PA[:, bq * 512: bq * 512 + 512],
                        PA[:, bq * 512: bq * 512 + 512],
                        self.c_stair1[:, 384 - dlt: 896 - dlt], ALU.add),
                        reads=(A[bq], "c_stair1"), writes=(A[bq],))
                    p.step("vector", lambda h, bq=bq, mc=mc, ic=ic: h.tensor_reduce(
                        mseg[:, mc + ic: mc + ic + 1],
                        PA[:, bq * 512: bq * 512 + 512], AX.X, ALU.max),
                        reads=(A[bq],), writes=(f"msg{mc + ic}",))
                mc += 4
            mall = tuple(f"msg{i}" for i in range(56))
            p.step("tensor", lambda h: h.transpose(
                PA[0:56, 1024:1152], mseg[:], self.c_identf[:]),
                reads=mall + ("c_identf",), writes=(A[2],), tag="msT")
            p.step("vector", lambda h: h.tensor_copy(m4s[:], PA[0:56, 1024:1152]),
                   reads=(A[2],), writes=("m4s",))
            p.step("sync", lambda h: h.dma_start(mfl, m4s[:]),
                   reads=("m4s",), writes=("mfl",), dma=True, pclass="misc")
            for bi, (wseg, r) in enumerate(SEGS):
                off = hh % r
                nseg = T // wseg
                blen = nseg * 512
                r0, r1 = BR_ROWS[bi]
                p.step("sync", lambda h, r0=r0, r1=r1, blen=blen: h.dma_start(
                    mgrow[:, 0:blen],
                    mfl[r0:r1].rearrange("p n -> () (p n)")),
                    reads=("mfl",), writes=("mgrow",), dma=True, pclass="p1")
                p.step("scalar", lambda h, blen=blen: h.activation(
                    negm[:, 0:blen], mgrow[:, 0:blen], AF.Copy, scale=-1.0),
                    reads=("mgrow",), writes=("negm",))
                p.step("vector", lambda h: h.memset(mtf[:], 1e30), writes=("mtf",))
                p.step("vector", lambda h: h.memset(lsef[:], -1e30), writes=("lsef",))
                if bi > 0:
                    p.step("vector", lambda h, bi=bi: h.memset(oT[bi][:], 0.0),
                           writes=(f"oT{bi}",))
                pend = [None]

                def make_sfinals(sg, base, gbase, bi, r):
                    sb2 = sg % 2
                    po = 2 + sb2
                    poo = 1024 + sb2 * 512

                    def fin():
                        p.step("scalar", lambda h, poo=poo: h.activation(
                            lnl[:], PA[64:65, poo: poo + 512], AF.Ln),
                            reads=(A[po],), writes=("lnl",))
                        p.step("vector", lambda h, base=base, r=r, gbase=gbase: h.tensor_tensor(
                            lsef[:, base: base + 511 * r + 1: r], lnl[:],
                            negm[:, gbase: gbase + 512], ALU.subtract),
                            reads=("lnl", "negm", "lsef"), writes=("lsef",))
                        p.step("vector", lambda h, base=base, r=r, gbase=gbase: h.tensor_copy(
                            mtf[:, base: base + 511 * r + 1: r], negm[:, gbase: gbase + 512]),
                            reads=("negm", "mtf"), writes=("mtf",))
                        p.step("vector", lambda h, base=base, r=r, bi=bi,
                               poo=poo: h.tensor_copy(
                            oT[bi][0:64, base: base + 511 * r + 1: r],
                            PA[0:64, poo: poo + 512]),
                            reads=(A[po], f"oT{bi}"), writes=(f"oT{bi}",))
                    return fin

                for sg in range(nseg):
                    base = sg * wseg + off
                    gbase = sg * 512
                    sb2 = sg % 2
                    po = 2 + sb2
                    poo = 1024 + sb2 * 512
                    kk, kk1 = f"kgex{sb2}", f"kgex1_{sb2}"
                    kv, kvo = f"vgx{sb2}", f"vgx_ones{sb2}"
                    p.step("vector", lambda h, base=base, r=r, sb2=sb2: h.tensor_copy(
                        kgex[sb2][0:64, :], khT[:, base: base + 511 * r + 1: r]),
                        reads=("khT",), writes=(kk,))
                    p.step("vector", lambda h, base=base, r=r, sb2=sb2: h.tensor_copy(
                        qgex[sb2][0:64, :], qhT[:, base: base + 511 * r + 1: r]),
                        reads=("qhT",), writes=(f"qgex{sb2}",))
                    p.step("vector", lambda h, gbase=gbase, sb2=sb2: h.tensor_copy(
                        qgex[sb2][64:65, :], negm[:, gbase: gbase + 512]),
                        reads=("negm",), writes=(f"qgex1x{sb2}",))
                    for jc in range(4):
                        tb = jc % 2
                        p.step("tensor", lambda h, base=base, r=r, jc=jc, tb=tb: h.transpose(
                            PT[:, tb * 1024: tb * 1024 + 64],
                            vhT[:, base + jc * 128 * r: base + (jc * 128 + 127) * r + 1: r],
                            self.c_ident[0:64, 0:64]),
                            reads=("vhT", "c_ident"), writes=(Tk[tb],), tag="vgT")
                        p.step("vector", lambda h, jc=jc, tb=tb, sb2=sb2: h.tensor_copy(
                            vgx[sb2][:, jc * 65: jc * 65 + 64],
                            PT[:, tb * 1024: tb * 1024 + 64]),
                            reads=(Tk[tb], kvo), writes=(kv,))
                    for jc in range(4):
                        bq = jc % 2
                        pb3 = jc % 3
                        p.step("tensor", lambda h, jc=jc, bq=bq, sb2=sb2: h.matmul(
                            PA[:, bq * 512: bq * 512 + 512],
                            kgex[sb2][:, jc * 128:(jc + 1) * 128], qgex[sb2][:],
                            start=True, stop=True),
                            reads=(kk, kk1, f"qgex{sb2}", f"qgex1x{sb2}"),
                            writes=(A[bq],), tag="sp2mm")
                        d2 = 128 * jc
                        p.step("vector", lambda h, bq=bq, d2=d2: h.tensor_tensor(
                            PA[:, bq * 512: bq * 512 + 512],
                            PA[:, bq * 512: bq * 512 + 512],
                            self.c_stairT[:, 384 - d2: 896 - d2], ALU.add),
                            reads=(A[bq], "c_stairT"), writes=(A[bq],))
                        p.step("scalar", lambda h, bq=bq, pb3=pb3: h.activation(
                            pbuf[pb3][:], PA[:, bq * 512: bq * 512 + 512], AF.Exp),
                            reads=(A[bq],), writes=(f"s_pbuf{pb3}",))
                        if jc == 1 and pend[0] is not None:
                            pend[0]()
                            pend[0] = None
                        if jc >= 2:
                            jp = jc - 2
                            p.step("tensor", lambda h, jp=jp, sb2=sb2, poo=poo,
                                   po=po: h.matmul(
                                PA[0:65, poo: poo + 512],
                                vgx[sb2][:, jp * 65:(jp + 1) * 65],
                                pbuf[jp % 3][:], start=(jp == 0), stop=False),
                                reads=(f"s_pbuf{jp % 3}", kv), writes=(A[po],),
                                tag="spv")
                    for jp in (2, 3):
                        p.step("tensor", lambda h, jp=jp, sb2=sb2, poo=poo,
                               po=po: h.matmul(
                            PA[0:65, poo: poo + 512],
                            vgx[sb2][:, jp * 65:(jp + 1) * 65],
                            pbuf[jp % 3][:], start=False, stop=(jp == 3)),
                            reads=(f"s_pbuf{jp % 3}", kv), writes=(A[po],),
                            tag="spv")
                    pend[0] = make_sfinals(sg, base, gbase, bi, r)
                if pend[0] is not None:
                    pend[0]()
                    pend[0] = None
                p.step("sync", lambda h, bi=bi, hh=hh: h.dma_start(
                    LSEh[bi][hh:hh + 1, :], lsef[:]),
                    reads=("lsef",), writes=(f"{P}LSE{bi}h{hh}",), dma=True, pclass="p1")
                p.step("sync", lambda h, bi=bi, hh=hh: h.dma_start(
                    MTh[bi][hh:hh + 1, :], mtf[:]),
                    reads=("mtf",), writes=(f"{P}MT{bi}h{hh}",), dma=True, pclass="p1")
                p.step("sync", lambda h, bi=bi, hh=hh: h.dma_start(
                    oTh[bi][hh * 64:(hh + 1) * 64, :], oT[bi][:]),
                    reads=(f"oT{bi}",), writes=(f"{P}oTh{bi}h{hh}",), dma=True, pclass="p64")
        p.barrier()
        st.close()

        st = ExitStack()
        CW = 2048
        Lc = [self.sb(f"Lc{b}", [H, CW], stack=st) for b in range(3)]
        Mc = [self.sb(f"Mc{b}", [H, CW], BF16, stack=st) for b in range(3)]
        mg = self.sb("mg_m", [H, CW], stack=st)
        rden = self.sb("rden_m", [H, CW], stack=st)
        escr = self.sb("escr_m", [H, CW], stack=st)
        allL = tuple(f"{P}LSE{b}h{hh}" for b in range(3) for hh in range(H))
        allM = tuple(f"{P}MT{b}h{hh}" for b in range(3) for hh in range(H))
        for ch in range(T // CW):
            c0 = ch * CW
            for b in range(3):
                p.step("sync", lambda h, b=b, c0=c0: h.dma_start(
                    Lc[b][:], LSEh[b][:, c0:c0 + CW]),
                    reads=allL, writes=(f"Lc{b}",), dma=True, pclass="misc")
                p.step("sync", lambda h, b=b, c0=c0: h.dma_start(
                    Mc[b][:], MTh[b][:, c0:c0 + CW]),
                    reads=allM, writes=(f"Mc{b}",), dma=True, pclass="misc")
            p.step("vector", lambda h: h.tensor_tensor(
                mg[:], Lc[0][:], Lc[1][:], ALU.max),
                reads=("Lc0", "Lc1"), writes=("mg_m",))
            p.step("vector", lambda h: h.tensor_tensor(
                mg[:], mg[:], Lc[2][:], ALU.max),
                reads=("mg_m", "Lc2"), writes=("mg_m",))
            for b in range(3):
                p.step("vector", lambda h, b=b: h.tensor_tensor(
                    escr[:], Lc[b][:], mg[:], ALU.subtract),
                    reads=(f"Lc{b}", "mg_m"), writes=("escr_m",))
                p.step("scalar", lambda h, b=b: h.activation(
                    Lc[b][:], escr[:], AF.Exp),
                    reads=("escr_m",), writes=(f"Le{b}",))
            p.step("vector", lambda h: h.tensor_tensor(
                rden[:], Lc[0][:], Lc[1][:], ALU.add),
                reads=("Le0", "Le1"), writes=("rden_m",))
            p.step("vector", lambda h: h.tensor_tensor(
                rden[:], rden[:], Lc[2][:], ALU.add),
                reads=("rden_m", "Le2"), writes=("rden_m",))
            p.step("vector", lambda h: h.reciprocal(rden[:], rden[:]),
                   reads=("rden_m",), writes=("rden_m",))
            for b in range(3):
                p.step("vector", lambda h, b=b: h.tensor_tensor(
                    escr[:], Mc[b][:], mg[:], ALU.add),
                    reads=(f"Mc{b}", "mg_m"), writes=("escr_m",))
                p.step("scalar", lambda h, b=b: h.activation(
                    escr[:], escr[:], AF.Exp, scale=-1.0),
                    reads=("escr_m",), writes=("escr_m",))
                p.step("vector", lambda h, b=b: h.tensor_tensor(
                    Lc[b][:], escr[:], rden[:], ALU.mult),
                    reads=("escr_m", "rden_m"), writes=(f"Uc{b}",))
                p.step("sync", lambda h, b=b, c0=c0: h.dma_start(
                    Uh[b][:, c0:c0 + CW], Lc[b][:]),
                    reads=(f"Uc{b}",), writes=(f"{P}U{b}c{c0}",), dma=True, pclass="misc")
        p.barrier()
        st.close()

        st = ExitStack()
        urow = [self.sb(f"urow{b}", [1, T], stack=st) for b in range(3)]
        oTb = [self.sb(f"oTb{b}", [64, T], BF16, stack=st) for b in range(3)]
        ych = self.sb("ych", [64, 512], stack=st)
        ytmp = self.sb("ytmp", [64, 512], stack=st)
        yhT = self.sb("yhT", [64, T], BF16, stack=st)
        PB = self.ps("PB_s2y", [128, 2048], stack=st)
        Bk = [f"{P}yPBb{i}" for i in range(4)]
        ukeys = tuple(f"{P}U{b}c{c0}" for b in range(3) for c0 in range(0, T, CW))
        for hh in range(H):
            for b in range(3):
                p.step("sync", lambda h, b=b, hh=hh: h.dma_start(
                    urow[b][:], Uh[b][hh:hh + 1, :]),
                    reads=ukeys, writes=(f"urow{b}",), dma=True, pclass="p1")
                p.step("sync", lambda h, b=b, hh=hh: h.dma_start(
                    oTb[b][:], oTh[b][hh * 64:(hh + 1) * 64, :]),
                    reads=(f"{P}oTh{b}h{hh}",), writes=(f"oTb{b}",), dma=True, pclass="p64")
            for ch in range(8):
                c0 = ch * 512
                for b in range(3):
                    p.step("tensor", lambda h, b=b, c0=c0: h.matmul(
                        PB[0:64, b * 512:(b + 1) * 512], self.c_ones128f[:, 0:64],
                        urow[b][:, c0:c0 + 512], start=True, stop=True),
                        reads=(f"urow{b}", "c_ones128f"), writes=(Bk[b],), tag="ubc")
                p.step("vector", lambda h, c0=c0: h.tensor_tensor(
                    ych[:], PB[0:64, 0:512], oTb[0][:, c0:c0 + 512], ALU.mult),
                    reads=(Bk[0], "oTb0"), writes=("ych",))
                p.step("vector", lambda h, c0=c0: h.tensor_tensor(
                    ytmp[:], PB[0:64, 512:1024], oTb[1][:, c0:c0 + 512], ALU.mult),
                    reads=(Bk[1], "oTb1"), writes=("ytmp",))
                p.step("vector", lambda h: h.tensor_tensor(
                    ych[:], ych[:], ytmp[:], ALU.add),
                    reads=("ych", "ytmp"), writes=("ych",))
                p.step("vector", lambda h, c0=c0: h.tensor_tensor(
                    ytmp[:], PB[0:64, 1024:1536], oTb[2][:, c0:c0 + 512], ALU.mult),
                    reads=(Bk[2], "oTb2"), writes=("ytmp",))
                p.step("vector", lambda h, c0=c0: h.tensor_tensor(
                    yhT[:, c0:c0 + 512], ych[:], ytmp[:], ALU.add),
                    reads=("ych", "ytmp"), writes=("yhT",))
            p.step("sync", lambda h, hh=hh: h.dma_start(
                self.yT2[hh * 64:(hh + 1) * 64, :], yhT[:]),
                reads=("yhT",), writes=(f"{P}yT2h{hh}",), dma=True, pclass="p64")
        if (P + "yT2") in self.taps:
            self.tap_dram(P + "yT2", self.yT2,
                          [f"{P}yT2h{hh}" for hh in range(H)])
        p.barrier()
        st.close()

    def _sparse_s3(self, in_hbm, out_hbm, P):
        p = self.p
        st = ExitStack()
        owb = self.sb("s3_owb", [128, 8 * C], BF16, stack=st)
        yTc = self.sb("s3_yTc", [128, 8 * 512], BF16, stack=st)
        xt = [self.sb(f"s3_xt{i}", [128, C], stack=st) for i in range(2)]
        ot = [self.sb(f"s3_ot{i}", [128, C], stack=st) for i in range(2)]
        PA = self.ps("PA_s3", [128, 2048], stack=st)
        A = [f"{P}3PAb{i}" for i in range(4)]
        ykeys = tuple(f"{P}yT2h{hh}" for hh in range(H))
        p.step("sync", lambda h: h.dma_start(
            _s3(owb[:], {"f_wq":1024,"f_wk":256,"f_wv":256,"f_wo":1024,"ca_w":3072,"q_w":1024,"k_w":1024,"v_w":1024,"o_w":1024}["o_w"]), _t3(self.wb["o_w"])),
            reads=("o_w_b_done",), writes=("s3_owb",), dma=True)
        for tc in range(NCH):
            p.step("sync", lambda h, tc=tc: h.dma_start(
                _s3(yTc[:], 512), _t3(self.yT2[:, tc * 512:(tc + 1) * 512])),
                reads=ykeys, writes=("s3_yTc",), dma=True)
            for qq in range(4):
                tt = tc * 4 + qq
                b = tt % 2
                ab = 2 * b
                p.step("sync", lambda h, b=b, tt=tt: h.dma_start(
                    xt[b][:], in_hbm[tt * 128:(tt + 1) * 128, :]),
                    writes=(f"s3_xt{b}",), dma=True)
                for n0 in range(2):
                    p.step("tensor", lambda h, qq=qq, n0=n0, ab=ab: self._mmgroup(
                        h, PA[:, (ab + n0) * 512:(ab + n0 + 1) * 512],
                        [(yTc[:, kt * 512 + qq * 128: kt * 512 + qq * 128 + 128],
                          owb[:, kt * C + n0 * 512: kt * C + n0 * 512 + 512])
                         for kt in range(8)] +
                        [(self.c_ones128b[:],
                          self.obrow[:, n0 * 512:(n0 + 1) * 512])]),
                        reads=("s3_yTc", "s3_owb", "c_ones128b", "obrow"),
                        writes=(A[ab + n0],), tag="s3mm")
                p.step("vector", lambda h, b=b, ab=ab: h.tensor_tensor(
                    ot[b][:, 0:512], PA[:, ab * 512:(ab + 1) * 512],
                    xt[b][:, 0:512], ALU.add),
                    reads=(A[ab], f"s3_xt{b}"), writes=(f"s3_ot{b}a",))
                p.step("vector", lambda h, b=b, ab=ab: h.tensor_tensor(
                    ot[b][:, 512:1024], PA[:, (ab + 1) * 512:(ab + 2) * 512],
                    xt[b][:, 512:1024], ALU.add),
                    reads=(A[ab + 1], f"s3_xt{b}"), writes=(f"s3_ot{b}b",))
                p.step("sync", lambda h, b=b, tt=tt: h.dma_start(
                    out_hbm[tt * 128:(tt + 1) * 128, :], ot[b][:]),
                    reads=(f"s3_ot{b}a", f"s3_ot{b}b"),
                    writes=(f"{P}xout{tt}",), dma=True)
        if (P + "xout") in self.taps:
            self.tap_dram(P + "xout", out_hbm, [f"{P}xout{tt}" for tt in range(NT)])
        st.close()


    def _ln_stats(self, xtile, kx, mu_ap, kmu, rstd_ap, krstd, sq, smallc):
        p = self.p
        p.step("vector", lambda h: h.tensor_reduce(
            smallc[:, 0:1], xtile[:], AX.X, ALU.add),
            reads=(kx,), writes=("sm0",))
        p.step("scalar", lambda h: h.activation(
            sq[:], xtile[:], AF.Square, accum_out=smallc[:, 1:2]),
            reads=(kx,), writes=("sm1", "sqA", "sqB"))
        p.step("vector", lambda h: h.tensor_scalar(
            mu_ap, smallc[:, 0:1], 1.0 / C, None, ALU.mult),
            reads=("sm0",), writes=(kmu,))
        p.step("vector", lambda h: h.tensor_scalar(
            smallc[:, 2:3], smallc[:, 1:2], 1.0 / C, None, ALU.mult),
            reads=("sm1",), writes=("sm2",))
        p.step("vector", lambda h: h.tensor_tensor(
            smallc[:, 3:4], mu_ap, mu_ap, ALU.mult),
            reads=(kmu,), writes=("sm3",))
        p.step("vector", lambda h: h.tensor_tensor(
            smallc[:, 4:5], smallc[:, 2:3], smallc[:, 3:4], ALU.subtract),
            reads=("sm2", "sm3"), writes=("sm4",))
        p.step("scalar", lambda h: h.activation(
            smallc[:, 5:6], smallc[:, 4:5], AF.Sqrt, bias=self.epscol5[:]),
            reads=("sm4", "epscol5"), writes=("sm5",))
        p.step("vector", lambda h: h.reciprocal(rstd_ap, smallc[:, 5:6]),
            reads=("sm5",), writes=(krstd,))



def _in_maps(inputs, consts):
    m = {
        "ca_w": inputs["L0_attn_c_attn_w"], "ca_b": inputs["L0_attn_c_attn_b"],
        "q_w": inputs["L0_attn_q_w"], "q_b": inputs["L0_attn_q_b"],
        "k_w": inputs["L0_attn_k_w"], "k_b": inputs["L0_attn_k_b"],
        "v_w": inputs["L0_attn_v_w"], "v_b": inputs["L0_attn_v_b"],
        "o_w": inputs["L0_attn_o_w"], "o_b": inputs["L0_attn_o_b"],
        "f_n1": inputs["L0_ffn_norm1_w"], "f_wq": inputs["L0_ffn_Wq"],
        "f_wk": inputs["L0_ffn_Wk"], "f_wv": inputs["L0_ffn_Wv"],
        "f_wo": inputs["L0_ffn_Wo"], "f_n2": inputs["L0_ffn_norm2_w"],
        "f_rw": inputs["L0_ffn_router_w"], "f_rb": inputs["L0_ffn_router_b"],
        "f_w1": inputs["L0_ffn_W1"], "f_b1": inputs["L0_ffn_b1"],
        "f_w2": inputs["L0_ffn_W2"], "f_b2": inputs["L0_ffn_b2"],
        "f_w3": inputs["L0_ffn_W3"], "f_b3": inputs["L0_ffn_b3"],
    }
    for k, v in consts.items():
        m["c_" + k] = v
    x = np.ascontiguousarray(inputs["L0_attn_x"], dtype=np.float32)
    maps = []
    for b in range(8):
        d = {k: np.ascontiguousarray(v) for k, v in m.items()}
        d["x"] = np.ascontiguousarray(x[b])
        maps.append(d)
    return maps


def kernel(**inputs):
    consts = host_constants()
    kb = KB()
    nc = kb.build()
    maps = _in_maps(inputs, consts)
    res = run_bass_kernel_spmd(nc, maps, core_ids=list(range(8)))
    return np.stack([res.results[i]["y"] for i in range(8)], axis=0)


# version 05 (final)
# speedup vs baseline: 1.1723x; 1.1705x over previous
import numpy as np
import ml_dtypes
from collections import defaultdict
from contextlib import ExitStack

import concourse.bass as bass
import concourse.mybir as mybir
from concourse.bass_utils import run_bass_kernel_spmd

F32 = mybir.dt.float32
BF16 = mybir.dt.bfloat16
AF = mybir.ActivationFunctionType
ALU = mybir.AluOpType
AX = mybir.AxisListType

ENGINES = ("tensor", "vector", "scalar", "gpsimd", "sync")
NDMALANES = 8
DMALANES = tuple(f"dma{i}" for i in range(NDMALANES))
ALLSEMS = ENGINES + DMALANES


def _t3(ap, p=128):
    return ap.rearrange("(kt p) m -> p kt m", p=p)


def _s3(ap, inner):
    return ap.rearrange("p (kt m) -> p kt m", m=inner)


T = 4096
C = 1024
NT = 32
NCH = 8
H = 16
G = 4
D = 64
E = 8
FF = 4096
TBLK = 1024
NBLK = T // TBLK
SEGS = ((512, 1), (1024, 2), (2048, 4))



class _Tok:
    __slots__ = ("eng", "cnt", "clock")

    def __init__(self, eng, cnt, clock):
        self.eng, self.cnt, self.clock = eng, cnt, clock


class Prog:
    def __init__(self, nc):
        self.nc = nc
        self.ops = {e: [] for e in ENGINES}
        self.cnt = {e: 0 for e in ALLSEMS}
        self.clock = {e: {f: 0 for f in ALLSEMS} for e in ENGINES}
        self.writes = {}
        self.readers = defaultdict(list)
        self.sems = None
        self._lane_rr = 0

    def step(self, eng, fn, reads=(), writes=(), dma=False, tag=None,
             pclass="full"):
        chain = None
        if dma:
            if pclass == "full":
                lane = DMALANES[self._lane_rr % 5]
                self._lane_rr += 1
            elif pclass == "p64":
                lane = DMALANES[5]
            elif pclass == "p1":
                lane = DMALANES[6]
            else:
                lane = DMALANES[7]
                chain = self.cnt[lane]
            amt = 16
        else:
            lane = eng
            amt = 1
        my = self.clock[eng]
        need = {}
        toks = []
        for k in reads:
            t = self.writes.get(k)
            if t is not None:
                toks.append(t)
        for k in writes:
            t = self.writes.get(k)
            if t is not None:
                toks.append(t)
            toks.extend(self.readers.get(k, ()))
        for t in toks:
            if my.get(t.eng, 0) < t.cnt and need.get(t.eng, 0) < t.cnt:
                need[t.eng] = t.cnt
        waits = sorted(need.items())
        for e2, c2 in waits:
            my[e2] = c2
        for t in toks:
            if my.get(t.eng, 0) >= t.cnt:
                for e2, c2 in t.clock.items():
                    if my[e2] < c2:
                        my[e2] = c2
        self.cnt[lane] += amt
        mytok = _Tok(lane, self.cnt[lane], dict(my))
        mytok.clock[lane] = self.cnt[lane]
        for k in writes:
            self.writes[k] = mytok
            self.readers[k] = []
        for k in reads:
            self.readers[k].append(mytok)

        def closure(h, _waits=waits, _fn=fn, _amt=amt, _lane=lane, _tag=tag,
                    _chain=chain):
            for e2, c2 in _waits:
                h.wait_ge(self.sems[e2], c2)
            if _chain:
                h.wait_ge(self.sems[_lane], _chain)
            last = _fn(h)
            if last is None:
                raise RuntimeError(f"step returned None ({_lane}, {_tag})")
            last.then_inc(self.sems[_lane], _amt)

        self.ops[eng].append(closure)
        return mytok

    def barrier(self):
        toks = {e: self.cnt[e] for e in ALLSEMS}
        for e in ENGINES:
            my = self.clock[e]
            for e2, c in toks.items():
                if e2 != e and my[e2] < c and c > 0:
                    def w(h, _e2=e2, _c=c):
                        h.wait_ge(self.sems[_e2], _c)
                    self.ops[e].append(w)
                    my[e2] = c

    def final_wait(self):
        for ln in DMALANES:
            c = self.cnt[ln]
            if c > 0:
                def w(h, _c=c, _ln=ln):
                    h.wait_ge(self.sems[_ln], _c)
                self.ops["sync"].append(w)

    def build(self, block, sems):
        self.sems = sems
        for e in ENGINES:
            if not self.ops[e]:
                continue
            lst = self.ops[e]

            def runner(h, _lst=lst):
                for fn in _lst:
                    fn(h)

            getattr(block, e)(runner)



def host_constants():
    HALF = 512
    theta = 1.0 / 10000.0 ** (np.arange(HALF, dtype=np.float64) / HALF)
    ang = np.arange(T, dtype=np.float64)[:, None] * theta[None, :]
    cos = np.cos(ang).astype(np.float32)
    sin = np.sin(ang).astype(np.float32)
    u = np.arange(896)[None, :]
    p = np.arange(128)[:, None]
    stair1 = np.where(u > p + 384, np.float32(-8e9), np.float32(0.0)).astype(np.float32)
    stairT = np.where(u < p + 384, np.float32(-8e9), np.float32(0.0)).astype(np.float32)
    ident = np.eye(128, dtype=np.float32).astype(ml_dtypes.bfloat16)
    identf = np.eye(128, dtype=np.float32)
    iota8 = np.broadcast_to(np.arange(8, dtype=np.float32), (128, 8)).copy()
    iotam99 = (iota8 - 99.0).astype(np.float32)
    ones128f = np.ones((1, 128), np.float32)
    ones128b = np.ones((1, 128), np.float32).astype(ml_dtypes.bfloat16)
    return dict(cos=cos, sin=sin, stair1=stair1, stairT=stairT, ident=ident,
                identf=identf, iota8=iota8, iotam99=iotam99, ones128f=ones128f,
                ones128b=ones128b)



class KB:
    def __init__(self, taps=(), phases=("prologue", "ffn1", "sparse", "ffn2"),
                 stop_after=None):
        self.nc = bass.Bass("TRN2", target_bir_lowering=False)
        self.p = Prog(self.nc)
        self.es = ExitStack()
        self.taps = set(taps)
        self.phases = phases
        self.stop_after = stop_after
        self._names = {}

    def dram(self, name, shape, dt=F32, kind="Internal"):
        return self.nc.dram_tensor(name, shape, dt, kind=kind).ap()

    def _uniq(self, name):
        n = self._names.get(name, 0)
        self._names[name] = n + 1
        return name if n == 0 else f"{name}__{n}"

    def sb(self, name, shape, dt=F32, stack=None):
        return (stack or self.es).enter_context(
            self.nc.sbuf_tensor(self._uniq(name), shape, dt))

    def ps(self, name, shape, dt=F32, stack=None):
        return (stack or self.es).enter_context(
            self.nc.psum_tensor(self._uniq(name), shape, dt))

    def tap(self, name, ap_sbuf, reads):
        if name not in self.taps:
            return
        shape = list(ap_sbuf.shape)
        hbm = self.nc.dram_tensor("tap_" + name, shape, ap_sbuf.tensor.dtype,
                                  kind="ExternalOutput").ap()
        self.p.step("sync", lambda h: h.dma_start(hbm, ap_sbuf),
                    reads=tuple(reads), writes=("tap_" + name,), dma=True, pclass="misc")

    def tap_dram(self, name, dram_ap, reads):
        if name not in self.taps:
            return
        shape = list(dram_ap.shape)
        hbm = self.nc.dram_tensor("tap_" + name, shape, dram_ap.tensor.dtype,
                                  kind="ExternalOutput").ap()
        st = ExitStack()
        rows, cols = shape
        tbuf = self.sb(f"tapb_{name}", [128, cols], dram_ap.tensor.dtype, stack=st)
        for r0 in range(0, rows, 128):
            rr = min(128, rows - r0)
            self.p.step("sync", lambda h, r0=r0, rr=rr: h.dma_start(
                tbuf[0:rr, :], dram_ap[r0:r0 + rr, :]),
                reads=tuple(reads), writes=(f"tapb_{name}",), dma=True, pclass="misc")
            self.p.step("sync", lambda h, r0=r0, rr=rr: h.dma_start(
                hbm[r0:r0 + rr, :], tbuf[0:rr, :]),
                reads=(f"tapb_{name}",), writes=("tap_" + name,), dma=True, pclass="misc")
        st.close()

    @staticmethod
    def _mmgroup(h, out_ap, pairs, start=True, stop=True):
        last = None
        n = len(pairs)
        for i, (l, r) in enumerate(pairs):
            last = h.matmul(out_ap, l, r, start=(start and i == 0),
                            stop=(stop and i == n - 1))
        return last

    def build(self):
        nc = self.nc
        self.x_in = self.dram("x", [T, C], kind="ExternalInput")
        shapes = {"ca_w": [C, 3 * C], "ca_b": [3 * C], "q_w": [C, C], "q_b": [C],
                  "k_w": [C, C], "k_b": [C], "v_w": [C, C], "v_b": [C],
                  "o_w": [C, C], "o_b": [C], "f_n1": [C], "f_wq": [C, C],
                  "f_wk": [C, 256], "f_wv": [C, 256], "f_wo": [C, C],
                  "f_n2": [C], "f_rw": [C, E], "f_rb": [E],
                  "f_w1": [E, C, FF], "f_b1": [E, FF], "f_w2": [E, C, FF],
                  "f_b2": [E, FF], "f_w3": [E, FF, C], "f_b3": [E, C]}
        self.w = {n: self.dram(n, s, kind="ExternalInput") for n, s in shapes.items()}
        cshapes = {"cos": [T, 512], "sin": [T, 512], "stair1": [128, 896],
                   "stairT": [128, 896], "iota8": [128, 8], "iotam99": [128, 8],
                   "ones128f": [1, 128], "identf": [128, 128]}
        self.cst = {n: self.dram("c_" + n, s, kind="ExternalInput")
                    for n, s in cshapes.items()}
        self.cst["ident"] = self.nc.dram_tensor("c_ident", [128, 128], BF16,
                                                kind="ExternalInput").ap()
        self.cst["ones128b"] = self.nc.dram_tensor("c_ones128b", [1, 128], BF16,
                                                   kind="ExternalInput").ap()
        self.y_out = self.nc.dram_tensor("y", [T, C], F32, kind="ExternalOutput").ap()
        self.xs1 = self.dram("xs1", [T, C])
        self.xs2 = self.dram("xs2", [T, C])
        self.cinit = self.dram("cinit", [T, C])
        self.bT_hbm = self.nc.dram_tensor("bT_hbm", [C, T], BF16, kind="Internal").ap()
        self.mflat = self.dram("mflat", [32, 128])
        self.mflatb = self.nc.dram_tensor("mflatb", [32, 128], BF16, kind="Internal").ap()
        self.wb = {}
        for n in ["f_wq", "f_wk", "f_wv", "f_wo", "f_w1", "f_w2", "f_w3",
                  "ca_w", "q_w", "k_w", "v_w", "o_w"]:
            self.wb[n] = self.nc.dram_tensor(n + "_b", shapes[n], BF16,
                                             kind="Internal").ap()
        self.qT2 = self.nc.dram_tensor("qT2", [C, T], BF16, kind="Internal").ap()
        self.kT2 = self.nc.dram_tensor("kT2", [C, T], BF16, kind="Internal").ap()
        self.vT2 = self.nc.dram_tensor("vT2", [C, T], BF16, kind="Internal").ap()

        self.c_stair1 = self.sb("c_stair1s", [128, 896])
        self.c_stairT = self.sb("c_stairTs", [128, 896])
        self.c_ident = self.sb("c_idents", [128, 128], BF16)
        self.c_identf = self.sb("c_identfs", [128, 128])
        self.c_iota8 = self.sb("c_iota8s", [128, 8])
        self.c_iotam99 = self.sb("c_iotam99s", [128, 8])
        self.c_ones128f = self.sb("c_ones128fs", [1, 128])
        self.c_ones128b = self.sb("c_ones128bs", [1, 128], BF16)
        self.epscol5 = self.sb("epscol5", [128, 1])
        self.epscol6 = self.sb("epscol6", [128, 1])
        self.normw1 = self.sb("normw1", [128, C])
        self.normw2 = self.sb("normw2", [128, C])
        self.muS = self.sb("muS", [128, NT])
        self.rstdS = self.sb("rstdS", [128, NT])
        self.comb = self.sb("comb", [128, NT * E])
        self.routw = self.sb("routw", [128, E * 8])
        self.routwh = self.sb("routwh", [128, E * 8], BF16)
        self.routwl = self.sb("routwl", [128, E * 8], BF16)
        self.routbbc = self.sb("routbbc", [128, E])
        self.b1sb = self.sb("b1sb", [128, E * (FF // 128)])
        self.b2sb = self.sb("b2sb", [128, E * (FF // 128)])
        self.b3cat = self.sb("b3cat", [1, E * C], BF16)
        self.obrow = self.sb("obrow", [1, C], BF16)
        self.cabT = self.sb("cabT", [128, 24])
        self.qb8T = self.sb("qb8T", [128, 8])
        self.kbT = self.sb("kbT", [128, 8])
        self.vbT = self.sb("vbT", [128, 8])

        self._load_persistent()
        self.cvt_stack = None
        if "prologue" in self.phases:
            self.cvt_stack = ExitStack()
            self.cvt_stage = [self.sb(f"cvt_st{i}", [128, 2048], stack=self.cvt_stack)
                              for i in range(3)]
            self.cvt_out = [self.sb(f"cvt_out{i}", [128, 2048], BF16, stack=self.cvt_stack)
                            for i in range(3)]
            self._convert_weights("qkv")
        if "ffn1" in self.phases:
            self._ffn_call(self.x_in, self.xs1, "A")
        elif "ffn1f1" in self.phases:
            self.stop_after = "f1"
            self._ffn_f12(self.x_in, "A")
            self.p.barrier()
            self.f12_stack.close()
        elif "ffn1f2c" in self.phases:
            self.stop_after = "f2c"
            self._ffn_f12(self.x_in, "A")
            self.p.barrier()
            self.f12_stack.close()
        elif "ffn1f2d" in self.phases:
            self.stop_after = "f2d"
            self._ffn_f12(self.x_in, "A")
            self.p.barrier()
            self.f12_stack.close()
        elif "ffn1f2" in self.phases:
            self.stop_after = "f2"
            self._ffn_f12(self.x_in, "A")
            self.p.barrier()
            self.f12_stack.close()
        elif "ffn1f12" in self.phases:
            self._ffn_f12(self.x_in, "A")
            self.p.barrier()
            self.f12_stack.close()
        if "sparse" in self.phases:
            self._sparse_slot(self.xs1, self.xs2, "S")
        elif "sparse_s1" in self.phases:
            self._sparse_s1(self.xs1, "S")
            self.p.barrier()
        if "ffn2" in self.phases:
            self._ffn_call(self.xs2, self.y_out, "B")
        if self.cvt_stack is not None:
            self.p.barrier()
            self.cvt_stack.close()
            self.cvt_stack = None
        self.p.final_wait()

        sems = {e: self.es.enter_context(self.nc.semaphore("sem_" + e))
                for e in ALLSEMS}
        block = self.es.enter_context(self.nc.Block())
        self.p.build(block, sems)
        self.es.close()
        return self.nc

    def _load_persistent(self):
        p, w, cst = self.p, self.w, self.cst

        def ld(dst, src, key, reads=(), pc="full"):
            p.step("sync", lambda h: h.dma_start(dst, src), reads=reads,
                   writes=(key,), dma=True, pclass=pc)

        def ldT(dst, src, key, reads=()):
            p.step("sync", lambda h: h.dma_start_transpose(dst, src), reads=reads,
                   writes=(key,), dma=True, pclass="misc")

        ld(self.c_stair1[:], cst["stair1"], "c_stair1")
        ld(self.c_stairT[:], cst["stairT"], "c_stairT")
        ld(self.c_ident[:], cst["ident"], "c_ident")
        ld(self.c_identf[:], cst["identf"], "c_identf")
        ld(self.c_iota8[:], cst["iota8"], "c_iota8")
        ld(self.c_iotam99[:], cst["iotam99"], "c_iotam99")
        ld(self.c_ones128f[:], cst["ones128f"], "c_ones128f", pc="p1")
        ld(self.c_ones128b[:], cst["ones128b"], "c_ones128b", pc="p1")
        p.step("vector", lambda h: h.memset(self.epscol5[:], 1e-5), writes=("epscol5",))
        p.step("vector", lambda h: h.memset(self.epscol6[:], 1e-6), writes=("epscol6",))
        ld(self.normw1[:], w["f_n1"].rearrange("c -> () c").broadcast_to((128, C)), "normw1")
        ld(self.normw2[:], w["f_n2"].rearrange("c -> () c").broadcast_to((128, C)), "normw2")
        ld(_s3(self.routw[:], 8), _t3(w["f_rw"]), "routw")
        ld(self.routbbc[:], w["f_rb"].rearrange("e -> () e").broadcast_to((128, E)), "routbbc")
        p.step("vector", lambda h: h.tensor_copy(self.routwh[:], self.routw[:]),
               reads=("routw",), writes=("routwh",))
        p.step("vector", lambda h: h.tensor_tensor(
            self.routwl[:], self.routw[:], self.routwh[:], ALU.subtract),
            reads=("routw", "routwh"), writes=("routwl",))
        self.tap("routwh", self.routwh[:], ["routwh"])
        self.tap("routwf", self.routw[:], ["routw"])
        ptmp = ExitStack()
        b1nat = self.sb("b1nat", [8, FF], stack=ptmp)
        b2nat = self.sb("b2nat", [8, FF], stack=ptmp)
        ptp = self.ps("ptp_tmp", [128, 512], stack=ptmp)
        ld(b1nat[:], w["f_b1"], "b1nat", pc="misc")
        ld(b2nat[:], w["f_b2"], "b2nat", pc="misc")
        for nat, knat, dst, kdst in ((b1nat, "b1nat", self.b1sb, "b1sb"),
                                     (b2nat, "b2nat", self.b2sb, "b2sb")):
            for ft in range(32):
                tb = ft % 2
                p.step("tensor", lambda h, nat=nat, ft=ft, tb=tb: h.transpose(
                    ptp[:, tb * 128: tb * 128 + 8], nat[:, ft * 128:(ft + 1) * 128],
                    self.c_identf[0:8, 0:8]),
                    reads=(knat, "c_identf"), writes=("ptp",))
                p.step("vector", lambda h, dst=dst, ft=ft, tb=tb: h.tensor_copy(
                    dst[:, ft: 224 + ft + 1: 32], ptp[:, tb * 128: tb * 128 + 8]),
                    reads=("ptp",), writes=(kdst + str(ft),))
        b3f = self.sb("b3f_tmp", [1, E * C], stack=ptmp)
        ld(b3f[:], w["f_b3"].rearrange("e c -> () (e c)"), "b3f_tmp", pc="misc")
        p.step("vector", lambda h: h.tensor_copy(self.b3cat[:], b3f[:]),
               reads=("b3f_tmp",), writes=("b3cat",))
        obf = self.sb("obf_tmp", [1, C], stack=ptmp)
        ld(obf[:], w["o_b"].rearrange("c -> () c"), "obf_tmp", pc="misc")
        p.step("vector", lambda h: h.tensor_copy(self.obrow[:], obf[:]),
               reads=("obf_tmp",), writes=("obrow",))
        cabnat = self.sb("cabnat", [24, 128], stack=ptmp)
        qbnat = self.sb("qbnat", [8, 128], stack=ptmp)
        kbnat = self.sb("kbnat", [8, 128], stack=ptmp)
        vbnat = self.sb("vbnat", [8, 128], stack=ptmp)
        ld(cabnat[:], w["ca_b"].rearrange("(mt p) -> mt p", p=128), "cabnat", pc="misc")
        ld(qbnat[:], w["q_b"].rearrange("(mt p) -> mt p", p=128), "qbnat", pc="misc")
        ld(kbnat[:], w["k_b"].rearrange("(mt p) -> mt p", p=128), "kbnat", pc="misc")
        ld(vbnat[:], w["v_b"].rearrange("(mt p) -> mt p", p=128), "vbnat", pc="misc")
        p.step("tensor", lambda h: h.transpose(
            ptp[:, 256:280], cabnat[:], self.c_identf[0:24, 0:24]),
            reads=("cabnat", "c_identf"), writes=("ptp",))
        p.step("vector", lambda h: h.tensor_copy(self.cabT[:], ptp[:, 256:280]),
               reads=("ptp",), writes=("cabT",))
        p.step("tensor", lambda h: h.transpose(
            ptp[:, 280:288], qbnat[:], self.c_identf[0:8, 0:8]),
            reads=("qbnat", "c_identf"), writes=("ptp",))
        p.step("vector", lambda h: h.tensor_scalar(
            self.qb8T[:], ptp[:, 280:288], 0.125, None, ALU.mult),
            reads=("ptp",), writes=("qb8T",))
        p.step("tensor", lambda h: h.transpose(
            ptp[:, 288:296], kbnat[:], self.c_identf[0:8, 0:8]),
            reads=("kbnat", "c_identf"), writes=("ptp",))
        p.step("vector", lambda h: h.tensor_copy(self.kbT[:], ptp[:, 288:296]),
               reads=("ptp",), writes=("kbT",))
        p.step("tensor", lambda h: h.transpose(
            ptp[:, 296:304], vbnat[:], self.c_identf[0:8, 0:8]),
            reads=("vbnat", "c_identf"), writes=("ptp",))
        p.step("vector", lambda h: h.tensor_copy(self.vbT[:], ptp[:, 296:304]),
               reads=("ptp",), writes=("vbT",))
        p.barrier()
        ptmp.close()

    def _convert_weights(self, part):
        g = self._convert_weights_gen(part)
        for _ in g:
            pass

    def _convert_weights_gen(self, part):
        p = self.p
        stage, out = self.cvt_stage, self.cvt_out
        jobs = []

        def add(src, dst, rows, cols, scale=None):
            for r in range(rows // 128):
                c0 = 0
                while c0 < cols:
                    cw = min(2048, cols - c0)
                    jobs.append((src[r * 128:(r + 1) * 128, c0:c0 + cw],
                                 dst[r * 128:(r + 1) * 128, c0:c0 + cw], cw, scale))
                    c0 += cw

        if part == "qkv":
            for n in ["f_wq", "f_wk", "f_wv"]:
                sh = self.w[n].shape
                add(self.w[n], self.wb[n], sh[0], sh[1])
        else:
            for n in ["f_wo", "ca_w", "k_w", "v_w", "o_w"]:
                sh = self.w[n].shape
                add(self.w[n], self.wb[n], sh[0], sh[1])
            add(self.w["q_w"], self.wb["q_w"], C, C, scale=0.125)
            for n in ["f_w1", "f_w2", "f_w3"]:
                for e in range(E):
                    sh = self.w[n].shape
                    add(self.w[n][e], self.wb[n][e], sh[1], sh[2])
        for i, (src, dst, cw, scale) in enumerate(jobs):
            sl = i % 3
            kst, kout = f"cvt_st{sl}", f"cvt_out{sl}"
            p.step("sync", lambda h, s=stage[sl], a=src, c=cw: h.dma_start(s[:, 0:c], a),
                   writes=(kst,), dma=True, tag="cvtin")
            if scale is None:
                p.step("gpsimd", lambda h, s=stage[sl], o=out[sl], c=cw: h.tensor_copy(
                    o[:, 0:c], s[:, 0:c]),
                    reads=(kst,), writes=(kout,), tag="cvtcast")
            else:
                p.step("gpsimd", lambda h, s=stage[sl], o=out[sl], c=cw, sc=scale: h.tensor_scalar(
                    o[:, 0:c], s[:, 0:c], sc, None, ALU.mult),
                    reads=(kst,), writes=(kout,), tag="cvtscale")
            p.step("sync", lambda h, o=out[sl], d=dst, c=cw: h.dma_start(d, o[:, 0:c]),
                   reads=(kout,), writes=(dst.tensor.name + "_done",), dma=True, tag="cvtout")
            yield

    def _ffn_call(self, x_hbm, out_hbm, pref):
        self._ffn_f12(x_hbm, pref)
        self.p.barrier()
        self.f12_stack.close()
        if self.cvt_stack is not None:
            self.cvt_stack.close()
            self.cvt_stack = None
        self._ffn_f3(out_hbm, pref)
        self.p.barrier()

    def _ffn_f12(self, x_hbm, pref):
        p = self.p
        P = pref
        st = self.f12_stack = ExitStack()
        KT2 = [self.sb(P + f"KT2_{i}", [65, T], BF16, stack=st) for i in range(4)]
        Vx = self.sb(P + "Vx", [128, NT * 260], BF16, stack=st)
        QTh = self.nc.dram_tensor(P + "QTh", [C, T], BF16, kind="Internal").ap()
        OCh = self.nc.dram_tensor(P + "OCh", [C, T], BF16, kind="Internal").ap()
        PA = self.ps(P + "PA_f12", [128, 2048], stack=st)
        PB = self.ps(P + "PB_f12", [128, 1024], stack=st)
        A = [f"{P}PAb{i}" for i in range(4)]
        Bk = [f"{P}PBb{i}" for i in range(2)]
        Tk = [f"{P}PTb{i}" for i in range(2)]

        ph = ExitStack()
        PT = self.ps(P + "PT_f1", [128, 2048], BF16, stack=ph)
        wqb = self.sb("wqb", [128, 8 * C], BF16, stack=ph)
        wkb = self.sb("wkb", [128, 8 * 256], BF16, stack=ph)
        wvb = self.sb("wvb", [128, 8 * 256], BF16, stack=ph)
        xt = [self.sb(f"xt{i}", [128, C], stack=ph) for i in range(2)]
        csb = [self.sb(f"csb{i}", [128, 512], stack=ph) for i in range(2)]
        snb = [self.sb(f"snb{i}", [128, 512], stack=ph) for i in range(2)]
        h0t = [self.sb(f"h0t{i}", [128, C], stack=ph) for i in range(2)]
        at = [self.sb(f"at{i}", [128, C], stack=ph) for i in range(2)]
        xrb = [self.sb(f"xrb{i}", [128, C], BF16, stack=ph) for i in range(2)]
        sq = self.sb("sq_scr", [128, C], stack=ph)
        rs = self.sb("rs_scr", [128, C], stack=ph)
        smallc = self.sb("smallc", [128, 8], stack=ph)
        xrT = self.sb("xrT", [128, 8 * 512], BF16, stack=ph)
        qstg = [self.sb(f"qstg{i}", [128, 512], BF16, stack=ph) for i in range(2)]

        p.step("sync", lambda h: h.dma_start(
            _s3(wqb[:], 1024), _t3(self.wb["f_wq"])),
            reads=("f_wq_b_done",), writes=("wqb",), dma=True)
        p.step("sync", lambda h: h.dma_start(
            _s3(wkb[:], 256), _t3(self.wb["f_wk"])),
            reads=("f_wk_b_done",), writes=("wkb",), dma=True)
        p.step("sync", lambda h: h.dma_start(
            _s3(wvb[:], 256), _t3(self.wb["f_wv"])),
            reads=("f_wv_b_done",), writes=("wvb",), dma=True)
        p.step("vector", lambda h: h.memset(Vx[:], 1.0), writes=(f"{P}Vxones",))
        for g in range(G):
            p.step("vector", lambda h, g=g: h.memset(KT2[g][64:65, :], 1.0),
                   writes=(f"{P}KTones{g}",))

        for tc in range(NCH):
            for qq in range(4):
                tt = tc * 4 + qq
                b = tt % 2
                kx, kcs = f"xt{b}", f"cs{b}"
                kh0, ka, kxr = f"h0t{b}", f"at{b}", f"xrb{b}"
                p.step("sync", lambda h, b=b, tt=tt: h.dma_start(
                    xt[b][:], x_hbm[tt * 128:(tt + 1) * 128, :]),
                    writes=(kx,), dma=True, tag=f"{P}x{tt}")
                p.step("sync", lambda h, b=b, tt=tt: h.dma_start(
                    csb[b][:], self.cst["cos"][tt * 128:(tt + 1) * 128, :]),
                    writes=(kcs + "c",), dma=True)
                p.step("sync", lambda h, b=b, tt=tt: h.dma_start(
                    snb[b][:], self.cst["sin"][tt * 128:(tt + 1) * 128, :]),
                    writes=(kcs + "s",), dma=True)
                self._ln_stats(xt[b], kx, self.muS[:, tt:tt + 1], f"{P}mu{tt}",
                               self.rstdS[:, tt:tt + 1], f"{P}rstd{tt}",
                               sq, smallc)
                p.step("vector", lambda h, b=b, tt=tt: h.tensor_scalar(
                    h0t[b][:], xt[b][:], self.muS[:, tt:tt + 1],
                    self.rstdS[:, tt:tt + 1], ALU.subtract, ALU.mult),
                    reads=(kx, f"{P}mu{tt}", f"{P}rstd{tt}"), writes=(kh0,))
                p.step("scalar", lambda h, b=b: h.activation(
                    sq[:], h0t[b][:], AF.Square, accum_out=smallc[:, 6:7]),
                    reads=(kh0,), writes=("sm6", "sqA", "sqB"))
                p.step("vector", lambda h: h.tensor_scalar(
                    smallc[:, 7:8], smallc[:, 6:7], 1.0 / C, None, ALU.mult),
                    reads=("sm6",), writes=("sm7",))
                p.step("scalar", lambda h: h.activation(
                    smallc[:, 0:1], smallc[:, 7:8], AF.Sqrt, bias=self.epscol6[:]),
                    reads=("sm7", "epscol6"), writes=("sm0",))
                p.step("vector", lambda h: h.reciprocal(smallc[:, 1:2], smallc[:, 0:1]),
                    reads=("sm0",), writes=("sm1",))
                p.step("vector", lambda h, b=b: h.scalar_tensor_tensor(
                    at[b][:], h0t[b][:], smallc[:, 1:2], self.normw1[:],
                    ALU.mult, ALU.mult),
                    reads=(kh0, "sm1", "normw1"), writes=(ka,))
                p.step("vector", lambda h, b=b: h.tensor_tensor(
                    sq[:, 0:512], at[b][:, 0::2], csb[b][:], ALU.mult),
                    reads=(ka, kcs + "c"), writes=("sqA",))
                p.step("vector", lambda h, b=b: h.tensor_tensor(
                    sq[:, 512:1024], at[b][:, 1::2], snb[b][:], ALU.mult),
                    reads=(ka, kcs + "s"), writes=("sqB",))
                p.step("vector", lambda h, b=b: h.tensor_tensor(
                    xrb[b][:, 0:512], sq[:, 0:512], sq[:, 512:1024], ALU.subtract),
                    reads=("sqA", "sqB"), writes=(kxr + "a",))
                p.step("vector", lambda h, b=b: h.tensor_tensor(
                    rs[:, 0:512], at[b][:, 0::2], snb[b][:], ALU.mult),
                    reads=(ka, kcs + "s"), writes=("rsA",))
                p.step("vector", lambda h, b=b: h.tensor_tensor(
                    rs[:, 512:1024], at[b][:, 1::2], csb[b][:], ALU.mult),
                    reads=(ka, kcs + "c"), writes=("rsB",))
                p.step("vector", lambda h, b=b: h.tensor_tensor(
                    xrb[b][:, 512:1024], rs[:, 0:512], rs[:, 512:1024], ALU.add),
                    reads=("rsA", "rsB"), writes=(kxr + "b",))
                for kt in range(8):
                    tb = kt % 2
                    p.step("tensor", lambda h, b=b, kt=kt, tb=tb: h.transpose(
                        PT[:, tb * 1024: tb * 1024 + 128],
                        xrb[b][:, kt * 128:(kt + 1) * 128], self.c_ident[:]),
                        reads=(kxr + "a", kxr + "b", "c_ident"), writes=(Tk[tb],))
                    p.step("scalar", lambda h, kt=kt, qq=qq, tb=tb: h.copy(
                        xrT[:, kt * 512 + qq * 128: kt * 512 + qq * 128 + 128],
                        PT[:, tb * 1024: tb * 1024 + 128]),
                        reads=(Tk[tb],), writes=(f"xrT{kt}q{qq}",))
                vb = 2 + tt % 2
                p.step("tensor", lambda h, qq=qq, vb=vb: self._mmgroup(
                    h, PA[:, (vb - 2) * 512 + 1024: (vb - 2) * 512 + 1024 + 256],
                    [(xrT[:, kt * 512 + qq * 128: kt * 512 + qq * 128 + 128],
                      wvb[:, kt * 256:(kt + 1) * 256]) for kt in range(8)]),
                    reads=tuple(f"xrT{kt}q{qq}" for kt in range(8)) + ("wvb",),
                    writes=(A[vb],))
                for g in range(G):
                    p.step("scalar", lambda h, g=g, tt=tt, vb=vb: h.copy(
                        Vx[:, tt * 260 + g * 65: tt * 260 + g * 65 + 64],
                        PA[:, (vb - 2) * 512 + 1024 + g * 64:
                           (vb - 2) * 512 + 1024 + (g + 1) * 64]),
                        reads=(A[vb], f"{P}Vxones"), writes=(f"{P}Vx{tt}g{g}",))
            xr_all = tuple(f"xrT{kt}q{qq}" for kt in range(8) for qq in range(4))
            for mt in range(8):
                bq = mt % 2
                ob = mt % 2
                p.step("tensor", lambda h, mt=mt, bq=bq: self._mmgroup(
                    h, PA[:, bq * 512: bq * 512 + 512],
                    [(wqb[:, kt * 1024 + mt * 128: kt * 1024 + mt * 128 + 128],
                      xrT[:, kt * 512:(kt + 1) * 512]) for kt in range(8)]),
                    reads=xr_all + ("wqb",), writes=(A[bq],))
                p.step("scalar", lambda h, bq=bq, ob=ob: h.copy(
                    qstg[ob][:], PA[:, bq * 512: bq * 512 + 512]),
                    reads=(A[bq],), writes=(f"qstg{ob}",))
                p.step("sync", lambda h, mt=mt, ob=ob, tc=tc: h.dma_start(
                    QTh[mt * 128:(mt + 1) * 128, tc * 512:(tc + 1) * 512],
                    qstg[ob][:]),
                    reads=(f"qstg{ob}",), writes=(f"{P}QTm{mt}c{tc}",), dma=True)
            for mt in range(2):
                bq = mt % 2
                p.step("tensor", lambda h, mt=mt, bq=bq: self._mmgroup(
                    h, PA[:, bq * 512: bq * 512 + 512],
                    [(wkb[:, kt * 256 + mt * 128: kt * 256 + mt * 128 + 128],
                      xrT[:, kt * 512:(kt + 1) * 512]) for kt in range(8)]),
                    reads=xr_all + ("wkb",), writes=(A[bq],))
                for gg in range(2):
                    g = 2 * mt + gg
                    p.step("scalar", lambda h, g=g, gg=gg, bq=bq, tc=tc: h.copy(
                        KT2[g][0:64, tc * 512:(tc + 1) * 512],
                        PA[gg * 64:(gg + 1) * 64, bq * 512: bq * 512 + 512]),
                        reads=(A[bq],), writes=(f"{P}KT{g}c{tc}",))
        if (P + "QT") in self.taps:
            self.tap_dram(P + "QT", QTh,
                          [f"{P}QTm{mt}c{tc}" for mt in range(8) for tc in range(8)])
        self.tap(P + "KT0", KT2[0][:], [f"{P}KT0c{tc}" for tc in range(8)])

        self.tap(P + "Vx", Vx[:], [f"{P}Vx{tt}g{g}" for tt in range(NT) for g in range(G)])
        if P == "A" and self.cvt_stack is not None and self.stop_after == "f1":
            self._convert_weights("moe")
        p.barrier()
        ph.close()
        if self.stop_after == "f1":
            return

        ph = ExitStack()
        PC = self.ps(P + "PC_f2", [128, 1024], stack=ph)
        Ck = [f"{P}PCb{i}" for i in range(2)]
        qhT = [self.sb(f"f2qhT{i}", [65, T], BF16, stack=ph) for i in range(2)]
        mbuf = self.sb("mbuf", [128, 32], stack=ph)
        m32b = self.sb("m32b", [32, 128], BF16, stack=ph)
        pbuf = [self.sb(f"pbuf{i}", [128, 512], BF16, stack=ph) for i in range(3)]
        rrow = self.sb("rrow", [1, 512], stack=ph)
        rbs = self.sb("rbs", [64, 512], stack=ph)
        ohT = self.sb("ohT", [64, T], BF16, stack=ph)
        sc2 = self.sb("f2small", [128, 2], stack=ph)
        KTkeys = lambda g: tuple(f"{P}KT{g}c{tc}" for tc in range(8))
        Vxkeys = lambda g: tuple(f"{P}Vx{tt}g{g}" for tt in range(NT))
        QTkeys = tuple(f"{P}QTm{mt}c{tc}" for mt in range(8) for tc in range(8))

        def gen_pass1(hh):
            g = hh // 4
            qb = hh % 2
            kq = f"f2qhT{qb}"
            p.step("sync", lambda h, hh=hh, qb=qb: h.dma_start(
                qhT[qb][0:64, :], QTh[hh * 64:(hh + 1) * 64, :]),
                reads=QTkeys, writes=(kq,), dma=True, pclass="p64")
            yield
            cntp = 0
            for it in range(NT):
                njc = it // 4 + 1
                jc = 0
                while jc < njc:
                    w2 = 2 if jc + 1 < njc else 1
                    cntp += 1
                    pb_t = PB
                    keys = (Bk[0], Bk[1])
                    for u in range(w2):
                        p.step("tensor", lambda h, it=it, jc=jc, u=u, qb=qb, g=g,
                               pb_t=pb_t: h.matmul(
                            pb_t[:, u * 512: u * 512 + 512],
                            qhT[qb][0:64, it * 128:(it + 1) * 128],
                            KT2[g][0:64, (jc + u) * 512:(jc + u + 1) * 512],
                            start=True, stop=True),
                            reads=(kq,) + KTkeys(g), writes=(keys[u],), tag="p1mm")
                        yield
                        if jc + u == it // 4:
                            dlt = 128 * (it % 4)
                            p.step("vector", lambda h, dlt=dlt, u=u, pb_t=pb_t: h.tensor_tensor(
                                pb_t[:, u * 512: u * 512 + 512],
                                pb_t[:, u * 512: u * 512 + 512],
                                self.c_stair1[:, 384 - dlt: 896 - dlt], ALU.add),
                                reads=(keys[u], "c_stair1"), writes=(keys[u],))
                            yield
                    if jc == 0:
                        p.step("vector", lambda h, it=it, w2=w2, pb_t=pb_t: h.tensor_reduce(
                            mbuf[:, it:it + 1], pb_t[:, 0: w2 * 512],
                            AX.X, ALU.max),
                            reads=keys[:w2], writes=(f"mb{it}",))
                        yield
                    else:
                        p.step("vector", lambda h, w2=w2, pb_t=pb_t: h.tensor_reduce(
                            sc2[:, 0:1], pb_t[:, 0: w2 * 512],
                            AX.X, ALU.max),
                            reads=keys[:w2], writes=("f2sm",))
                        yield
                        p.step("vector", lambda h, it=it: h.tensor_tensor(
                            mbuf[:, it:it + 1], mbuf[:, it:it + 1],
                            sc2[:, 0:1], ALU.max),
                            reads=(f"mb{it}", "f2sm"), writes=(f"mb{it}",))
                        yield
                    jc += w2
            mkeys = tuple(f"mb{it}" for it in range(NT))
            p.step("tensor", lambda h: h.transpose(
                PC[0:32, 512:640], mbuf[:], self.c_identf[:]),
                reads=mkeys + ("c_identf",), writes=(Ck[1],), tag="mT")
            yield
            p.step("vector", lambda h: h.tensor_scalar(
                m32b[:], PC[0:32, 512:640], -1.0, None, ALU.mult),
                reads=(Ck[1],), writes=("m32",))
            yield
            p.step("sync", lambda h: h.dma_start(self.mflatb, m32b[:]),
                   reads=("m32",), writes=("mflat",), dma=True, pclass="misc")
            yield
            p.step("sync", lambda h, qb=qb: h.dma_start(
                qhT[qb][64:65, :], self.mflatb.rearrange("p n -> () (p n)")),
                reads=("mflat",), writes=(kq + "m",), dma=True, pclass="p1")
            yield

        def gen_pass2(hh):
            g = hh // 4
            qb = hh % 2
            kq = f"f2qhT{qb}"
            pend = [None]

            def make_finals(isp):
                po = 2 + isp % 2
                po2 = 2 + (isp + 1) % 2
                poo = 1024 + (isp % 2) * 512

                def finals():
                    p.step("tensor", lambda h, po2=po2: h.matmul(
                        PA[0:64, po2 * 512: po2 * 512 + 512],
                        self.c_ones128f[:, 0:64], rrow[:],
                        start=True, stop=True),
                        reads=("rrow", "c_ones128f"), writes=(A[po2],), tag="rb")
                    p.step("vector", lambda h, po2=po2: h.tensor_copy(
                        rbs[:], PA[0:64, po2 * 512: po2 * 512 + 512]),
                        reads=(A[po2],), writes=("rbs",))
                    p.step("vector", lambda h, isp=isp, poo=poo: h.tensor_tensor(
                        ohT[:, isp * 512:(isp + 1) * 512],
                        PA[0:64, poo: poo + 512], rbs[:], ALU.mult),
                        reads=(A[po], "rbs"), writes=("ohT",))
                return finals

            for isp in range(8):
                njc = 4 * isp + 4
                lastjc = njc - 1
                po = 2 + isp % 2
                poo = 1024 + (isp % 2) * 512
                poo2_r = poo

                def emit_pv(jc, lastjc=lastjc, g=g, po=po, poo=poo):
                    pb3 = jc % 3
                    p.step("tensor", lambda h, jc=jc, pb3=pb3, g=g,
                           lastjc=lastjc, poo=poo: h.matmul(
                        PA[0:65, poo: poo + 512],
                        Vx[:, jc * 260 + g * 65: jc * 260 + (g + 1) * 65],
                        pbuf[pb3][:], start=(jc == 0), stop=(jc == lastjc)),
                        reads=(f"pbuf{pb3}",) + Vxkeys(g), writes=(A[po],),
                        tag="pv")

                for jc in range(njc):
                    b3 = jc % 3
                    spb = [PA, PA, PC][b3]
                    soff = [0, 512, 0][b3]
                    skey = [A[0], A[1], Ck[0]][b3]
                    pb3 = jc % 3
                    p.step("tensor", lambda h, isp=isp, jc=jc, qb=qb, g=g,
                           spb=spb, soff=soff: h.matmul(
                        spb[:, soff: soff + 512],
                        KT2[g][:, jc * 128:(jc + 1) * 128],
                        qhT[qb][:, isp * 512:(isp + 1) * 512],
                        start=True, stop=True),
                        reads=(kq, kq + "m", f"{P}KTones{g}") + KTkeys(g),
                        writes=(skey,), tag="p2mm")
                    yield
                    if jc // 4 == isp:
                        d2 = 128 * (jc % 4)
                        p.step("vector", lambda h, d2=d2, spb=spb,
                               soff=soff: h.tensor_tensor(
                            spb[:, soff: soff + 512],
                            spb[:, soff: soff + 512],
                            self.c_stairT[:, 384 - d2: 896 - d2], ALU.add),
                            reads=(skey, "c_stairT"), writes=(skey,))
                        yield
                    p.step("scalar", lambda h, pb3=pb3, spb=spb, soff=soff: h.activation(
                        pbuf[pb3][:], spb[:, soff: soff + 512],
                        AF.Exp, scale=0.125),
                        reads=(skey,), writes=(f"pbuf{pb3}",))
                    yield
                    if jc == 1 and pend[0] is not None:
                        pend[0]()
                        pend[0] = None
                        yield
                    if jc >= 2:
                        emit_pv(jc - 2)
                        yield
                emit_pv(njc - 2)
                emit_pv(njc - 1)
                yield
                p.step("vector", lambda h, poo=poo2_r: h.reciprocal(
                    rrow[:], PA[64:65, poo: poo + 512]),
                    reads=(A[po],), writes=("rrow",))
                pend[0] = make_finals(isp)
            pend[0]()
            p.step("sync", lambda h, hh=hh: h.dma_start(
                OCh[hh * 64:(hh + 1) * 64, :], ohT[:]),
                reads=("ohT",), writes=(f"{P}OC{hh}",), dma=True, pclass="p64")
            yield

        cvtg = None
        if P == "A" and self.cvt_stack is not None:
            cvtg = self._convert_weights_gen("moe")
        for _ in gen_pass1(0):
            pass
        for hh in range(H):
            g2 = gen_pass2(hh)
            g1 = gen_pass1(hh + 1) if hh + 1 < H else None
            k = 0
            while True:
                done2 = False
                for _ in range(3):
                    try:
                        next(g2)
                    except StopIteration:
                        done2 = True
                        break
                if g1 is not None:
                    for _ in range(2):
                        try:
                            next(g1)
                        except StopIteration:
                            g1 = None
                            break
                if cvtg is not None:
                    k += 1
                    if k % 4 == 0:
                        try:
                            next(cvtg)
                        except StopIteration:
                            cvtg = None
                if done2:
                    if g1 is not None:
                        for _ in g1:
                            pass
                    break
        if cvtg is not None:
            for _ in cvtg:
                pass
        ockeys = tuple(f"{P}OC{hh}" for hh in range(H))
        if (P + "ocT") in self.taps:
            self.tap_dram(P + "ocT", OCh, ockeys)
        p.barrier()
        ph.close()
        if self.stop_after == "f2":
            return

        ph = ExitStack()
        PT = self.ps(P + "PT_f2b", [128, 2048], BF16, stack=ph)
        wob = self.sb("wob", [128, 8 * C], BF16, stack=ph)
        occ = [self.sb(f"occ{i}", [128, 8 * 128], BF16, stack=ph) for i in range(2)]
        xt2 = [self.sb(f"xt2{i}", [128, C], stack=ph) for i in range(2)]
        h0r = self.sb("h0r", [128, C], stack=ph)
        ht = self.sb("ht", [128, C], stack=ph)
        c0t = [self.sb(f"c0t{i}", [128, C], stack=ph) for i in range(2)]
        bt = self.sb("bt", [128, C], stack=ph)
        bthi = self.sb("bthi", [128, C], BF16, stack=ph)
        btlo = self.sb("btlo", [128, C], BF16, stack=ph)
        blosb = self.sb("blosb", [128, C], BF16, stack=ph)
        lgs = self.sb("lgs", [128, 8], stack=ph)
        lgall = self.sb("lgall", [128, NT * 8], stack=ph)
        dbg2 = self.sb("dbg2", [128, NT * 4], stack=ph)
        btb = [self.sb(f"btb{i}", [128, C], BF16, stack=ph) for i in range(2)]
        sq2 = self.sb("sq2_scr", [128, C], stack=ph)
        smallc2 = self.sb("smallc2", [128, 8], stack=ph)
        tp8 = self.sb("tp8", [128, 8], stack=ph)
        self.lg2scr = self.sb("lg2scr", [128, 8], stack=ph)

        p.step("sync", lambda h: h.dma_start(
            _s3(wob[:], 1024), _t3(self.wb["f_wo"])),
            reads=("f_wo_b_done",), writes=("wob",), dma=True)

        SUB = {"f2c": 1, "f2d": 2}.get(self.stop_after, 9)
        for tt in range(NT):
            b = tt % 2
            p.step("sync", lambda h, b=b, tt=tt: h.dma_start(
                xt2[b][:], x_hbm[tt * 128:(tt + 1) * 128, :]),
                writes=(f"xt2{b}",), dma=True)
            p.step("sync", lambda h, b=b, tt=tt: h.dma_start(
                _s3(occ[b][:], 128), _t3(OCh[:, tt * 128:(tt + 1) * 128])),
                reads=ockeys, writes=(f"occ{b}",), dma=True)
            for n0 in range(2):
                p.step("tensor", lambda h, b=b, n0=n0: self._mmgroup(
                    h, PB[:, n0 * 512:(n0 + 1) * 512],
                    [(occ[b][:, kt * 128:(kt + 1) * 128],
                      wob[:, kt * C + n0 * 512: kt * C + n0 * 512 + 512])
                     for kt in range(8)]),
                    reads=(f"occ{b}", "wob"), writes=(Bk[n0],), tag="womm")
            p.step("vector", lambda h, b=b, tt=tt: h.tensor_scalar(
                h0r[:], xt2[b][:], self.muS[:, tt:tt + 1], self.rstdS[:, tt:tt + 1],
                ALU.subtract, ALU.mult),
                reads=(f"xt2{b}", f"{P}mu{tt}", f"{P}rstd{tt}"), writes=("h0r",))
            for n0 in range(2):
                p.step("vector", lambda h, n0=n0: h.tensor_tensor(
                    ht[:, n0 * 512:(n0 + 1) * 512], PB[:, n0 * 512:(n0 + 1) * 512],
                    h0r[:, n0 * 512:(n0 + 1) * 512], ALU.add),
                    reads=(Bk[n0], "h0r"), writes=(f"ht{n0}",))
            p.step("vector", lambda h, b=b: h.scalar_tensor_tensor(
                c0t[b][:], ht[:], 0.5, xt2[b][:], ALU.mult, ALU.add),
                reads=("ht0", "ht1", f"xt2{b}"), writes=(f"c0t{b}",))
            p.step("sync", lambda h, b=b, tt=tt: h.dma_start(
                self.cinit[tt * 128:(tt + 1) * 128, :], c0t[b][:]),
                reads=(f"c0t{b}",), writes=(f"{P}cinit{tt}",), dma=True)
            if SUB < 2:
                continue
            p.step("scalar", lambda h: h.activation(
                sq2[:], ht[:], AF.Square, accum_out=smallc2[:, 0:1]),
                reads=("ht0", "ht1"), writes=("s2m0", "sq2"))
            p.step("vector", lambda h: h.tensor_scalar(
                smallc2[:, 1:2], smallc2[:, 0:1], 1.0 / C, None, ALU.mult),
                reads=("s2m0",), writes=("s2m1",))
            p.step("scalar", lambda h: h.activation(
                smallc2[:, 2:3], smallc2[:, 1:2], AF.Sqrt, bias=self.epscol6[:]),
                reads=("s2m1", "epscol6"), writes=("s2m2",))
            p.step("vector", lambda h: h.reciprocal(smallc2[:, 3:4], smallc2[:, 2:3]),
                reads=("s2m2",), writes=("s2m3",))
            p.step("vector", lambda h: h.scalar_tensor_tensor(
                bt[:], ht[:], smallc2[:, 3:4], self.normw2[:], ALU.mult, ALU.mult),
                reads=("ht0", "ht1", "s2m3", "normw2"), writes=("bt",))
            p.step("scalar", lambda h: h.activation(
                bthi[:], bt[:], AF.Copy),
                reads=("bt",), writes=("bthi",))
            p.step("vector", lambda h: h.tensor_tensor(
                btlo[:], bt[:], bthi[:], ALU.subtract),
                reads=("bt", "bthi"), writes=("btlo",))
            for kt in range(8):
                tb = kt % 2
                p.step("tensor", lambda h, kt=kt, tb=tb: h.transpose(
                    PT[:, tb * 1024: tb * 1024 + 128], bthi[:, kt * 128:(kt + 1) * 128],
                    self.c_ident[:]),
                    reads=("bthi", "c_ident"), writes=(Tk[tb],), tag="btT")
                p.step("scalar", lambda h, kt=kt, tb=tb, b=b: h.copy(
                    btb[b][:, kt * 128:(kt + 1) * 128], PT[:, tb * 1024: tb * 1024 + 128]),
                    reads=(Tk[tb],), writes=(f"btb{b}_{kt}",))
                p.step("tensor", lambda h, kt=kt, tb=tb: h.transpose(
                    PT[:, tb * 1024 + 128: tb * 1024 + 256],
                    btlo[:, kt * 128:(kt + 1) * 128], self.c_ident[:]),
                    reads=("btlo", "c_ident"), writes=(Tk[tb],), tag="bloT")
                p.step("vector", lambda h, kt=kt, tb=tb: h.tensor_copy(
                    blosb[:, kt * 128:(kt + 1) * 128],
                    PT[:, tb * 1024 + 128: tb * 1024 + 256]),
                    reads=(Tk[tb],), writes=(f"blo{kt}",))
            p.step("sync", lambda h, b=b, tt=tt: h.dma_start(
                _t3(self.bT_hbm[:, tt * 128:(tt + 1) * 128]), _s3(btb[b][:], 128)),
                reads=tuple(f"btb{b}_{kt}" for kt in range(8)),
                writes=(f"{P}bT{tt}",), dma=True)
            if SUB < 3:
                continue
            p.step("tensor", lambda h, b=b: self._mmgroup(
                h, PA[:, 0:8],
                [(btb[b][:, kt * 128:(kt + 1) * 128], self.routwh[:, kt * 8:(kt + 1) * 8])
                 for kt in range(8)] +
                [(btb[b][:, kt * 128:(kt + 1) * 128], self.routwl[:, kt * 8:(kt + 1) * 8])
                 for kt in range(8)] +
                [(blosb[:, kt * 128:(kt + 1) * 128], self.routwh[:, kt * 8:(kt + 1) * 8])
                 for kt in range(8)]),
                reads=tuple(f"btb{b}_{kt}" for kt in range(8)) +
                      tuple(f"blo{kt}" for kt in range(8)) + ("routwh", "routwl"),
                writes=(A[0],), tag="router")
            p.step("vector", lambda h: h.tensor_tensor(
                lgs[:], PA[:, 0:8], self.routbbc[:, 0:8], ALU.add),
                reads=(A[0], "routbbc"), writes=("lgs",))
            if (P + "lgall") in self.taps:
                p.step("vector", lambda h, tt=tt: h.tensor_copy(
                    lgall[:, tt * 8:(tt + 1) * 8], lgs[:]),
                    reads=("lgs",), writes=(f"lgall{tt}",))
            self._top2(tt, lgs[:], "lgs", smallc2, tp8, P)
            if (P + "dbg2") in self.taps:
                p.step("vector", lambda h, tt=tt: h.tensor_copy(
                    dbg2[:, tt * 4: tt * 4 + 1], smallc2[:, 4:5]),
                    reads=("t2w1",), writes=(f"dbg2a{tt}",))
                p.step("vector", lambda h, tt=tt: h.tensor_copy(
                    dbg2[:, tt * 4 + 1: tt * 4 + 2], smallc2[:, 5:6]),
                    reads=("t2i2",), writes=(f"dbg2b{tt}",))
                p.step("vector", lambda h, tt=tt: h.tensor_copy(
                    dbg2[:, tt * 4 + 2: tt * 4 + 3], smallc2[:, 6:7]),
                    reads=("t2w2",), writes=(f"dbg2c{tt}",))
                p.step("vector", lambda h, tt=tt: h.tensor_copy(
                    dbg2[:, tt * 4 + 3: tt * 4 + 4], smallc2[:, 7:8]),
                    reads=("t2m2b",), writes=(f"dbg2d{tt}",))
        if (P + "lgall") in self.taps:
            self.tap(P + "lgall", lgall[:], [f"lgall{tt}" for tt in range(NT)])
        if (P + "dbg2") in self.taps:
            self.tap(P + "dbg2", dbg2[:], [f"dbg2a{tt}" for tt in range(NT)] +
                     [f"dbg2d{tt}" for tt in range(NT)])
        p.barrier()
        ph.close()
        if (P + "cinit") in self.taps:
            self.tap_dram(P + "cinit", self.cinit,
                          [f"{P}cinit{tt}" for tt in range(NT)])
        if (P + "bT") in self.taps:
            self.tap_dram(P + "bT", self.bT_hbm, [f"{P}bT{tt}" for tt in range(NT)])
        if (P + "comb") in self.taps:
            self.tap(P + "comb", self.comb[:], [f"{P}comb{tt}" for tt in range(NT)])

    def _top2(self, tt, lg_psum, lg_key, smallc2, tp8, P):
        p = self.p
        iota = self.c_iota8
        lg2 = self.lg2scr
        p.step("vector", lambda h: h.tensor_reduce(
            smallc2[:, 4:5], lg_psum, AX.X, ALU.max),
            reads=(lg_key,), writes=("t2m1",))
        p.step("vector", lambda h: h.tensor_scalar(
            tp8[:], lg_psum, smallc2[:, 4:5], None, ALU.is_ge),
            reads=(lg_key, "t2m1"), writes=("t2a",))
        p.step("vector", lambda h: h.tensor_scalar(
            tp8[:], tp8[:], -99.0, 99.0, ALU.mult, ALU.add),
            reads=("t2a",), writes=("t2a",))
        p.step("vector", lambda h: h.tensor_tensor(
            tp8[:], tp8[:], iota[:], ALU.add),
            reads=("t2a", "c_iota8"), writes=("t2a",))
        p.step("vector", lambda h: h.tensor_reduce(
            smallc2[:, 5:6], tp8[:], AX.X, ALU.min),
            reads=("t2a",), writes=("t2i1",))
        p.step("vector", lambda h: h.tensor_scalar(
            tp8[:], iota[:], smallc2[:, 5:6], None, ALU.is_equal),
            reads=("c_iota8", "t2i1"), writes=("t2mask1",))
        p.step("vector", lambda h: h.scalar_tensor_tensor(
            lg2[:], tp8[:], -1e30, lg_psum, ALU.mult, ALU.add),
            reads=("t2mask1", lg_key), writes=("t2lg2",))
        p.step("vector", lambda h: h.tensor_reduce(
            smallc2[:, 6:7], lg2[:], AX.X, ALU.max),
            reads=("t2lg2",), writes=("t2m2",))
        p.step("vector", lambda h: h.tensor_tensor(
            smallc2[:, 7:8], smallc2[:, 4:5], smallc2[:, 6:7], ALU.subtract),
            reads=("t2m1", "t2m2"), writes=("t2d",))
        p.step("scalar", lambda h: h.activation(
            smallc2[:, 4:5], smallc2[:, 7:8], AF.Sigmoid),
            reads=("t2d",), writes=("t2w1",))
        p.step("scalar", lambda h: h.activation(
            smallc2[:, 6:7], smallc2[:, 7:8], AF.Sigmoid, scale=-1.0),
            reads=("t2d",), writes=("t2w2",))
        p.step("vector", lambda h, tt=tt: h.tensor_scalar(
            self.comb[:, tt * 8:(tt + 1) * 8], tp8[:], smallc2[:, 4:5], 0.5,
            ALU.mult, ALU.mult),
            reads=("t2mask1", "t2w1"), writes=(f"{P}combA{tt}",))
        p.step("vector", lambda h: h.tensor_reduce(
            smallc2[:, 7:8], lg2[:], AX.X, ALU.max),
            reads=("t2lg2",), writes=("t2m2b",))
        p.step("vector", lambda h: h.tensor_scalar(
            tp8[:], lg2[:], smallc2[:, 7:8], None, ALU.is_ge),
            reads=("t2lg2", "t2m2b"), writes=("t2b",))
        p.step("vector", lambda h: h.tensor_scalar(
            tp8[:], tp8[:], -99.0, 99.0, ALU.mult, ALU.add),
            reads=("t2b",), writes=("t2b",))
        p.step("vector", lambda h: h.tensor_tensor(
            tp8[:], tp8[:], iota[:], ALU.add),
            reads=("t2b", "c_iota8"), writes=("t2b",))
        p.step("vector", lambda h: h.tensor_reduce(
            smallc2[:, 5:6], tp8[:], AX.X, ALU.min),
            reads=("t2b",), writes=("t2i2",))
        p.step("vector", lambda h: h.tensor_scalar(
            tp8[:], iota[:], smallc2[:, 5:6], None, ALU.is_equal),
            reads=("c_iota8", "t2i2"), writes=("t2mask2",))
        p.step("vector", lambda h, tt=tt: h.tensor_scalar(
            lg2[:], tp8[:], smallc2[:, 6:7], 0.5, ALU.mult, ALU.mult),
            reads=("t2mask2", "t2w2"), writes=("t2part2",))
        p.step("vector", lambda h, tt=tt: h.tensor_tensor(
            self.comb[:, tt * 8:(tt + 1) * 8],
            self.comb[:, tt * 8:(tt + 1) * 8], lg2[:], ALU.add),
            reads=(f"{P}combA{tt}", "t2part2"), writes=(f"{P}comb{tt}",))

    def _ffn_f3(self, out_hbm, pref):
        p = self.p
        P = pref
        st = ExitStack()
        hhT = self.sb("hhT", [128, 32 * TBLK], BF16, stack=st)
        cblk = self.sb("cblk", [128, 8 * C], stack=st)
        bTb = self.sb("bTb", [128, 8 * TBLK], BF16, stack=st)
        w12 = [self.sb(f"w12_{i}", [128, 2 * 8 * 512], BF16, stack=st) for i in range(2)]
        w3s = [self.sb(f"w3s_{i}", [128, 4 * C], BF16, stack=st) for i in range(2)]
        s1b = [self.sb(f"s1b_{i}", [128, 512], BF16, stack=st) for i in range(2)]
        PA = self.ps("PA_f3", [128, 2048], stack=st)
        PB = self.ps("PB_f3", [128, 2048], stack=st)
        A = [f"{P}3PAb{i}" for i in range(4)]
        Bk = [f"{P}3PBb{i}" for i in range(4)]

        w1b, w2b, w3b = self.wb["f_w1"], self.wb["f_w2"], self.wb["f_w3"]

        for nb in range(NBLK):
            p.step("sync", lambda h, nb=nb: h.dma_start(
                _s3(cblk[:], C), _t3(self.cinit[nb * TBLK:(nb + 1) * TBLK, :])),
                reads=tuple(f"{P}cinit{nb * 8 + i}" for i in range(8)),
                writes=(f"cblk",), dma=True)
            p.step("sync", lambda h, nb=nb: h.dma_start(
                _s3(bTb[:], TBLK), _t3(self.bT_hbm[:, nb * TBLK:(nb + 1) * TBLK])),
                reads=tuple(f"{P}bT{nb * 8 + i}" for i in range(8)),
                writes=("bTb",), dma=True)
            for e in range(E):
                for fc in range(8):
                    sl = (e * 8 + fc) % 2
                    ksl = f"w12_{sl}"
                    p.step("sync", lambda h, e=e, fc=fc, sl=sl: h.dma_start(
                        _s3(w12[sl][:, 0:4096], 512),
                        _t3(w1b[e][:, fc * 512:(fc + 1) * 512])),
                        reads=("f_w1_b_done",), writes=(ksl,), dma=True)
                    p.step("sync", lambda h, e=e, fc=fc, sl=sl: h.dma_start(
                        _s3(w12[sl][:, 4096:8192], 512),
                        _t3(w2b[e][:, fc * 512:(fc + 1) * 512])),
                        reads=("f_w2_b_done",), writes=(ksl + "x",), dma=True)
                    for ft in range(4):
                        fg = fc * 4 + ft
                        b1c = self.b1sb[:, e * 32 + fg: e * 32 + fg + 1]
                        b2c = self.b2sb[:, e * 32 + fg: e * 32 + fg + 1]
                        for th in range(2):
                            r = (ft * 2 + th) % 2
                            kh1, kh2 = A[r], A[2 + r]
                            sb1 = s1b[r]
                            p.step("tensor", lambda h, sl=sl, ft=ft, th=th, r=r: self._mmgroup(
                                h, PA[:, r * 512: r * 512 + 512],
                                [(w12[sl][:, kt * 512 + ft * 128: kt * 512 + ft * 128 + 128],
                                  bTb[:, kt * TBLK + th * 512: kt * TBLK + th * 512 + 512])
                                 for kt in range(8)]),
                                reads=(ksl, "bTb"), writes=(kh1,), tag="h1mm")
                            p.step("tensor", lambda h, sl=sl, ft=ft, th=th, r=r: self._mmgroup(
                                h, PA[:, 1024 + r * 512: 1024 + r * 512 + 512],
                                [(w12[sl][:, 4096 + kt * 512 + ft * 128:
                                          4096 + kt * 512 + ft * 128 + 128],
                                  bTb[:, kt * TBLK + th * 512: kt * TBLK + th * 512 + 512])
                                 for kt in range(8)]),
                                reads=(ksl + "x", "bTb"), writes=(kh2,), tag="h2mm")
                            p.step("scalar", lambda h, r=r, b1c=b1c: h.activation(
                                s1b[r][:], PA[:, r * 512: r * 512 + 512],
                                AF.Silu, bias=b1c),
                                reads=(kh1, "b1sb"), writes=(f"s1b_{r}",))
                            p.step("vector", lambda h, r=r, b2c=b2c, fg=fg, th=th: h.scalar_tensor_tensor(
                                hhT[:, fg * TBLK + th * 512: fg * TBLK + th * 512 + 512],
                                PA[:, 1024 + r * 512: 1024 + r * 512 + 512],
                                b2c, s1b[r][:], ALU.add, ALU.mult),
                                reads=(kh2, "b2sb", f"s1b_{r}"),
                                writes=(f"hhT{fg}t{th}",))
                hkeys = tuple(f"hhT{fg}t{th}" for fg in range(32) for th in range(2))
                for sp in range(2):
                    for fc in range(8):
                        sl = fc % 2
                        kw3 = f"w3s_{sl}"
                        p.step("sync", lambda h, e=e, fc=fc, sl=sl: h.dma_start(
                            _s3(w3s[sl][:], C),
                            _t3(w3b[e][fc * 512:(fc + 1) * 512, :])),
                            reads=("f_w3_b_done",), writes=(kw3,), dma=True)
                        for tq in range(4):
                            tt = sp * 4 + tq
                            pb = [PA, PA, PB, PB][tq]
                            off = [0, 1024, 0, 1024][tq]
                            bk = [A, A, Bk, Bk][tq]
                            bko = [0, 2, 0, 2][tq]
                            for n0 in range(2):
                                p.step("tensor", lambda h, sl=sl, fc=fc, tt=tt, n0=n0,
                                       pb=pb, off=off: self._mmgroup(
                                    h, pb[:, off + n0 * 512: off + n0 * 512 + 512],
                                    [(hhT[:, (fc * 4 + fi) * TBLK + tt * 128:
                                           (fc * 4 + fi) * TBLK + tt * 128 + 128],
                                      w3s[sl][:, fi * C + n0 * 512: fi * C + n0 * 512 + 512])
                                     for fi in range(4)],
                                    start=(fc == 0), stop=False),
                                    reads=(kw3,) + hkeys,
                                    writes=(bk[bko + n0],), tag="w3mm")
                    for tq in range(4):
                        tt = sp * 4 + tq
                        pb = [PA, PA, PB, PB][tq]
                        off = [0, 1024, 0, 1024][tq]
                        bk = [A, A, Bk, Bk][tq]
                        bko = [0, 2, 0, 2][tq]
                        for n0 in range(2):
                            p.step("tensor", lambda h, e=e, n0=n0, pb=pb, off=off: h.matmul(
                                pb[:, off + n0 * 512: off + n0 * 512 + 512],
                                self.c_ones128b[:],
                                self.b3cat[:, e * C + n0 * 512: e * C + n0 * 512 + 512],
                                start=False, stop=True),
                                reads=("c_ones128b", "b3cat", bk[bko + n0]),
                                writes=(bk[bko + n0],), tag="b3mm")
                        combc = self.comb[:, (nb * 8 + tt) * 8 + e: (nb * 8 + tt) * 8 + e + 1]
                        p.step("vector", lambda h, tt=tt, e=e, pb=pb, off=off,
                               combc=combc: h.scalar_tensor_tensor(
                            cblk[:, tt * C: (tt + 1) * C], pb[:, off: off + 1024],
                            combc, cblk[:, tt * C: (tt + 1) * C], ALU.mult, ALU.add),
                            reads=(bk[bko], bk[bko + 1], f"{P}comb{nb * 8 + tt}", "cblk"),
                            writes=("cblk",), tag="cacc")
            p.step("sync", lambda h, nb=nb: h.dma_start(
                _t3(out_hbm[nb * TBLK:(nb + 1) * TBLK, :]), _s3(cblk[:], C)),
                reads=("cblk",), writes=(f"{P}out{nb}",), dma=True)
        if (P + "out") in self.taps:
            self.tap_dram(P + "out", out_hbm, [f"{P}out{nb}" for nb in range(NBLK)])
        st.close()

    def _sparse_slot(self, in_hbm, out_hbm, pref):
        self._sparse_s1(in_hbm, pref)
        self.p.barrier()
        self._sparse_s2(pref)
        self.p.barrier()
        self._sparse_s3(in_hbm, out_hbm, pref)
        self.p.barrier()

    def _sparse_s1(self, in_hbm, P):
        p = self.p
        st = ExitStack()
        cawb = self.sb("cawb", [128, 8 * 3 * C], BF16, stack=st)
        qwb = self.sb("s_qwb", [128, 8 * C], BF16, stack=st)
        kwb = self.sb("s_kwb", [128, 8 * C], BF16, stack=st)
        vwb = self.sb("s_vwb", [128, 8 * C], BF16, stack=st)
        xt = [self.sb(f"s_xt{i}", [128, C], stack=st) for i in range(2)]
        gt = [self.sb(f"s_gt{i}", [128, C], BF16, stack=st) for i in range(2)]
        sq = self.sb("s_sq", [128, C], stack=st)
        smallc = self.sb("s_smallc", [128, 8], stack=st)
        gT = self.sb("s_gT", [128, 8 * 512], BF16, stack=st)
        qkvT = self.sb("s_qkvT", [128, 24 * 512], BF16, stack=st)
        ostg = [self.sb(f"s_ostg{i}", [128, 512], BF16, stack=st) for i in range(2)]
        PA = self.ps("PA_s1", [128, 2048], stack=st)
        PT = self.ps("PT_s1", [128, 2048], BF16, stack=st)
        A = [f"{P}1PAb{i}" for i in range(4)]
        Tk = [f"{P}1PTb{i}" for i in range(2)]

        p.step("sync", lambda h: h.dma_start(
            _s3(cawb[:], {"f_wq":1024,"f_wk":256,"f_wv":256,"f_wo":1024,"ca_w":3072,"q_w":1024,"k_w":1024,"v_w":1024,"o_w":1024}["ca_w"]), _t3(self.wb["ca_w"])),
            reads=("ca_w_b_done",), writes=("cawb",), dma=True)
        p.step("sync", lambda h: h.dma_start(
            _s3(qwb[:], {"f_wq":1024,"f_wk":256,"f_wv":256,"f_wo":1024,"ca_w":3072,"q_w":1024,"k_w":1024,"v_w":1024,"o_w":1024}["q_w"]), _t3(self.wb["q_w"])),
            reads=("q_w_b_done",), writes=("s_qwb",), dma=True)
        p.step("sync", lambda h: h.dma_start(
            _s3(kwb[:], {"f_wq":1024,"f_wk":256,"f_wv":256,"f_wo":1024,"ca_w":3072,"q_w":1024,"k_w":1024,"v_w":1024,"o_w":1024}["k_w"]), _t3(self.wb["k_w"])),
            reads=("k_w_b_done",), writes=("s_kwb",), dma=True)
        p.step("sync", lambda h: h.dma_start(
            _s3(vwb[:], {"f_wq":1024,"f_wk":256,"f_wv":256,"f_wo":1024,"ca_w":3072,"q_w":1024,"k_w":1024,"v_w":1024,"o_w":1024}["v_w"]), _t3(self.wb["v_w"])),
            reads=("v_w_b_done",), writes=("s_vwb",), dma=True)

        for tc in range(NCH):
            for qq in range(4):
                tt = tc * 4 + qq
                b = tt % 2
                kx, kg = f"s_xt{b}", f"s_gt{b}"
                p.step("sync", lambda h, b=b, tt=tt: h.dma_start(
                    xt[b][:], in_hbm[tt * 128:(tt + 1) * 128, :]),
                    writes=(kx,), dma=True)
                self._ln_stats(xt[b], kx, self.muS[:, tt:tt + 1], f"{P}mu{tt}",
                               self.rstdS[:, tt:tt + 1], f"{P}rstd{tt}",
                               sq, smallc)
                p.step("vector", lambda h, b=b, tt=tt: h.tensor_scalar(
                    gt[b][:], xt[b][:], self.muS[:, tt:tt + 1],
                    self.rstdS[:, tt:tt + 1], ALU.subtract, ALU.mult),
                    reads=(kx, f"{P}mu{tt}", f"{P}rstd{tt}"), writes=(kg,))
                for kt in range(8):
                    tb = kt % 2
                    p.step("tensor", lambda h, b=b, kt=kt, tb=tb: h.transpose(
                        PT[:, tb * 1024: tb * 1024 + 128],
                        gt[b][:, kt * 128:(kt + 1) * 128], self.c_ident[:]),
                        reads=(kg, "c_ident"), writes=(Tk[tb],))
                    p.step("vector", lambda h, kt=kt, qq=qq, tb=tb: h.tensor_copy(
                        gT[:, kt * 512 + qq * 128: kt * 512 + qq * 128 + 128],
                        PT[:, tb * 1024: tb * 1024 + 128]),
                        reads=(Tk[tb],), writes=(f"s_gT{kt}q{qq}",))
            gkeys = tuple(f"s_gT{kt}q{qq}" for kt in range(8) for qq in range(4))
            for mt in range(24):
                bq = mt % 2
                p.step("tensor", lambda h, mt=mt, bq=bq: self._mmgroup(
                    h, PA[:, bq * 512: bq * 512 + 512],
                    [(cawb[:, kt * 3072 + mt * 128: kt * 3072 + mt * 128 + 128],
                      gT[:, kt * 512:(kt + 1) * 512]) for kt in range(8)]),
                    reads=gkeys + ("cawb",), writes=(A[bq],))
                p.step("scalar", lambda h, mt=mt, bq=bq: h.activation(
                    qkvT[:, mt * 512:(mt + 1) * 512], PA[:, bq * 512: bq * 512 + 512],
                    AF.Identity, bias=self.cabT[:, mt:mt + 1]),
                    reads=(A[bq], "cabT"), writes=(f"s_qkvT{mt}",))
            midq = tuple(f"s_qkvT{mt}" for mt in range(8))
            midk = tuple(f"s_qkvT{mt}" for mt in range(8, 16))
            midv = tuple(f"s_qkvT{mt}" for mt in range(16, 24))
            for which, wsb, wkey, bias, bkey, dst, mids, moff in (
                    ("q", qwb, "s_qwb", self.qb8T, "qb8T", self.qT2, midq, 0),
                    ("k", kwb, "s_kwb", self.kbT, "kbT", self.kT2, midk, 8),
                    ("v", vwb, "s_vwb", self.vbT, "vbT", self.vT2, midv, 16)):
                for mt in range(8):
                    bq = 2 + mt % 2
                    ob = mt % 2
                    p.step("tensor", lambda h, mt=mt, bq=bq, wsb=wsb, moff=moff: self._mmgroup(
                        h, PA[:, bq * 512: bq * 512 + 512],
                        [(wsb[:, kt * C + mt * 128: kt * C + mt * 128 + 128],
                          qkvT[:, (moff + kt) * 512:(moff + kt + 1) * 512])
                         for kt in range(8)]),
                        reads=mids + (wkey,), writes=(A[bq],))
                    p.step("scalar", lambda h, mt=mt, bq=bq, ob=ob, bias=bias: h.activation(
                        ostg[ob][:], PA[:, bq * 512: bq * 512 + 512],
                        AF.Identity, bias=bias[:, mt:mt + 1]),
                        reads=(A[bq], bkey), writes=(f"s_ostg{ob}",))
                    p.step("sync", lambda h, mt=mt, ob=ob, dst=dst, tc=tc: h.dma_start(
                        dst[mt * 128:(mt + 1) * 128, tc * 512:(tc + 1) * 512],
                        ostg[ob][:]),
                        reads=(f"s_ostg{ob}",),
                        writes=(f"{P}{dst.tensor.name}m{mt}c{tc}",), dma=True)
        self.tap(P + "mid", qkvT[:], [f"s_qkvT{mt}" for mt in range(24)])
        self.tap(P + "gT", gT[:], [f"s_gT{kt}q{qq}" for kt in range(8) for qq in range(4)])
        if (P + "qT2") in self.taps:
            self.tap_dram(P + "qT2", self.qT2,
                          [f"{P}qT2m{mt}c{tc}" for mt in range(8) for tc in range(8)])
        if (P + "kT2") in self.taps:
            self.tap_dram(P + "kT2", self.kT2,
                          [f"{P}kT2m{mt}c{tc}" for mt in range(8) for tc in range(8)])
        if (P + "vT2") in self.taps:
            self.tap_dram(P + "vT2", self.vT2,
                          [f"{P}vT2m{mt}c{tc}" for mt in range(8) for tc in range(8)])
        p.barrier()
        st.close()

    def _sparse_s2(self, P):
        p = self.p
        st = ExitStack()
        qhTb = [self.sb(f"qhT{i}", [64, T], BF16, stack=st) for i in range(2)]
        khTb = [self.sb(f"khT{i}", [64, T], BF16, stack=st) for i in range(2)]
        vhTb = [self.sb(f"vhT{i}", [64, T], BF16, stack=st) for i in range(2)]
        qgex = [self.sb(f"qgex{i}", [65, 512], BF16, stack=st) for i in range(2)]
        kgex = [self.sb(f"kgex{i}", [65, 512], BF16, stack=st) for i in range(2)]
        vgx = [self.sb(f"vgx{i}", [128, 4 * 65], BF16, stack=st) for i in range(2)]
        mseg = self.sb("mseg", [128, 56], stack=st)
        m4s = self.sb("m4s", [56, 128], stack=st)
        mgrow = self.sb("mgrow", [1, T], stack=st)
        negm = self.sb("negm", [1, T], BF16, stack=st)
        oT = [self.sb(f"oT{b}", [64, T], BF16, stack=st) for b in range(3)]
        mtf = self.sb("mtf", [1, T], BF16, stack=st)
        lsef = self.sb("lsef", [1, T], stack=st)
        lnl = self.sb("lnl", [1, 512], stack=st)
        pbuf = [self.sb(f"s_pbuf{i}", [128, 512], BF16, stack=st) for i in range(3)]
        PA = self.ps("PA_s2", [128, 2048], stack=st)
        PT = self.ps("PT_s2", [128, 2048], BF16, stack=st)
        A = [f"{P}2PAb{i}" for i in range(4)]
        Tk = [f"{P}2PTb{i}" for i in range(2)]
        mfl = self.dram(P + "mfl", [56, 128])
        self.yT2 = self.nc.dram_tensor(P + "yT2", [C, T], BF16, kind="Internal").ap()
        oTh = [self.nc.dram_tensor(f"{P}oTh{b}", [C, T], BF16, kind="Internal").ap()
               for b in range(3)]
        LSEh = [self.dram(f"{P}LSEh{b}", [H, T]) for b in range(3)]
        MTh = [self.nc.dram_tensor(f"{P}MTh{b}", [H, T], BF16, kind="Internal").ap()
               for b in range(3)]
        Uh = [self.dram(f"{P}Uh{b}", [H, T]) for b in range(3)]

        qkeys = tuple(f"{P}qT2m{mt}c{tc}" for mt in range(8) for tc in range(8))
        kkeys = tuple(f"{P}kT2m{mt}c{tc}" for mt in range(8) for tc in range(8))
        vkeys = tuple(f"{P}vT2m{mt}c{tc}" for mt in range(8) for tc in range(8))

        for i in range(2):
            p.step("vector", lambda h, i=i: h.memset(vgx[i][:], 1.0),
                   writes=(f"vgx_ones{i}",))
            p.step("vector", lambda h, i=i: h.memset(kgex[i][64:65, :], 1.0),
                   writes=(f"kgex1_{i}",))

        BR_ROWS = ((0, 32), (32, 48), (48, 56))
        def load_head(hh):
            hb = hh % 2
            p.step("sync", lambda h, hh=hh, hb=hb: h.dma_start(
                qhTb[hb][:], self.qT2[hh * 64:(hh + 1) * 64, :]),
                reads=qkeys, writes=(f"qhT{hb}",), dma=True, pclass="p64")
            p.step("sync", lambda h, hh=hh, hb=hb: h.dma_start(
                khTb[hb][:], self.kT2[hh * 64:(hh + 1) * 64, :]),
                reads=kkeys, writes=(f"khT{hb}",), dma=True, pclass="p64")
            p.step("sync", lambda h, hh=hh, hb=hb: h.dma_start(
                vhTb[hb][:], self.vT2[hh * 64:(hh + 1) * 64, :]),
                reads=vkeys, writes=(f"vhT{hb}",), dma=True, pclass="p64")

        load_head(0)
        for hh in range(H):
            hb = hh % 2
            qhT, khT, vhT = qhTb[hb], khTb[hb], vhTb[hb]
            kqh, kkh, kvh = f"qhT{hb}", f"khT{hb}", f"vhT{hb}"
            segl = []
            for bi, (wseg, r) in enumerate(SEGS):
                off = hh % r
                for sg in range(T // wseg):
                    segl.append((bi, sg, sg * wseg + off, r))
            mc = 0
            for bi, sg, base, r in segl:
                for ic in range(4):
                    bq = ic % 2
                    p.step("tensor", lambda h, base=base, r=r, ic=ic, bq=bq,
                           qhT=qhT, khT=khT: h.matmul(
                        PA[:, bq * 512: bq * 512 + 512],
                        qhT[:, base + ic * 128 * r: base + (ic * 128 + 127) * r + 1: r],
                        khT[:, base: base + 511 * r + 1: r], start=True, stop=True),
                        reads=(kqh, kkh), writes=(A[bq],), tag="sp1mm")
                    dlt = 128 * ic
                    p.step("vector", lambda h, bq=bq, dlt=dlt: h.tensor_tensor(
                        PA[:, bq * 512: bq * 512 + 512],
                        PA[:, bq * 512: bq * 512 + 512],
                        self.c_stair1[:, 384 - dlt: 896 - dlt], ALU.add),
                        reads=(A[bq], "c_stair1"), writes=(A[bq],))
                    p.step("vector", lambda h, bq=bq, mc=mc, ic=ic: h.tensor_reduce(
                        mseg[:, mc + ic: mc + ic + 1],
                        PA[:, bq * 512: bq * 512 + 512], AX.X, ALU.max),
                        reads=(A[bq],), writes=(f"msg{mc + ic}",))
                mc += 4
            mall = tuple(f"msg{i}" for i in range(56))
            p.step("tensor", lambda h: h.transpose(
                PA[0:56, 1024:1152], mseg[:], self.c_identf[:]),
                reads=mall + ("c_identf",), writes=(A[2],), tag="msT")
            p.step("vector", lambda h: h.tensor_copy(m4s[:], PA[0:56, 1024:1152]),
                   reads=(A[2],), writes=("m4s",))
            p.step("sync", lambda h: h.dma_start(mfl, m4s[:]),
                   reads=("m4s",), writes=("mfl",), dma=True, pclass="misc")
            if hh + 1 < H:
                load_head(hh + 1)
            for bi, (wseg, r) in enumerate(SEGS):
                off = hh % r
                nseg = T // wseg
                blen = nseg * 512
                r0, r1 = BR_ROWS[bi]
                p.step("sync", lambda h, r0=r0, r1=r1, blen=blen: h.dma_start(
                    mgrow[:, 0:blen],
                    mfl[r0:r1].rearrange("p n -> () (p n)")),
                    reads=("mfl",), writes=("mgrow",), dma=True, pclass="p1")
                p.step("scalar", lambda h, blen=blen: h.activation(
                    negm[:, 0:blen], mgrow[:, 0:blen], AF.Copy, scale=-1.0),
                    reads=("mgrow",), writes=("negm",))
                p.step("vector", lambda h: h.memset(mtf[:], 1e30), writes=("mtf",))
                p.step("vector", lambda h: h.memset(lsef[:], -1e30), writes=("lsef",))
                if bi > 0:
                    p.step("vector", lambda h, bi=bi: h.memset(oT[bi][:], 0.0),
                           writes=(f"oT{bi}",))
                pend = [None]

                def make_sfinals(sg, base, gbase, bi, r):
                    sb2 = sg % 2
                    po = 2 + sb2
                    poo = 1024 + sb2 * 512

                    def fin():
                        p.step("scalar", lambda h, poo=poo: h.activation(
                            lnl[:], PA[64:65, poo: poo + 512], AF.Ln),
                            reads=(A[po],), writes=("lnl",))
                        p.step("vector", lambda h, base=base, r=r, gbase=gbase: h.tensor_tensor(
                            lsef[:, base: base + 511 * r + 1: r], lnl[:],
                            negm[:, gbase: gbase + 512], ALU.subtract),
                            reads=("lnl", "negm", "lsef"), writes=("lsef",))
                        p.step("vector", lambda h, base=base, r=r, gbase=gbase: h.tensor_copy(
                            mtf[:, base: base + 511 * r + 1: r], negm[:, gbase: gbase + 512]),
                            reads=("negm", "mtf"), writes=("mtf",))
                        p.step("vector", lambda h, base=base, r=r, bi=bi,
                               poo=poo: h.tensor_copy(
                            oT[bi][0:64, base: base + 511 * r + 1: r],
                            PA[0:64, poo: poo + 512]),
                            reads=(A[po], f"oT{bi}"), writes=(f"oT{bi}",))
                    return fin

                for sg in range(nseg):
                    base = sg * wseg + off
                    gbase = sg * 512
                    sb2 = sg % 2
                    po = 2 + sb2
                    poo = 1024 + sb2 * 512
                    kk, kk1 = f"kgex{sb2}", f"kgex1_{sb2}"
                    kv, kvo = f"vgx{sb2}", f"vgx_ones{sb2}"
                    p.step("vector", lambda h, base=base, r=r, sb2=sb2,
                           khT=khT: h.tensor_copy(
                        kgex[sb2][0:64, :], khT[:, base: base + 511 * r + 1: r]),
                        reads=(kkh,), writes=(kk,))
                    p.step("vector", lambda h, base=base, r=r, sb2=sb2,
                           qhT=qhT: h.tensor_copy(
                        qgex[sb2][0:64, :], qhT[:, base: base + 511 * r + 1: r]),
                        reads=(kqh,), writes=(f"qgex{sb2}",))
                    p.step("vector", lambda h, gbase=gbase, sb2=sb2: h.tensor_copy(
                        qgex[sb2][64:65, :], negm[:, gbase: gbase + 512]),
                        reads=("negm",), writes=(f"qgex1x{sb2}",))
                    for jc in range(4):
                        tb = jc % 2
                        p.step("tensor", lambda h, base=base, r=r, jc=jc, tb=tb,
                               vhT=vhT: h.transpose(
                            PT[:, tb * 1024: tb * 1024 + 64],
                            vhT[:, base + jc * 128 * r: base + (jc * 128 + 127) * r + 1: r],
                            self.c_ident[0:64, 0:64]),
                            reads=(kvh, "c_ident"), writes=(Tk[tb],), tag="vgT")
                        p.step("vector", lambda h, jc=jc, tb=tb, sb2=sb2: h.tensor_copy(
                            vgx[sb2][:, jc * 65: jc * 65 + 64],
                            PT[:, tb * 1024: tb * 1024 + 64]),
                            reads=(Tk[tb], kvo), writes=(kv,))
                    for jc in range(4):
                        bq = jc % 2
                        pb3 = jc % 3
                        p.step("tensor", lambda h, jc=jc, bq=bq, sb2=sb2: h.matmul(
                            PA[:, bq * 512: bq * 512 + 512],
                            kgex[sb2][:, jc * 128:(jc + 1) * 128], qgex[sb2][:],
                            start=True, stop=True),
                            reads=(kk, kk1, f"qgex{sb2}", f"qgex1x{sb2}"),
                            writes=(A[bq],), tag="sp2mm")
                        d2 = 128 * jc
                        p.step("vector", lambda h, bq=bq, d2=d2: h.tensor_tensor(
                            PA[:, bq * 512: bq * 512 + 512],
                            PA[:, bq * 512: bq * 512 + 512],
                            self.c_stairT[:, 384 - d2: 896 - d2], ALU.add),
                            reads=(A[bq], "c_stairT"), writes=(A[bq],))
                        p.step("scalar", lambda h, bq=bq, pb3=pb3: h.activation(
                            pbuf[pb3][:], PA[:, bq * 512: bq * 512 + 512], AF.Exp),
                            reads=(A[bq],), writes=(f"s_pbuf{pb3}",))
                        if jc == 1 and pend[0] is not None:
                            pend[0]()
                            pend[0] = None
                        if jc >= 2:
                            jp = jc - 2
                            p.step("tensor", lambda h, jp=jp, sb2=sb2, poo=poo,
                                   po=po: h.matmul(
                                PA[0:65, poo: poo + 512],
                                vgx[sb2][:, jp * 65:(jp + 1) * 65],
                                pbuf[jp % 3][:], start=(jp == 0), stop=False),
                                reads=(f"s_pbuf{jp % 3}", kv), writes=(A[po],),
                                tag="spv")
                    for jp in (2, 3):
                        p.step("tensor", lambda h, jp=jp, sb2=sb2, poo=poo,
                               po=po: h.matmul(
                            PA[0:65, poo: poo + 512],
                            vgx[sb2][:, jp * 65:(jp + 1) * 65],
                            pbuf[jp % 3][:], start=False, stop=(jp == 3)),
                            reads=(f"s_pbuf{jp % 3}", kv), writes=(A[po],),
                            tag="spv")
                    pend[0] = make_sfinals(sg, base, gbase, bi, r)
                if pend[0] is not None:
                    pend[0]()
                    pend[0] = None
                p.step("sync", lambda h, bi=bi, hh=hh: h.dma_start(
                    LSEh[bi][hh:hh + 1, :], lsef[:]),
                    reads=("lsef",), writes=(f"{P}LSE{bi}h{hh}",), dma=True, pclass="p1")
                p.step("sync", lambda h, bi=bi, hh=hh: h.dma_start(
                    MTh[bi][hh:hh + 1, :], mtf[:]),
                    reads=("mtf",), writes=(f"{P}MT{bi}h{hh}",), dma=True, pclass="p1")
                p.step("sync", lambda h, bi=bi, hh=hh: h.dma_start(
                    oTh[bi][hh * 64:(hh + 1) * 64, :], oT[bi][:]),
                    reads=(f"oT{bi}",), writes=(f"{P}oTh{bi}h{hh}",), dma=True, pclass="p64")
        p.barrier()
        st.close()

        st = ExitStack()
        CW = 2048
        Lc = [self.sb(f"Lc{b}", [H, CW], stack=st) for b in range(3)]
        Mc = [self.sb(f"Mc{b}", [H, CW], BF16, stack=st) for b in range(3)]
        mg = self.sb("mg_m", [H, CW], stack=st)
        rden = self.sb("rden_m", [H, CW], stack=st)
        escr = self.sb("escr_m", [H, CW], stack=st)
        allL = tuple(f"{P}LSE{b}h{hh}" for b in range(3) for hh in range(H))
        allM = tuple(f"{P}MT{b}h{hh}" for b in range(3) for hh in range(H))
        for ch in range(T // CW):
            c0 = ch * CW
            for b in range(3):
                p.step("sync", lambda h, b=b, c0=c0: h.dma_start(
                    Lc[b][:], LSEh[b][:, c0:c0 + CW]),
                    reads=allL, writes=(f"Lc{b}",), dma=True, pclass="misc")
                p.step("sync", lambda h, b=b, c0=c0: h.dma_start(
                    Mc[b][:], MTh[b][:, c0:c0 + CW]),
                    reads=allM, writes=(f"Mc{b}",), dma=True, pclass="misc")
            p.step("vector", lambda h: h.tensor_tensor(
                mg[:], Lc[0][:], Lc[1][:], ALU.max),
                reads=("Lc0", "Lc1"), writes=("mg_m",))
            p.step("vector", lambda h: h.tensor_tensor(
                mg[:], mg[:], Lc[2][:], ALU.max),
                reads=("mg_m", "Lc2"), writes=("mg_m",))
            for b in range(3):
                p.step("vector", lambda h, b=b: h.tensor_tensor(
                    escr[:], Lc[b][:], mg[:], ALU.subtract),
                    reads=(f"Lc{b}", "mg_m"), writes=("escr_m",))
                p.step("scalar", lambda h, b=b: h.activation(
                    Lc[b][:], escr[:], AF.Exp),
                    reads=("escr_m",), writes=(f"Le{b}",))
            p.step("vector", lambda h: h.tensor_tensor(
                rden[:], Lc[0][:], Lc[1][:], ALU.add),
                reads=("Le0", "Le1"), writes=("rden_m",))
            p.step("vector", lambda h: h.tensor_tensor(
                rden[:], rden[:], Lc[2][:], ALU.add),
                reads=("rden_m", "Le2"), writes=("rden_m",))
            p.step("vector", lambda h: h.reciprocal(rden[:], rden[:]),
                   reads=("rden_m",), writes=("rden_m",))
            for b in range(3):
                p.step("vector", lambda h, b=b: h.tensor_tensor(
                    escr[:], Mc[b][:], mg[:], ALU.add),
                    reads=(f"Mc{b}", "mg_m"), writes=("escr_m",))
                p.step("scalar", lambda h, b=b: h.activation(
                    escr[:], escr[:], AF.Exp, scale=-1.0),
                    reads=("escr_m",), writes=("escr_m",))
                p.step("vector", lambda h, b=b: h.tensor_tensor(
                    Lc[b][:], escr[:], rden[:], ALU.mult),
                    reads=("escr_m", "rden_m"), writes=(f"Uc{b}",))
                p.step("sync", lambda h, b=b, c0=c0: h.dma_start(
                    Uh[b][:, c0:c0 + CW], Lc[b][:]),
                    reads=(f"Uc{b}",), writes=(f"{P}U{b}c{c0}",), dma=True, pclass="misc")
        p.barrier()
        st.close()

        st = ExitStack()
        urow = [[self.sb(f"urow{b}_{i}", [1, T], stack=st) for b in range(3)]
                for i in range(2)]
        oTb = [[self.sb(f"oTb{b}_{i}", [64, T], BF16, stack=st) for b in range(3)]
               for i in range(2)]
        ych = self.sb("ych", [64, 512], stack=st)
        ytmp = self.sb("ytmp", [64, 512], stack=st)
        yhT = self.sb("yhT", [64, T], BF16, stack=st)
        PB = self.ps("PB_s2y", [128, 2048], stack=st)
        Bk = [f"{P}yPBb{i}" for i in range(4)]
        ukeys = tuple(f"{P}U{b}c{c0}" for b in range(3) for c0 in range(0, T, CW))
        def load_fy(hh):
            hb = hh % 2
            for b in range(3):
                p.step("sync", lambda h, b=b, hh=hh, hb=hb: h.dma_start(
                    urow[hb][b][:], Uh[b][hh:hh + 1, :]),
                    reads=ukeys, writes=(f"urow{b}_{hb}",), dma=True, pclass="p1")
                p.step("sync", lambda h, b=b, hh=hh, hb=hb: h.dma_start(
                    oTb[hb][b][:], oTh[b][hh * 64:(hh + 1) * 64, :]),
                    reads=(f"{P}oTh{b}h{hh}",), writes=(f"oTb{b}_{hb}",),
                    dma=True, pclass="p64")

        load_fy(0)
        for hh in range(H):
            hb = hh % 2
            if hh + 1 < H:
                load_fy(hh + 1)
            for ch in range(8):
                c0 = ch * 512
                for b in range(3):
                    p.step("tensor", lambda h, b=b, c0=c0, hb=hb: h.matmul(
                        PB[0:64, b * 512:(b + 1) * 512], self.c_ones128f[:, 0:64],
                        urow[hb][b][:, c0:c0 + 512], start=True, stop=True),
                        reads=(f"urow{b}_{hb}", "c_ones128f"), writes=(Bk[b],), tag="ubc")
                p.step("vector", lambda h, c0=c0, hb=hb: h.tensor_tensor(
                    ych[:], PB[0:64, 0:512], oTb[hb][0][:, c0:c0 + 512], ALU.mult),
                    reads=(Bk[0], f"oTb0_{hb}"), writes=("ych",))
                p.step("vector", lambda h, c0=c0, hb=hb: h.tensor_tensor(
                    ytmp[:], PB[0:64, 512:1024], oTb[hb][1][:, c0:c0 + 512], ALU.mult),
                    reads=(Bk[1], f"oTb1_{hb}"), writes=("ytmp",))
                p.step("vector", lambda h: h.tensor_tensor(
                    ych[:], ych[:], ytmp[:], ALU.add),
                    reads=("ych", "ytmp"), writes=("ych",))
                p.step("vector", lambda h, c0=c0, hb=hb: h.tensor_tensor(
                    ytmp[:], PB[0:64, 1024:1536], oTb[hb][2][:, c0:c0 + 512], ALU.mult),
                    reads=(Bk[2], f"oTb2_{hb}"), writes=("ytmp",))
                p.step("vector", lambda h, c0=c0: h.tensor_tensor(
                    yhT[:, c0:c0 + 512], ych[:], ytmp[:], ALU.add),
                    reads=("ych", "ytmp"), writes=("yhT",))
            p.step("sync", lambda h, hh=hh: h.dma_start(
                self.yT2[hh * 64:(hh + 1) * 64, :], yhT[:]),
                reads=("yhT",), writes=(f"{P}yT2h{hh}",), dma=True, pclass="p64")
        if (P + "yT2") in self.taps:
            self.tap_dram(P + "yT2", self.yT2,
                          [f"{P}yT2h{hh}" for hh in range(H)])
        p.barrier()
        st.close()

    def _sparse_s3(self, in_hbm, out_hbm, P):
        p = self.p
        st = ExitStack()
        owb = self.sb("s3_owb", [128, 8 * C], BF16, stack=st)
        yTc = self.sb("s3_yTc", [128, 8 * 512], BF16, stack=st)
        xt = [self.sb(f"s3_xt{i}", [128, C], stack=st) for i in range(2)]
        ot = [self.sb(f"s3_ot{i}", [128, C], stack=st) for i in range(2)]
        PA = self.ps("PA_s3", [128, 2048], stack=st)
        A = [f"{P}3PAb{i}" for i in range(4)]
        ykeys = tuple(f"{P}yT2h{hh}" for hh in range(H))
        p.step("sync", lambda h: h.dma_start(
            _s3(owb[:], {"f_wq":1024,"f_wk":256,"f_wv":256,"f_wo":1024,"ca_w":3072,"q_w":1024,"k_w":1024,"v_w":1024,"o_w":1024}["o_w"]), _t3(self.wb["o_w"])),
            reads=("o_w_b_done",), writes=("s3_owb",), dma=True)
        for tc in range(NCH):
            p.step("sync", lambda h, tc=tc: h.dma_start(
                _s3(yTc[:], 512), _t3(self.yT2[:, tc * 512:(tc + 1) * 512])),
                reads=ykeys, writes=("s3_yTc",), dma=True)
            for qq in range(4):
                tt = tc * 4 + qq
                b = tt % 2
                ab = 2 * b
                p.step("sync", lambda h, b=b, tt=tt: h.dma_start(
                    xt[b][:], in_hbm[tt * 128:(tt + 1) * 128, :]),
                    writes=(f"s3_xt{b}",), dma=True)
                for n0 in range(2):
                    p.step("tensor", lambda h, qq=qq, n0=n0, ab=ab: self._mmgroup(
                        h, PA[:, (ab + n0) * 512:(ab + n0 + 1) * 512],
                        [(yTc[:, kt * 512 + qq * 128: kt * 512 + qq * 128 + 128],
                          owb[:, kt * C + n0 * 512: kt * C + n0 * 512 + 512])
                         for kt in range(8)] +
                        [(self.c_ones128b[:],
                          self.obrow[:, n0 * 512:(n0 + 1) * 512])]),
                        reads=("s3_yTc", "s3_owb", "c_ones128b", "obrow"),
                        writes=(A[ab + n0],), tag="s3mm")
                p.step("vector", lambda h, b=b, ab=ab: h.tensor_tensor(
                    ot[b][:, 0:512], PA[:, ab * 512:(ab + 1) * 512],
                    xt[b][:, 0:512], ALU.add),
                    reads=(A[ab], f"s3_xt{b}"), writes=(f"s3_ot{b}a",))
                p.step("vector", lambda h, b=b, ab=ab: h.tensor_tensor(
                    ot[b][:, 512:1024], PA[:, (ab + 1) * 512:(ab + 2) * 512],
                    xt[b][:, 512:1024], ALU.add),
                    reads=(A[ab + 1], f"s3_xt{b}"), writes=(f"s3_ot{b}b",))
                p.step("sync", lambda h, b=b, tt=tt: h.dma_start(
                    out_hbm[tt * 128:(tt + 1) * 128, :], ot[b][:]),
                    reads=(f"s3_ot{b}a", f"s3_ot{b}b"),
                    writes=(f"{P}xout{tt}",), dma=True)
        if (P + "xout") in self.taps:
            self.tap_dram(P + "xout", out_hbm, [f"{P}xout{tt}" for tt in range(NT)])
        st.close()


    def _ln_stats(self, xtile, kx, mu_ap, kmu, rstd_ap, krstd, sq, smallc):
        p = self.p
        p.step("vector", lambda h: h.tensor_reduce(
            smallc[:, 0:1], xtile[:], AX.X, ALU.add),
            reads=(kx,), writes=("sm0",))
        p.step("scalar", lambda h: h.activation(
            sq[:], xtile[:], AF.Square, accum_out=smallc[:, 1:2]),
            reads=(kx,), writes=("sm1", "sqA", "sqB"))
        p.step("vector", lambda h: h.tensor_scalar(
            mu_ap, smallc[:, 0:1], 1.0 / C, None, ALU.mult),
            reads=("sm0",), writes=(kmu,))
        p.step("vector", lambda h: h.tensor_scalar(
            smallc[:, 2:3], smallc[:, 1:2], 1.0 / C, None, ALU.mult),
            reads=("sm1",), writes=("sm2",))
        p.step("vector", lambda h: h.tensor_tensor(
            smallc[:, 3:4], mu_ap, mu_ap, ALU.mult),
            reads=(kmu,), writes=("sm3",))
        p.step("vector", lambda h: h.tensor_tensor(
            smallc[:, 4:5], smallc[:, 2:3], smallc[:, 3:4], ALU.subtract),
            reads=("sm2", "sm3"), writes=("sm4",))
        p.step("scalar", lambda h: h.activation(
            smallc[:, 5:6], smallc[:, 4:5], AF.Sqrt, bias=self.epscol5[:]),
            reads=("sm4", "epscol5"), writes=("sm5",))
        p.step("vector", lambda h: h.reciprocal(rstd_ap, smallc[:, 5:6]),
            reads=("sm5",), writes=(krstd,))



def _in_maps(inputs, consts):
    m = {
        "ca_w": inputs["L0_attn_c_attn_w"], "ca_b": inputs["L0_attn_c_attn_b"],
        "q_w": inputs["L0_attn_q_w"], "q_b": inputs["L0_attn_q_b"],
        "k_w": inputs["L0_attn_k_w"], "k_b": inputs["L0_attn_k_b"],
        "v_w": inputs["L0_attn_v_w"], "v_b": inputs["L0_attn_v_b"],
        "o_w": inputs["L0_attn_o_w"], "o_b": inputs["L0_attn_o_b"],
        "f_n1": inputs["L0_ffn_norm1_w"], "f_wq": inputs["L0_ffn_Wq"],
        "f_wk": inputs["L0_ffn_Wk"], "f_wv": inputs["L0_ffn_Wv"],
        "f_wo": inputs["L0_ffn_Wo"], "f_n2": inputs["L0_ffn_norm2_w"],
        "f_rw": inputs["L0_ffn_router_w"], "f_rb": inputs["L0_ffn_router_b"],
        "f_w1": inputs["L0_ffn_W1"], "f_b1": inputs["L0_ffn_b1"],
        "f_w2": inputs["L0_ffn_W2"], "f_b2": inputs["L0_ffn_b2"],
        "f_w3": inputs["L0_ffn_W3"], "f_b3": inputs["L0_ffn_b3"],
    }
    for k, v in consts.items():
        m["c_" + k] = v
    x = np.ascontiguousarray(inputs["L0_attn_x"], dtype=np.float32)
    maps = []
    for b in range(8):
        d = {k: np.ascontiguousarray(v) for k, v in m.items()}
        d["x"] = np.ascontiguousarray(x[b])
        maps.append(d)
    return maps


def kernel(**inputs):
    consts = host_constants()
    kb = KB()
    nc = kb.build()
    maps = _in_maps(inputs, consts)
    res = run_bass_kernel_spmd(nc, maps, core_ids=list(range(8)))
    return np.stack([res.results[i]["y"] for i in range(8)], axis=0)
